# version 11
# baseline (speedup 1.0000x reference)
import numpy as np
import concourse.bass as bass
import concourse.mybir as mybir
from concourse.bass_utils import run_bass_kernel_spmd

F32 = mybir.dt.float32
BF16 = mybir.dt.bfloat16
AF = mybir.ActivationFunctionType
ALU = mybir.AluOpType
AX = mybir.AxisListType

D = 1024
DCN = 8
H = 16
KD = 64
NCORE = 8
T = 16384
TC = T // NCORE
CCH = 128
NCHK = TC // CCH
TBS = 256
NTB = TC // TBS
CPB = TBS // CCH
TBX = TBS + 1
E = 8
FF = 2048
LN_EPS = 1e-5
GN_EPS = 0.00064


class Sem:
    def __init__(self, nc, name, auto=False):
        self._cm = nc.semaphore(name)
        self.h = self._cm.__enter__()
        self.v = 0
        self.auto = auto

    def inc(self, inst, n=1):
        if self.auto:
            return inst
        inst.then_inc(self.h, n)
        self.v += n
        return inst

    def forced_inc(self, inst, n=1):
        inst.then_inc(self.h, n)
        self.v += n
        return inst


VOPS = {"memset", "memzero", "tensor_copy", "tensor_tensor", "tensor_add",
        "tensor_sub", "tensor_mul", "tensor_max", "tensor_scalar",
        "tensor_single_scalar", "tensor_scalar_add", "tensor_scalar_sub",
        "tensor_scalar_mul", "tensor_scalar_max", "tensor_scalar_min",
        "tensor_relu", "tensor_reduce", "reduce_max", "reduce_sum",
        "scalar_tensor_tensor", "tensor_tensor_scan", "reciprocal", "select",
        "copy_predicated", "transpose", "bn_stats", "bn_aggr", "engine_nop"}
SOPS = {"activation", "sign", "square", "sqrt", "mul", "copy", "add", "memzero"}
TOPS = {"matmul", "transpose"}


class EW:
    def __init__(self, eng, en, sem, W, ops, selfwait):
        self._e = eng
        self._en = en
        self._s = sem
        self._W = W
        self._ops = ops
        self._sw = selfwait

    def __getattr__(self, name):
        f = getattr(self._e, name)
        if name not in self._ops:
            return f

        def g(*a, **k):
            if self._sw:
                key = (self._en, id(self._s))
                if self._s.v > self._W.seen.get(key, -1):
                    self._e.wait_ge(self._s.h, self._s.v)
                    self._W.seen[key] = self._s.v
            inst = f(*a, **k)
            inst.then_inc(self._s.h, 1)
            self._s.v += 1
            return inst
        return g


class Waits:
    def __init__(self):
        self.seen = {}

    def wge(self, eng, ename, sem, val=None):
        val = sem.v if val is None else val
        if val <= 0:
            return
        key = (ename, id(sem))
        if self.seen.get(key, -1) >= val:
            return
        eng.wait_ge(sem.h, val)
        self.seen[key] = val


class Res:
    def __init__(self):
        self.w = None
        self.r = None

    @staticmethod
    def _tgt(rec):
        sem, val = rec
        if getattr(sem, "dma", False):
            return sem, sem.v
        return sem, val

    def need_write(self, W, eng, en):
        if self.r is not None:
            W.wge(eng, en, *self._tgt(self.r))
        if self.w is not None:
            W.wge(eng, en, *self._tgt(self.w))

    def need_read(self, W, eng, en):
        if self.w is not None:
            W.wge(eng, en, *self._tgt(self.w))

    def need_write_after_readers(self, W, eng, en):
        if self.r is not None:
            W.wge(eng, en, *self._tgt(self.r))

    def wrote(self, sem):
        self.w = (sem, sem.v)

    def read(self, sem):
        self.r = (sem, sem.v)


def build(nc, taps=(), level=3, prjsel=None, ntb=None, hs=9):
    taps = set(taps)
    xT_d = nc.dram_tensor("xT", [D, TC + 1], F32, kind="ExternalInput").ap()
    attw_d = nc.dram_tensor("attw", [8, D, D], F32, kind="ExternalInput").ap()
    mixw_d = nc.dram_tensor("mixw", [D, 8], F32, kind="ExternalInput").ap()
    vecs_d = nc.dram_tensor("vecs", [D, 4], F32, kind="ExternalInput").ap()
    routw_d = nc.dram_tensor("routw", [D, 8], F32, kind="ExternalInput").ap()
    wg_d = nc.dram_tensor("wgT", [E, D, FF], F32, kind="ExternalInput").ap()
    wu_d = nc.dram_tensor("wuT", [E, D, FF], F32, kind="ExternalInput").ap()
    wd_d = nc.dram_tensor("wdT", [E, FF, D], F32, kind="ExternalInput").ap()
    cident_d = nc.dram_tensor("cident", [128, 128], F32, kind="ExternalInput").ap()
    cmask4_d = nc.dram_tensor("cmask4", [128, 512], F32, kind="ExternalInput").ap()
    cm32_d = nc.dram_tensor("cm32", [128, 64], F32, kind="ExternalInput").ap()
    cI32_d = nc.dram_tensor("cI32", [128, 32], F32, kind="ExternalInput").ap()
    cones2_d = nc.dram_tensor("cones2", [128, 2], F32, kind="ExternalInput").ap()
    csel2_d = nc.dram_tensor("csel2", [2, 128], F32, kind="ExternalInput").ap()
    cones_d = nc.dram_tensor("conesr", [128, 128], F32, kind="ExternalInput").ap()
    csel8_d = nc.dram_tensor("csel8", [8, 1024], F32, kind="ExternalInput").ap()
    selrep_d = nc.dram_tensor("selrep", [128, 8], F32, kind="ExternalInput").ap()
    out_d = nc.dram_tensor("out_ft", [D, TC], F32, kind="ExternalOutput").ap()

    psi_d = nc.dram_tensor("psi_s", [NCHK * 128, 1024], BF16).ap()
    olo_d = nc.dram_tensor("olo_s", [NCHK * 128, 1024], BF16).ap()
    g_d = nc.dram_tensor("g_s", [D, TC], BF16).ap()
    xffn_d = nc.dram_tensor("xffn_s", [D, TC], BF16).ap()
    dwT_d = nc.dram_tensor("dwT_s", [8, 2048], BF16).ap()
    cc_in = nc.dram_tensor("cc_in", [2048, 64], F32)
    cc_out = nc.dram_tensor("cc_out", [8 * 2048, 64], F32)

    tap_d = {}
    if "proj" in taps:
        for n in ("tp_r", "tp_w", "tp_k", "tp_a", "tp_b"):
            tap_d[n] = nc.dram_tensor(n, [D, TBS], F32, kind="ExternalOutput").ap()
        tap_d["tp_v"] = nc.dram_tensor("tp_v", [CPB * 128, D], F32, kind="ExternalOutput").ap()
    if "rec" in taps:
        tap_d["tp_o"] = nc.dram_tensor("tp_o", [NCHK * 128, 1024], F32, kind="ExternalOutput").ap()
        tap_d["tp_z"] = nc.dram_tensor("tp_z", [128, 512], F32, kind="ExternalOutput").ap()
    if "attn" in taps:
        tap_d["tp_sin"] = nc.dram_tensor("tp_sin", [128, 512], F32, kind="ExternalOutput").ap()
        tap_d["tp_x1"] = nc.dram_tensor("tp_x1", [D, TC], F32, kind="ExternalOutput").ap()
    if "xffn" in taps:
        tap_d["tp_xffn"] = nc.dram_tensor("tp_xffn", [D, TC], F32, kind="ExternalOutput").ap()
    if "dw" in taps:
        tap_d["tp_dw"] = nc.dram_tensor("tp_dw", [16 * 128, 8], F32, kind="ExternalOutput").ap()

    W = Waits()
    sPE = Sem(nc, "sPE", auto=True)
    sV = Sem(nc, "sV", auto=True)
    sS = Sem(nc, "sS", auto=True)
    sG = Sem(nc, "sG")
    sDW = Sem(nc, "sDW")
    sDW.dma = True
    sDX = Sem(nc, "sDX")
    sDX.dma = True
    sCC = Sem(nc, "sCC")

    rs = {}
    def R(name):
        if name not in rs:
            rs[name] = Res()
        return rs[name]

    def gthrottle(g):
        if sDW.v - getattr(sDW, "lastw", 0) >= 16:
            tgt = sDW.v
            g.wait_ge(sDW.h, tgt)
            sDW.lastw = tgt
            W.seen[("g", id(sDW))] = tgt

    def ythrottle(sy):
        if sDX.v - getattr(sDX, "lastw", 0) >= 16:
            tgt = sDX.v
            sy.wait_ge(sDX.h, tgt)
            sDX.lastw = tgt
            W.seen[("y", id(sDX))] = tgt

    blk_cm = nc.Block()
    block = blk_cm.__enter__()
    def TE(f):
        block.tensor(lambda t: f(EW(t, "t", sPE, W, TOPS, False)))
    def VE(f):
        block.vector(lambda v: f(EW(v, "v", sV, W, VOPS, True)))
    def SE(f):
        block.scalar(lambda s2: f(EW(s2, "s", sS, W, SOPS, True)))
    GE, SY = block.gpsimd, block.sync

    def barrier():
        def bt(t):
            W.wge(t, "t", sV, sV.v)
            W.wge(t, "t", sS, sS.v)
            W.wge(t, "t", sG, sG.v)
            W.wge(t, "t", sDX, sDX.v)
            W.wge(t, "t", sDW, sDW.v)
            sPE.forced_inc(t.nop())
        TE(bt)
        pe_t = sPE.v
        def bo(e, en):
            W.wge(e, en, sPE, pe_t)
        VE(lambda v: (bo(v, "v"), sV.inc(v.engine_nop())))
        SE(lambda s: (bo(s, "s"), sS.forced_inc(s.nop())))
        GE(lambda g: (bo(g, "g"), sG.inc(g.nop())))
        SY(lambda y: bo(y, "y"))

    ctx = []
    abctx = []
    actx = []
    def mksb(lst):
        def sbf(name, shape, dt):
            cm = nc.sbuf_tensor(name, shape, dt)
            t = cm.__enter__()
            lst.append(cm)
            return t
        return sbf
    sb = mksb(ctx)
    absb = mksb(abctx)
    asb = mksb(actx)

    xT = sb("xT_sb", [128, DCN * (TC + 1)], F32)
    XS = TC + 1
    cid_b = sb("cid_b", [128, 128], BF16)
    cid_f = sb("cid_f", [128, 128], F32)
    cmask4 = sb("cmask4_s", [128, 512], BF16)
    cm32 = sb("cm32_s", [128, 64], BF16)
    cI32 = sb("cI32_s", [128, 32], BF16)
    cones2 = sb("cones2_s", [128, 2], BF16)
    csel2 = sb("csel2_s", [2, 128], BF16)
    conesr = sb("conesr_s", [128, 128], BF16)
    csel8 = sb("csel8_s", [8, 1024], BF16)
    mixw = sb("mixw_s", [128, DCN * 8], F32)
    omix = sb("omix_s", [128, DCN * 8], F32)
    vecs_sb = sb("vecs_s", [128, DCN * 4], F32)
    routw_sb = sb("routw_s", [128, DCN * 8], BF16)
    selrep = sb("selrep_s", [128, 8], F32)
    wt1 = sb("wt1", [128, 512], F32)
    wt2 = sb("wt2", [128, 512], F32)
    kb = sb("kb", [128, 4], F32)

    bcctx = []

    z_sb = [absb("zP%d" % p, [128, DCN * 64], F32) for p in range(2)]
    QT_sb = [absb("qtP%d" % p, [128, DCN * 64], F32) for p in range(2)]
    SinT = absb("sinT", [128, DCN * 64], F32)
    sinb = absb("sinb", [128, DCN * 64], BF16)
    psi_sb = absb("psi_sb", [128, DCN * 128], BF16)
    olo_sb = absb("olo_sb", [128, H * 64], BF16)
    stat_sb = absb("stat_sb", [1, 2 * TBX], F32)
    statb = absb("statb", [1, 2 * TBX], BF16)

    def init_loads(g):
        gthrottle(g); sDW.inc(g.dma_start(xT[:].rearrange("p (c t) -> p c t", c=DCN), xT_d.rearrange("(c p) t -> p c t", p=128)), 16)
        for dst, src in ((cid_b, cident_d), (cmask4, cmask4_d), (cm32, cm32_d),
                         (cI32, cI32_d), (cones2, cones2_d), (csel2, csel2_d),
                         (conesr, cones_d), (csel8, csel8_d),
                         (routw_sb[:].rearrange("p (c e) -> p c e", c=DCN), routw_d.rearrange("(c p) e -> p c e", p=128))):
            gthrottle(g); sDW.inc(g.dma_start(dst[:], src), 16)
        for dst, src in ((cid_f, cident_d),
                         (mixw[:].rearrange("p (c m) -> p c m", c=DCN), mixw_d.rearrange("(c p) m -> p c m", p=128)),
                         (vecs_sb[:].rearrange("p (c m) -> p c m", c=DCN), vecs_d.rearrange("(c p) m -> p c m", p=128)),
                         (selrep, selrep_d)):
            gthrottle(g); sDW.inc(g.dma_start(dst[:], src), 16)
    GE(init_loads)
    ld0w, ld0x = sDW.v, sDX.v

    def init_v(v):
        W.wge(v, "v", sDW, ld0w)
        W.wge(v, "v", sDX, ld0x)
        v.tensor_scalar(omix[:], mixw[:], -1.0, 1.0, ALU.mult, ALU.add)
        v.memset(kb[:, 0:1], LN_EPS)
        v.memset(kb[:, 1:2], 1.0)
        v.memset(kb[:, 2:3], GN_EPS)
        v.memset(kb[:, 3:4], 0.0)
        for p in range(2):
            v.memset(z_sb[p][:], 0.0)
            v.memset(QT_sb[p][:], 0.0)
        v.memset(SinT[:], 0.0)
        for dc in range(DCN):
            for oh in (0, 64):
                v.tensor_copy(QT_sb[0][oh:oh + 64, dc * 64:(dc + 1) * 64],
                              cid_f[oh:oh + 64, oh:oh + 64])
        sV.inc(v.engine_nop())
    VE(init_v)

    ln_sb = asb("ln_sb", [128, DCN * TBX], BF16)
    mixed = asb("mixed", [128, DCN * TBS], BF16)
    msc = asb("msc", [128, DCN * TBS], BF16)
    watt = [asb("watt%d" % p, [128, DCN * 1024], BF16) for p in range(2)]
    r_sb = asb("r_sb", [128, DCN * TBS], BF16)
    k_sb = asb("k_sb", [128, DCN * TBS], BF16)
    a_sb = asb("a_sb", [128, DCN * TBS], BF16)
    b_sb = asb("b_sb", [128, DCN * TBS], BF16)
    w_sb = asb("w_sb", [128, DCN * TBS], F32)
    v_sb = asb("v_sb", [128, CPB * 1024], BF16)
    gtmp2 = asb("gtmp2", [128, TBS], BF16)
    rnf = asb("rnf", [2, TBS], F32)
    rnb = asb("rnb", [2, TBS], BF16)
    cw_sb = asb("cw_sb", [128, DCN * 129], F32)
    lam_sb = asb("lam_sb", [128, DCN * 129], F32)
    lami_sb = asb("lami_sb", [128, DCN * 128], F32)
    sml = asb("sml", [128, DCN * 4], F32)
    ones128 = asb("ones128", [128, 128], F32)
    ah_sb = asb("ah_sb", [128, DCN * 128], BF16)
    bt_sb = asb("bt_sb", [128, DCN * 128], BF16)
    kt_sb = asb("kt_sb", [128, DCN * 128], BF16)
    rh_sb = asb("rh_sb", [128, DCN * 128], BF16)
    Bh_sb = asb("Bh_sb", [128, DCN * 128], BF16)
    Kh_sb = asb("Kh_sb", [128, DCN * 128], BF16)
    Btok = asb("Btok", [128, DCN * 128], BF16)
    Ktok = asb("Ktok", [128, DCN * 128], BF16)
    prod = [asb("prodP%d" % p, [128, 512], BF16) for p in range(2)]
    qpow = [asb("qpowP%d" % p, [128, 64], BF16) for p in range(2)]
    accp = [asb("accpP%d" % p, [128, 32], BF16) for p in range(2)]
    y_sb = [asb("ysbP%d" % p, [128, 128], BF16) for p in range(2)]
    X_sb = [asb("XsbP%d" % p, [128, 128], BF16) for p in range(2)]
    phiT = asb("phiT", [128, DCN * 128], BF16)
    zs_sb = asb("zs_sb", [128, DCN * 64], BF16)
    qts_sb = asb("qts_sb", [128, DCN * 64], BF16)
    pr_sb = [asb("prP%d" % p, [128, 64], BF16) for p in range(2)]
    xb_of = lambda dc: watt[0][:, dc * TBX:(dc + 1) * TBX]
    sq_of = lambda dc: ln_sb[:, dc * TBX:(dc + 1) * TBX]
    qfin = cw_sb

    pctx = []
    def mkps(name, shape, dt=F32):
        cm = nc.psum_tensor(name, shape, dt)
        t = cm.__enter__()
        pctx.append(cm)
        return t
    pbP = [mkps("pbP0", [128, 512]), mkps("pbP1", [128, 512])]
    pbC = mkps("pbC", [128, 512])
    pbD = mkps("pbD", [128, 512])
    pbY = mkps("pbY", [128, 512])
    pbX = mkps("pbX", [128, 512])
    pbM = mkps("pbM", [128, 512])
    pbT = mkps("pbT", [128, 1024], BF16)

    def vones(v):
        sV.inc(v.memset(ones128[:], 1.0))
    VE(vones)

    PRJ = ["r", "w", "k", "v", "a", "b", "g"] if prjsel is None else list(prjsel)
    prj_tiles = {"r": r_sb, "k": k_sb, "a": a_sb, "b": b_sb}

    if level <= -1:
        finm = sV.v
        def out_dmam(sy, finm=finm):
            W.wge(sy, "y", sV, finm)
            for dc in range(DCN):
                ythrottle(sy); sDX.inc(sy.dma_start(out_d[dc * 128:(dc + 1) * 128, :],
                                     xT[:, dc * XS + 1:(dc + 1) * XS]), 16)
            sy.wait_ge(sDX.h, sDX.v)
        SY(out_dmam)
        blk_cm.__exit__(None, None, None)
        for cm in pctx[::-1]:
            cm.__exit__(None, None, None)
        for cm in actx[::-1]:
            cm.__exit__(None, None, None)
        for cm in abctx[::-1]:
            cm.__exit__(None, None, None)
        for cm in ctx[::-1]:
            cm.__exit__(None, None, None)
        return nc

    for it in range(NTB if ntb is None else ntb):
        t0 = it * TBS

        def a_cast(v, t0=t0):
            W.wge(v, "v", sS, sS.v)
            R("watt0").need_write(W, v, "v")
            R("lnt").need_write(W, v, "v")
            for dc in range(DCN):
                v.tensor_copy(xb_of(dc), xT[:, dc * XS + t0: dc * XS + t0 + TBX])
            for dc in range(DCN):
                v.tensor_tensor(sq_of(dc), xb_of(dc), xb_of(dc), ALU.mult)
            sV.inc(v.engine_nop())
            R("watt0").wrote(sV)
            R("lnt").wrote(sV)
        VE(a_cast)
        xb_t = sV.v

        def a_stats(t, xb_t=xb_t):
            W.wge(t, "t", sV, xb_t)
            R("pbC").need_write(W, t, "t")
            R("pbP0").need_write(W, t, "t")
            for dc in range(DCN):
                t.matmul(pbC[0:1, 0:TBX], conesr[:, 0:1], xb_of(dc),
                         start=(dc == 0), stop=(dc == DCN - 1))
            for dc in range(DCN):
                t.matmul(pbP[0][0:1, 0:TBX], conesr[:, 0:1], sq_of(dc),
                         start=(dc == 0), stop=(dc == DCN - 1))
            sPE.inc(t.nop())
            R("pbC").wrote(sPE)
            R("pbP0").wrote(sPE)
            R("watt0").read(sPE)
            R("lnt").read(sPE)
        TE(a_stats)
        st_t = sPE.v

        def a_mv(v, st_t=st_t):
            W.wge(v, "v", sPE, st_t)
            v.tensor_scalar(stat_sb[0:1, 0:TBX], pbC[0:1, 0:TBX], 1.0 / D, None, ALU.mult)
            v.tensor_scalar(stat_sb[0:1, TBX:2 * TBX], pbP[0][0:1, 0:TBX],
                            1.0 / D, None, ALU.mult)
            R("pbC").read(sV)
            R("pbP0").read(sV)
            v.tensor_tensor(wt1[0:1, 0:TBX], stat_sb[0:1, 0:TBX],
                            stat_sb[0:1, 0:TBX], ALU.mult)
            sV.inc(v.tensor_tensor(stat_sb[0:1, TBX:2 * TBX],
                                   stat_sb[0:1, TBX:2 * TBX], wt1[0:1, 0:TBX],
                                   ALU.subtract))
        VE(a_mv)
        mv_t = sV.v

        def a_rstd(s, mv_t=mv_t):
            W.wge(s, "s", sV, mv_t)
            s.activation(stat_sb[0:1, TBX:2 * TBX], stat_sb[0:1, TBX:2 * TBX],
                         AF.Ln, bias=kb[0:1, 0:1])
            sS.inc(s.activation(stat_sb[0:1, TBX:2 * TBX],
                                stat_sb[0:1, TBX:2 * TBX], AF.Exp, scale=-0.5))
        SE(a_rstd)
        rs_t = sS.v

        def a_stc(v, rs_t=rs_t):
            W.wge(v, "v", sS, rs_t)
            sV.inc(v.tensor_copy(statb[0:1, 0:2 * TBX], stat_sb[0:1, 0:2 * TBX]))
        VE(a_stc)
        stc_t = sV.v

        def a_bc(t, stc_t=stc_t):
            W.wge(t, "t", sV, stc_t)
            R("pbC").need_write(W, t, "t")
            R("pbP0").need_write(W, t, "t")
            t.matmul(pbC[:, 0:TBX], conesr[0:1, :], statb[0:1, 0:TBX], start=True, stop=True)
            sPE.inc(t.matmul(pbP[0][:, 0:TBX], conesr[0:1, :], statb[0:1, TBX:2 * TBX],
                             start=True, stop=True))
            R("pbC").wrote(sPE)
            R("pbP0").wrote(sPE)
        TE(a_bc)
        bc_t = sPE.v

        def a_ln(v, t0=t0, bc_t=bc_t):
            W.wge(v, "v", sPE, bc_t)
            R("lnt").need_write(W, v, "v")
            for dc in range(DCN):
                v.tensor_tensor(wt1[:, 0:TBX], xT[:, dc * XS + t0: dc * XS + t0 + TBX],
                                pbC[:, 0:TBX], ALU.subtract)
                v.tensor_tensor(ln_sb[:, dc * TBX:(dc + 1) * TBX], wt1[:, 0:TBX],
                                pbP[0][:, 0:TBX], ALU.mult)
            sV.inc(v.engine_nop())
            R("pbC").read(sV)
            R("pbP0").read(sV)
            R("lnt").wrote(sV)
        VE(a_ln)
        ln_t = sV.v

        def a_wl_emit(pi):
            pw = (pi + 1) % 2
            def a_wl(g, pi=pi, pw=pw):
                R("watt%d" % pw).need_write(W, g, "g")
                gthrottle(g); sDW.inc(g.dma_start(watt[pw][:].rearrange("p (c e) -> p c e", c=DCN), attw_d[pi].rearrange("(c p) e -> p c e", p=128)), 16)
                R("watt%d" % pw).wrote(sDW)
            GE(a_wl)
            return sDW.v
        wl_ts = {0: a_wl_emit(0)}
        for pi, pn in enumerate(PRJ):
            pw = (pi + 1) % 2
            if pi + 1 < len(PRJ):
                wl_ts[pi + 1] = a_wl_emit(pi + 1)
            wl_t = wl_ts[pi]

            def a_mix(v, pi=pi, ln_t=ln_t):
                W.wge(v, "v", sV, ln_t)
                R("mixed").need_write(W, v, "v")
                R("msc").need_write(W, v, "v")
                for dc in range(DCN):
                    mu = mixw[:, dc * 8 + pi:dc * 8 + pi + 1]
                    om = omix[:, dc * 8 + pi:dc * 8 + pi + 1]
                    v.tensor_scalar(msc[:, dc * TBS:(dc + 1) * TBS],
                                    ln_sb[:, dc * TBX + 1:dc * TBX + TBX], om, None, ALU.mult)
                    v.scalar_tensor_tensor(mixed[:, dc * TBS:(dc + 1) * TBS],
                                           ln_sb[:, dc * TBX:dc * TBX + TBS], mu,
                                           msc[:, dc * TBS:(dc + 1) * TBS],
                                           ALU.mult, ALU.add)
                sV.inc(v.engine_nop())
                R("mixed").wrote(sV)
                R("msc").wrote(sV)
            VE(a_mix)
            mx_t = sV.v

            if pn == "v":
                for tt in range(CPB):
                    for eh in range(2):
                        bi = (tt * 2 + eh) % 2
                        def a_vmm(t, tt=tt, eh=eh, bi=bi, pw=pw, wl_t=wl_t, mx_t=mx_t):
                            W.wge(t, "t", sDW, wl_t)
                            W.wge(t, "t", sV, mx_t)
                            R("pbP%d" % bi).need_write(W, t, "t")
                            for dc in range(DCN):
                                t.matmul(pbP[bi][:, 0:512],
                                         mixed[:, dc * TBS + tt * 128: dc * TBS + (tt + 1) * 128],
                                         watt[pw][:, dc * 1024 + eh * 512: dc * 1024 + (eh + 1) * 512],
                                         start=(dc == 0), stop=(dc == DCN - 1))
                            sPE.inc(t.nop())
                            R("pbP%d" % bi).wrote(sPE)
                            R("watt%d" % pw).read(sPE)
                            R("mixed").read(sPE)
                        TE(a_vmm)
                        def a_vev(v, tt=tt, eh=eh, bi=bi):
                            R("pbP%d" % bi).need_read(W, v, "v")
                            sV.inc(v.tensor_copy(
                                v_sb[:, tt * 1024 + eh * 512: tt * 1024 + (eh + 1) * 512],
                                pbP[bi][:, 0:512]))
                            R("pbP%d" % bi).read(sV)
                        VE(a_vev)
                continue

            for ec in range(DCN):
                bi = ec % 2
                def a_pmm(t, ec=ec, bi=bi, pw=pw, wl_t=wl_t, mx_t=mx_t):
                    W.wge(t, "t", sDW, wl_t)
                    W.wge(t, "t", sV, mx_t)
                    R("pbP%d" % bi).need_write(W, t, "t")
                    for dc in range(DCN):
                        t.matmul(pbP[bi][:, 0:TBS],
                                 watt[pw][:, dc * 1024 + ec * 128: dc * 1024 + (ec + 1) * 128],
                                 mixed[:, dc * TBS:(dc + 1) * TBS],
                                 start=(dc == 0), stop=(dc == DCN - 1))
                    sPE.inc(t.nop())
                    R("pbP%d" % bi).wrote(sPE)
                    R("watt%d" % pw).read(sPE)
                    R("mixed").read(sPE)
                TE(a_pmm)

                if pn == "r":
                    def a_ev(v, ec=ec, bi=bi):
                        R("pbP%d" % bi).need_read(W, v, "v")
                        sV.inc(v.tensor_copy(r_sb[:, ec * TBS:(ec + 1) * TBS], pbP[bi][:, 0:TBS]))
                        R("pbP%d" % bi).read(sV)
                    VE(a_ev)
                elif pn == "g":
                    def a_ev(v, ec=ec, bi=bi):
                        R("pbP%d" % bi).need_read(W, v, "v")
                        R("gtmp2").need_write(W, v, "v")
                        sV.inc(v.tensor_copy(gtmp2[:, 0:TBS], pbP[bi][:, 0:TBS]))
                        R("pbP%d" % bi).read(sV)
                        R("gtmp2").wrote(sV)
                    VE(a_ev)
                    gt = sV.v
                    def a_gd(sy, ec=ec, t0=t0, gt=gt):
                        W.wge(sy, "y", sV, gt)
                        ythrottle(sy); sDX.inc(sy.dma_start(g_d[ec * 128:(ec + 1) * 128, t0:t0 + TBS],
                                             gtmp2[:, 0:TBS]), 16)
                        R("gtmp2").read(sDX)
                    SY(a_gd)
                elif pn == "w":
                    def a_ev(s, ec=ec, bi=bi):
                        R("pbP%d" % bi).need_read(W, s, "s")
                        R("wt1").need_write(W, s, "s")
                        R("wt2").need_write(W, s, "s")
                        s.activation(wt1[:, 0:TBS], pbP[bi][:, 0:TBS], AF.Abs)
                        s.activation(wt1[:, 0:TBS], wt1[:, 0:TBS], AF.Exp, scale=-1.0)
                        sS.inc(s.activation(wt2[:, 0:TBS], wt1[:, 0:TBS], AF.Ln, bias=kb[:, 1:2]))
                        R("wt1").wrote(sS)
                        R("wt2").wrote(sS)
                    SE(a_ev)
                    wet = sS.v
                    def a_ev2(v, ec=ec, bi=bi, wet=wet):
                        W.wge(v, "v", sS, wet)
                        v.tensor_scalar_max(wt1[:, 0:TBS], pbP[bi][:, 0:TBS], 0.0)
                        sV.inc(v.scalar_tensor_tensor(w_sb[:, ec * TBS:(ec + 1) * TBS],
                                                      wt1[:, 0:TBS], -1.0, wt2[:, 0:TBS],
                                                      ALU.mult, ALU.subtract))
                        R("pbP%d" % bi).read(sV)
                        R("wt1").read(sV)
                        R("wt2").read(sV)
                    VE(a_ev2)
                elif pn == "k":
                    def a_ev(s, ec=ec, bi=bi):
                        R("pbP%d" % bi).need_read(W, s, "s")
                        R("wt1").need_write(W, s, "s")
                        sS.inc(s.activation(wt1[:, 0:TBS], pbP[bi][:, 0:TBS], AF.Exp, scale=2.0))
                        R("wt1").wrote(sS)
                        R("pbP%d" % bi).read(sS)
                    SE(a_ev)
                    ket = sS.v
                    def a_ev2(v, ec=ec, ket=ket):
                        W.wge(v, "v", sS, ket)
                        R("wt2").need_write(W, v, "v")
                        v.tensor_scalar_add(wt2[:, 0:TBS], wt1[:, 0:TBS], 1.0)
                        v.reciprocal(wt2[:, 0:TBS], wt2[:, 0:TBS])
                        sV.inc(v.tensor_scalar(k_sb[:, ec * TBS:(ec + 1) * TBS],
                                               wt2[:, 0:TBS], -2.0, 1.0, ALU.mult, ALU.add))
                        R("wt1").read(sV)
                        R("wt2").read(sV)
                    VE(a_ev2)
                else:
                    dst = prj_tiles[pn]
                    def a_ev(v, ec=ec, bi=bi, dst=dst):
                        R("pbP%d" % bi).need_read(W, v, "v")
                        R("msq").need_write(W, v, "v")
                        v.tensor_copy(dst[:, ec * TBS:(ec + 1) * TBS], pbP[bi][:, 0:TBS])
                        sV.inc(v.tensor_tensor(msc[:, 0:TBS],
                                               dst[:, ec * TBS:(ec + 1) * TBS],
                                               dst[:, ec * TBS:(ec + 1) * TBS], ALU.mult))
                        R("pbP%d" % bi).read(sV)
                        R("msq").wrote(sV)
                    VE(a_ev)
                    sqt = sV.v
                    def a_nmm(t, sqt=sqt):
                        W.wge(t, "t", sV, sqt)
                        R("pbC").need_write(W, t, "t")
                        sPE.inc(t.matmul(pbC[0:2, 0:TBS], cones2[:], msc[:, 0:TBS],
                                         start=True, stop=True))
                        R("pbC").wrote(sPE)
                        R("msq").read(sPE)
                    TE(a_nmm)
                    nmt = sPE.v
                    def a_rn1(s, nmt=nmt):
                        W.wge(s, "s", sPE, nmt)
                        R("rnf").need_write(W, s, "s")
                        s.activation(rnf[0:2, 0:TBS], pbC[0:2, 0:TBS], AF.Ln)
                        sS.inc(s.activation(rnf[0:2, 0:TBS], rnf[0:2, 0:TBS],
                                            AF.Exp, scale=0.5))
                        R("pbC").read(sS)
                        R("rnf").wrote(sS)
                    SE(a_rn1)
                    rn1 = sS.v
                    def a_rn2(v, rn1=rn1):
                        W.wge(v, "v", sS, rn1)
                        v.tensor_scalar_max(rnf[0:2, 0:TBS], rnf[0:2, 0:TBS], 1e-6)
                        v.reciprocal(rnf[0:2, 0:TBS], rnf[0:2, 0:TBS])
                        sV.inc(v.tensor_copy(rnb[0:2, 0:TBS], rnf[0:2, 0:TBS]))
                        R("rnf").read(sV)
                    VE(a_rn2)
                    rn2 = sV.v
                    def a_rbc(t, rn2=rn2):
                        W.wge(t, "t", sV, rn2)
                        R("pbC").need_write(W, t, "t")
                        sPE.inc(t.matmul(pbC[:, 0:TBS], csel2[:], rnb[0:2, 0:TBS],
                                         start=True, stop=True))
                        R("pbC").wrote(sPE)
                    TE(a_rbc)
                    rbt = sPE.v
                    def a_rap(v, ec=ec, dst=dst, rbt=rbt):
                        W.wge(v, "v", sPE, rbt)
                        R("wt1").need_write(W, v, "v")
                        v.tensor_tensor(wt1[:, 0:TBS], dst[:, ec * TBS:(ec + 1) * TBS],
                                        pbC[:, 0:TBS], ALU.mult)
                        sV.inc(v.tensor_copy(dst[:, ec * TBS:(ec + 1) * TBS], wt1[:, 0:TBS]))
                        R("pbC").read(sV)
                        R("wt1").read(sV)
                    VE(a_rap)

        if "proj" in taps and it == 0:
            for name, tile in (("tp_r", r_sb), ("tp_k", k_sb), ("tp_a", a_sb),
                               ("tp_b", b_sb), ("tp_w", w_sb)):
                for dc in range(DCN):
                    def tpc(v, tile=tile, dc=dc):
                        W.wge(v, "v", sV, sV.v)
                        R("wt1").need_write(W, v, "v")
                        sV.inc(v.tensor_copy(wt1[:, 0:TBS], tile[:, dc * TBS:(dc + 1) * TBS]))
                        R("wt1").wrote(sV)
                    VE(tpc)
                    tt_ = sV.v
                    def tpd(sy, name=name, dc=dc, tt_=tt_):
                        W.wge(sy, "y", sV, tt_)
                        ythrottle(sy); sDX.inc(sy.dma_start(tap_d[name][dc * 128:(dc + 1) * 128, :],
                                             wt1[:, 0:TBS]), 16)
                        R("wt1").read(sDX)
                    SY(tpd)
            for tt in range(CPB):
                for eh in range(2):
                    def tpv(v, tt=tt, eh=eh):
                        W.wge(v, "v", sV, sV.v)
                        W.wge(v, "v", sDX, sDX.v)
                        R("wt1").need_write(W, v, "v")
                        sV.inc(v.tensor_copy(wt1[:, 0:512],
                                             v_sb[:, tt * 1024 + eh * 512:tt * 1024 + (eh + 1) * 512]))
                        R("wt1").wrote(sV)
                    VE(tpv)
                    tt_ = sV.v
                    def tpvd(sy, tt=tt, eh=eh, tt_=tt_):
                        W.wge(sy, "y", sV, tt_)
                        ythrottle(sy); sDX.inc(sy.dma_start(
                            tap_d["tp_v"][tt * 128:(tt + 1) * 128, eh * 512:(eh + 1) * 512],
                            wt1[:, 0:512]), 16)
                        R("wt1").read(sDX)
                    SY(tpvd)

        for cc in range(CPB):
            if level < 1:
                barrier()
                continue
            c = it * CPB + cc
            par = c % 2
            ct0 = cc * 128

            def c_ew_v(v, ct0=ct0):
                W.wge(v, "v", sV, sV.v)
                W.wge(v, "v", sS, sS.v)
                R("cw").need_write(W, v, "v")
                for dc in range(DCN):
                    v.memset(cw_sb[:, dc * 129:dc * 129 + 1], 0.0)
                    v.tensor_tensor_scan(cw_sb[:, dc * 129 + 1:(dc + 1) * 129],
                                         ones128[:, 0:128],
                                         w_sb[:, dc * TBS + ct0:dc * TBS + ct0 + 128],
                                         0.0, ALU.mult, ALU.add)
                    v.tensor_scalar(sml[:, dc * 4:dc * 4 + 1],
                                    cw_sb[:, (dc + 1) * 129 - 1:(dc + 1) * 129],
                                    -0.5, None, ALU.mult)
                sV.inc(v.engine_nop())
                R("cw").wrote(sV)
            VE(c_ew_v)
            cwt = sV.v

            def c_ew_s(s, cwt=cwt):
                W.wge(s, "s", sV, cwt)
                R("lam").need_write(W, s, "s")
                for dc in range(DCN):
                    s.activation(lam_sb[:, dc * 129:(dc + 1) * 129],
                                 cw_sb[:, dc * 129:(dc + 1) * 129],
                                 AF.Exp, bias=sml[:, dc * 4:dc * 4 + 1])
                    s.activation(sml[:, dc * 4 + 1:dc * 4 + 2],
                                 sml[:, dc * 4:dc * 4 + 1], AF.Exp, scale=-1.0)
                    s.activation(sml[:, dc * 4 + 2:dc * 4 + 3],
                                 cw_sb[:, (dc + 1) * 129 - 1:(dc + 1) * 129], AF.Exp)
                sS.inc(s.nop())
                R("lam").wrote(sS)
            SE(c_ew_s)
            lam_t = sS.v

            def c_ops(v, ct0=ct0, lam_t=lam_t):
                W.wge(v, "v", sS, lam_t)
                R("ops").need_write(W, v, "v")
                for dc in range(DCN):
                    sl_l = lam_sb[:, dc * 129 + 1:(dc + 1) * 129]
                    sl_p = lam_sb[:, dc * 129:(dc + 1) * 129 - 1]
                    v.reciprocal(lami_sb[:, dc * 128:(dc + 1) * 128], sl_l)
                    v.tensor_tensor(ah_sb[:, dc * 128:(dc + 1) * 128],
                                    a_sb[:, dc * TBS + ct0:dc * TBS + ct0 + 128], sl_p, ALU.mult)
                    v.tensor_tensor(bt_sb[:, dc * 128:(dc + 1) * 128],
                                    b_sb[:, dc * TBS + ct0:dc * TBS + ct0 + 128],
                                    lami_sb[:, dc * 128:(dc + 1) * 128], ALU.mult)
                    v.tensor_tensor(kt_sb[:, dc * 128:(dc + 1) * 128],
                                    k_sb[:, dc * TBS + ct0:dc * TBS + ct0 + 128],
                                    lami_sb[:, dc * 128:(dc + 1) * 128], ALU.mult)
                    v.tensor_tensor(rh_sb[:, dc * 128:(dc + 1) * 128],
                                    r_sb[:, dc * TBS + ct0:dc * TBS + ct0 + 128], sl_l, ALU.mult)
                    lc = lam_sb[:, (dc + 1) * 129 - 1:(dc + 1) * 129]
                    v.tensor_scalar(Bh_sb[:, dc * 128:(dc + 1) * 128],
                                    bt_sb[:, dc * 128:(dc + 1) * 128], lc, None, ALU.mult)
                    v.tensor_scalar(Kh_sb[:, dc * 128:(dc + 1) * 128],
                                    kt_sb[:, dc * 128:(dc + 1) * 128], lc, None, ALU.mult)
                sV.inc(v.engine_nop())
                R("ops").wrote(sV)
            VE(c_ops)
            ops_t = sV.v

            for dc in range(DCN):
                for src, dsttile in ((Bh_sb, Btok), (Kh_sb, Ktok), (ah_sb, None)):
                    def c_tp(t, dc=dc, src=src, ops_t=ops_t):
                        W.wge(t, "t", sV, ops_t)
                        R("pbT").need_write(W, t, "t")
                        t.transpose(pbT[:, 0:128], src[:, dc * 128:(dc + 1) * 128], cid_b[:])
                        sPE.inc(t.nop())
                        R("pbT").wrote(sPE)
                    TE(c_tp)
                    tpt = sPE.v
                    def c_tpe(v, dc=dc, dsttile=dsttile, tpt=tpt):
                        W.wge(v, "v", sPE, tpt)
                        if dsttile is None:
                            R("watt0").need_write_after_readers(W, v, "v")
                            sV.inc(v.tensor_copy(watt[0][:, dc * 128:(dc + 1) * 128], pbT[:, 0:128]))
                            R("watt0").wrote(sV)
                        else:
                            sV.inc(v.tensor_copy(dsttile[:, dc * 128:(dc + 1) * 128], pbT[:, 0:128]))
                        R("pbT").read(sV)
                    VE(c_tpe)
            tok_t = sV.v

            for hh in range(H):
                if hs < 1:
                    break
                dc = hh // 2
                oh = 64 * (hh % 2)
                hp = hh % 2
                vtk = v_sb[:, cc * 1024 + hh * 64: cc * 1024 + (hh + 1) * 64]

                def h_prod(t, dc=dc, oh=oh, hp=hp, tok_t=tok_t):
                    W.wge(t, "t", sV, tok_t)
                    R("pbP%d" % hp).need_write(W, t, "t")
                    R("pbD").need_write(W, t, "t")
                    bankp = pbP[hp]
                    bt_ = bt_sb[oh:oh + 64, dc * 128:(dc + 1) * 128]
                    kt_ = kt_sb[oh:oh + 64, dc * 128:(dc + 1) * 128]
                    ah_ = ah_sb[oh:oh + 64, dc * 128:(dc + 1) * 128]
                    rh_ = rh_sb[oh:oh + 64, dc * 128:(dc + 1) * 128]
                    t.matmul(bankp[:, 0:128], bt_, ah_, start=True, stop=False)
                    t.matmul(bankp[:, 256:384], bt_, rh_, start=False, stop=False)
                    t.matmul(bankp[:, 128:256], kt_, ah_, start=False, stop=False)
                    t.matmul(bankp[:, 384:512], kt_, rh_, start=False, stop=True)
                    for i in range(4):
                        t.matmul(pbD[32 * i:32 * i + 32, 0:32],
                                 bt_sb[oh:oh + 64, dc * 128 + 32 * i:dc * 128 + 32 * i + 32],
                                 ah_sb[oh:oh + 64, dc * 128 + 32 * i:dc * 128 + 32 * i + 32],
                                 start=True, stop=False, tile_position=(oh, 32 * i), skip_group_check=True)
                    for i in range(4):
                        t.matmul(pbD[32 * i:32 * i + 32, 32:64],
                                 ah_sb[oh:oh + 64, dc * 128 + 32 * i:dc * 128 + 32 * i + 32],
                                 bt_sb[oh:oh + 64, dc * 128 + 32 * i:dc * 128 + 32 * i + 32],
                                 start=False, stop=True, tile_position=(oh, 32 * i), skip_group_check=True)
                    sPE.inc(t.nop())
                    R("pbP%d" % hp).wrote(sPE)
                    R("pbD").wrote(sPE)
                TE(h_prod)
                prod_t = sPE.v

                def h_mask(v, hp=hp, prod_t=prod_t):
                    W.wge(v, "v", sPE, prod_t)
                    R("prod%d" % hp).need_write(W, v, "v")
                    R("qpow%d" % hp).need_write(W, v, "v")
                    R("accp%d" % hp).need_write(W, v, "v")
                    v.scalar_tensor_tensor(prod[hp][:, 0:512], pbP[hp][:, 0:512], 1.0,
                                           cmask4[:], ALU.mult, ALU.mult)
                    R("pbP%d" % hp).read(sV)
                    v.tensor_tensor(qpow[hp][:, 0:64], pbD[:, 0:64], cm32[:], ALU.mult)
                    sV.inc(v.tensor_tensor(accp[hp][:, 0:32], qpow[hp][:, 32:64],
                                           cI32[:], ALU.add))
                    R("pbD").read(sV)
                    R("prod%d" % hp).wrote(sV)
                    R("qpow%d" % hp).wrote(sV)
                    R("accp%d" % hp).wrote(sV)
                VE(h_mask)
                if hs < 2:
                    continue

                for lvl in range(4):
                    def h_nm(t, hp=hp):
                        W.wge(t, "t", sV, sV.v)
                        R("pbD").need_write(W, t, "t")
                        for i in range(4):
                            t.matmul(pbD[32 * i:32 * i + 32, 64:96],
                                     qpow[hp][32 * i:32 * i + 32, 32:64],
                                     qpow[hp][32 * i:32 * i + 32, 0:32],
                                     start=True, stop=False, tile_position=(32 * i, 32 * i), skip_group_check=True)
                        for i in range(4):
                            t.matmul(pbD[32 * i:32 * i + 32, 96:128],
                                     qpow[hp][32 * i:32 * i + 32, 0:32],
                                     qpow[hp][32 * i:32 * i + 32, 32:64],
                                     start=False, stop=True, tile_position=(32 * i, 32 * i), skip_group_check=True)
                        sPE.inc(t.nop())
                        R("pbD").wrote(sPE)
                    TE(h_nm)
                    nm1 = sPE.v
                    def h_nme(v, hp=hp, nm1=nm1):
                        W.wge(v, "v", sPE, nm1)
                        R("qpow%d" % hp).need_write(W, v, "v")
                        sV.inc(v.tensor_copy(qpow[hp][:, 0:64], pbD[:, 64:128]))
                        R("pbD").read(sV)
                        R("qpow%d" % hp).wrote(sV)
                    VE(h_nme)
                    nme = sV.v
                    def h_na(t, hp=hp, nme=nme):
                        W.wge(t, "t", sV, nme)
                        R("pbD").need_write(W, t, "t")
                        for i in range(4):
                            t.matmul(pbD[32 * i:32 * i + 32, 128:160],
                                     qpow[hp][32 * i:32 * i + 32, 0:32],
                                     accp[hp][32 * i:32 * i + 32, 0:32],
                                     start=True, stop=True, tile_position=(32 * i, 32 * i), skip_group_check=True)
                        sPE.inc(t.nop())
                        R("pbD").wrote(sPE)
                    TE(h_na)
                    na1 = sPE.v
                    def h_nae(v, hp=hp, na1=na1):
                        W.wge(v, "v", sPE, na1)
                        sV.inc(v.scalar_tensor_tensor(accp[hp][:, 0:32], pbD[:, 128:160], 1.0,
                                                      accp[hp][:, 0:32], ALU.mult, ALU.add))
                        R("pbD").read(sV)
                        R("accp%d" % hp).wrote(sV)
                    VE(h_nae)

                if hs < 3:
                    continue
                NA = prod[hp][:, 0:128]
                NK = prod[hp][:, 128:256]
                MBt = prod[hp][:, 256:384]
                MKt = prod[hp][:, 384:512]
                for i in range(4):
                    def h_y(t, i=i, hp=hp, dc=dc, oh=oh, vtk=vtk, NA=NA, NK=NK):
                        W.wge(t, "t", sV, sV.v)
                        R("pbY").need_write(W, t, "t")
                        if i > 0:
                            t.matmul(pbY[32 * i:32 * i + 32, 0:64],
                                     NA[0:32 * i, 32 * i:32 * i + 32],
                                     X_sb[hp][0:32 * i, 0:64],
                                     start=True, stop=True, tile_position=(0, 32 * i), skip_group_check=True)
                        t.matmul(pbY[32 * i:32 * i + 32, 64:128],
                                 NK[0:32 * i + 32, 32 * i:32 * i + 32],
                                 vtk[0:32 * i + 32, :],
                                 start=True, stop=(i == 0), tile_position=(0, 32 * i), skip_group_check=True)
                        if i > 0:
                            t.matmul(pbY[32 * i:32 * i + 32, 64:128],
                                     NA[0:32 * i, 32 * i:32 * i + 32],
                                     X_sb[hp][0:32 * i, 64:128],
                                     start=False, stop=True, tile_position=(0, 32 * i), skip_group_check=True)
                        sPE.inc(t.nop())
                        R("pbY").wrote(sPE)
                    TE(h_y)
                    yt = sPE.v
                    def h_ye(v, i=i, hp=hp, dc=dc, oh=oh, yt=yt):
                        W.wge(v, "v", sPE, yt)
                        atk = watt[0][32 * i:32 * i + 32, dc * 128 + oh:dc * 128 + oh + 64]
                        if i > 0:
                            v.tensor_tensor(y_sb[hp][32 * i:32 * i + 32, 0:64],
                                            pbY[32 * i:32 * i + 32, 0:64], atk, ALU.add)
                        else:
                            v.tensor_copy(y_sb[hp][0:32, 0:64], atk)
                        sV.inc(v.tensor_copy(y_sb[hp][32 * i:32 * i + 32, 64:128],
                                             pbY[32 * i:32 * i + 32, 64:128]))
                        R("pbY").read(sV)
                        R("watt0").read(sV)
                    VE(h_ye)
                    yet = sV.v
                    def h_x(t, i=i, hp=hp, yet=yet):
                        W.wge(t, "t", sV, yet)
                        R("pbX").need_write(W, t, "t")
                        t.matmul(pbX[32 * i:32 * i + 32, 0:128],
                                 accp[hp][32 * i:32 * i + 32, 0:32],
                                 y_sb[hp][32 * i:32 * i + 32, :],
                                 start=True, stop=True, tile_position=(32 * i, 32 * i), skip_group_check=True)
                        sPE.inc(t.nop())
                        R("pbX").wrote(sPE)
                    TE(h_x)
                    xt = sPE.v
                    def h_xe(v, i=i, hp=hp, xt=xt):
                        W.wge(v, "v", sPE, xt)
                        sV.inc(v.tensor_copy(X_sb[hp][32 * i:32 * i + 32, :],
                                             pbX[32 * i:32 * i + 32, 0:128]))
                        R("pbX").read(sV)
                    VE(h_xe)
                solve_t = sV.v
                if hs < 4:
                    continue

                def h_zs(v, dc=dc, oh=oh, par=par, lam_t=lam_t):
                    W.wge(v, "v", sS, lam_t)
                    zo = z_sb[par][oh:oh + 64, dc * 64:(dc + 1) * 64]
                    qo = QT_sb[par][oh:oh + 64, dc * 64:(dc + 1) * 64]
                    v.tensor_scalar(zs_sb[oh:oh + 64, dc * 64:(dc + 1) * 64], zo,
                                    sml[oh:oh + 64, dc * 4 + 1:dc * 4 + 2], None, ALU.mult)
                    sV.inc(v.tensor_scalar(qts_sb[oh:oh + 64, dc * 64:(dc + 1) * 64], qo,
                                           sml[oh:oh + 64, dc * 4 + 1:dc * 4 + 2],
                                           None, ALU.mult))
                VE(h_zs)
                zs_t = sV.v

                def h_phi(t, hp=hp, oh=oh, solve_t=solve_t, MBt=MBt):
                    W.wge(t, "t", sV, solve_t)
                    R("pbM").need_write(W, t, "t")
                    for i in range(4):
                        t.matmul(pbM[oh:oh + 64, 64 + 32 * i:64 + 32 * (i + 1)],
                                 X_sb[hp][0:32 * (i + 1), 0:64],
                                 MBt[0:32 * (i + 1), 32 * i:32 * (i + 1)],
                                 start=(i == 0), stop=(i == 3), tile_position=(0, oh),
                                 skip_group_check=True)
                    sPE.inc(t.nop())
                    R("pbM").wrote(sPE)
                TE(h_phi)
                phi_t = sPE.v
                def h_phie(v, dc=dc, oh=oh, phi_t=phi_t):
                    W.wge(v, "v", sPE, phi_t)
                    sV.inc(v.scalar_tensor_tensor(phiT[oh:oh + 64, dc * 128:(dc + 1) * 128],
                                                  pbM[oh:oh + 64, 64:192], 1.0,
                                                  rh_sb[oh:oh + 64, dc * 128:(dc + 1) * 128],
                                                  ALU.mult, ALU.add))
                    R("pbM").read(sV)
                VE(h_phie)
                phie_t = sV.v
                if hs < 5:
                    continue

                def h_o(t, hp=hp, dc=dc, oh=oh, vtk=vtk, phie_t=phie_t, zs_t=zs_t,
                        MBt=MBt, MKt=MKt):
                    W.wge(t, "t", sV, phie_t)
                    W.wge(t, "t", sV, zs_t)
                    R("pbM").need_write(W, t, "t")
                    for i in range(4):
                        t.matmul(pbM[32 * i:32 * i + 32, 0:64],
                                 MBt[0:32 * (i + 1), 32 * i:32 * i + 32],
                                 X_sb[hp][0:32 * (i + 1), 64:128],
                                 start=True, stop=False, tile_position=(0, 32 * i), skip_group_check=True)
                        t.matmul(pbM[32 * i:32 * i + 32, 0:64],
                                 MKt[0:32 * (i + 1), 32 * i:32 * i + 32],
                                 vtk[0:32 * (i + 1), :],
                                 start=False, stop=True, tile_position=(0, 32 * i), skip_group_check=True)
                    W.wge(t, "t", sPE, sPE.v)
                    t.matmul(pbM[:, 64:128], phiT[oh:oh + 64, dc * 128:(dc + 1) * 128],
                             zs_sb[oh:oh + 64, dc * 64:(dc + 1) * 64],
                             start=True, stop=True, tile_position=(oh, 0), skip_group_check=True)
                    sPE.inc(t.nop())
                    R("pbM").wrote(sPE)
                TE(h_o)
                o_t = sPE.v
                def h_oe(v, hh=hh, o_t=o_t):
                    W.wge(v, "v", sPE, o_t)
                    v.tensor_copy(olo_sb[:, hh * 64:(hh + 1) * 64], pbM[:, 0:64])
                    sV.inc(v.tensor_tensor(olo_sb[:, hh * 64:(hh + 1) * 64],
                                           olo_sb[:, hh * 64:(hh + 1) * 64],
                                           pbM[:, 64:128], ALU.add))
                    R("pbM").read(sV)
                VE(h_oe)
                if hs < 6:
                    continue

                def h_psi(t, dc=dc, oh=oh):
                    W.wge(t, "t", sV, sV.v)
                    R("pbM").need_write(W, t, "t")
                    t.matmul(pbM[oh:oh + 64, 192:320],
                             qts_sb[oh:oh + 64, dc * 64:(dc + 1) * 64],
                             phiT[oh:oh + 64, dc * 128:(dc + 1) * 128],
                             start=True, stop=True, tile_position=(oh, oh), skip_group_check=True)
                    sPE.inc(t.nop())
                    R("pbM").wrote(sPE)
                TE(h_psi)
                psi_t = sPE.v
                def h_psie(v, dc=dc, oh=oh, psi_t=psi_t):
                    W.wge(v, "v", sPE, psi_t)
                    sV.inc(v.tensor_copy(psi_sb[oh:oh + 64, dc * 128:(dc + 1) * 128],
                                         pbM[oh:oh + 64, 192:320]))
                    R("pbM").read(sV)
                VE(h_psie)

                def h_p(t, hp=hp, dc=dc, oh=oh):
                    W.wge(t, "t", sV, sV.v)
                    R("pbM").need_write(W, t, "t")
                    t.matmul(pbM[oh:oh + 64, 320:384], X_sb[hp][:, 0:64],
                             Btok[:, dc * 128 + oh:dc * 128 + oh + 64],
                             start=True, stop=True, tile_position=(0, oh), skip_group_check=True)
                    sPE.inc(t.nop())
                    R("pbM").wrote(sPE)
                TE(h_p)
                p_t = sPE.v
                def h_pe(v, hp=hp, oh=oh, p_t=p_t):
                    W.wge(v, "v", sPE, p_t)
                    R("pr%d" % hp).need_write(W, v, "v")
                    sV.inc(v.tensor_copy(pr_sb[hp][oh:oh + 64, :], pbM[oh:oh + 64, 320:384]))
                    R("pbM").read(sV)
                    R("pr%d" % hp).wrote(sV)
                VE(h_pe)
                pe_t = sV.v
                if hs < 7:
                    continue

                def h_sloc(t, hp=hp, dc=dc, oh=oh, vtk=vtk, pe_t=pe_t):
                    W.wge(t, "t", sV, pe_t)
                    R("pbM").need_write(W, t, "t")
                    t.matmul(pbM[oh:oh + 64, 384:448],
                             Btok[:, dc * 128 + oh:dc * 128 + oh + 64],
                             X_sb[hp][:, 64:128], start=True, stop=False,
                             tile_position=(0, oh), skip_group_check=True)
                    t.matmul(pbM[oh:oh + 64, 384:448],
                             Ktok[:, dc * 128 + oh:dc * 128 + oh + 64],
                             vtk[:], start=False, stop=True, tile_position=(0, oh), skip_group_check=True)
                    W.wge(t, "t", sPE, sPE.v)
                    t.matmul(pbM[oh:oh + 64, 64:128], pr_sb[hp][oh:oh + 64, :],
                             zs_sb[oh:oh + 64, dc * 64:(dc + 1) * 64],
                             start=True, stop=True, tile_position=(oh, oh), skip_group_check=True)
                    t.matmul(pbM[oh:oh + 64, 448:512], pr_sb[hp][oh:oh + 64, :],
                             qts_sb[oh:oh + 64, dc * 64:(dc + 1) * 64],
                             start=True, stop=True, tile_position=(oh, oh), skip_group_check=True)
                    sPE.inc(t.nop())
                    R("pbM").wrote(sPE)
                TE(h_sloc)
                sl_t = sPE.v
                def h_chain(v, par=par, dc=dc, oh=oh, sl_t=sl_t):
                    W.wge(v, "v", sPE, sl_t)
                    zo = z_sb[par][oh:oh + 64, dc * 64:(dc + 1) * 64]
                    zn = z_sb[1 - par][oh:oh + 64, dc * 64:(dc + 1) * 64]
                    qo = QT_sb[par][oh:oh + 64, dc * 64:(dc + 1) * 64]
                    qn = QT_sb[1 - par][oh:oh + 64, dc * 64:(dc + 1) * 64]
                    lca = sml[oh:oh + 64, dc * 4 + 2:dc * 4 + 3]
                    v.scalar_tensor_tensor(zn, zo, lca, pbM[oh:oh + 64, 384:448],
                                           ALU.mult, ALU.add)
                    v.tensor_tensor(zn, zn, pbM[oh:oh + 64, 64:128], ALU.add)
                    sV.inc(v.scalar_tensor_tensor(qn, qo, lca, pbM[oh:oh + 64, 448:512],
                                                  ALU.mult, ALU.add))
                    R("pbM").read(sV)
                VE(h_chain)

            sp_t = sV.v
            def c_spill(sy, c=c, sp_t=sp_t):
                W.wge(sy, "y", sV, sp_t)
                ythrottle(sy); sDX.inc(sy.dma_start(psi_d[c * 128:(c + 1) * 128, :], psi_sb[:]), 16)
                ythrottle(sy); sDX.inc(sy.dma_start(olo_d[c * 128:(c + 1) * 128, :], olo_sb[:]), 16)
            SY(c_spill)
            barrier()

    fpar = NCHK % 2

    if "rec" in taps:
        def tapzd(sy, fpar=fpar):
            W.wge(sy, "y", sV, sV.v)
            ythrottle(sy); sDX.inc(sy.dma_start(tap_d["tp_z"][:], z_sb[fpar][:]), 16)
        SY(tapzd)

    if level <= 1:
        fin0 = sV.v
        def out_dma0(sy, fin0=fin0):
            W.wge(sy, "y", sV, fin0)
            W.wge(sy, "y", sS, sS.v)
            for dc in range(DCN):
                ythrottle(sy); sDX.inc(sy.dma_start(out_d[dc * 128:(dc + 1) * 128, :],
                                     xT[:, dc * XS + 1:(dc + 1) * XS]), 16)
            sy.wait_ge(sDX.h, sDX.v)
        SY(out_dma0)
        blk_cm.__exit__(None, None, None)
        for cm in pctx[::-1]:
            cm.__exit__(None, None, None)
        for cm in actx[::-1]:
            cm.__exit__(None, None, None)
        for cm in abctx[::-1]:
            cm.__exit__(None, None, None)
        for cm in ctx[::-1]:
            cm.__exit__(None, None, None)
        return nc

    for dc in range(DCN):
        def f_q(t, dc=dc, fpar=fpar):
            W.wge(t, "t", sV, sV.v)
            R("pbY").need_write(W, t, "t")
            t.transpose(pbY[0:64, 0:128], QT_sb[fpar][:, dc * 64:(dc + 1) * 64], cid_f[:])
            sPE.inc(t.nop())
            R("pbY").wrote(sPE)
        TE(f_q)
        fq1 = sPE.v
        def f_qe(v, dc=dc, fq1=fq1):
            W.wge(v, "v", sPE, fq1)
            R("cw").need_write(W, v, "v")
            sV.inc(v.tensor_copy(qfin[0:64, dc * 129:dc * 129 + 128], pbY[0:64, 0:128]))
            R("pbY").read(sV)
            R("cw").wrote(sV)
        VE(f_qe)
    fq_t = sV.v

    def f_ccd(sy, fpar=fpar, fq_t=fq_t):
        W.wge(sy, "y", sV, fq_t)
        for dc in range(DCN):
            ythrottle(sy); sDX.inc(sy.dma_start(
                cc_in.ap()[dc * 128:(dc + 1) * 128, :].rearrange("(o p) k -> p o k", o=2),
                qfin[0:64, dc * 129:dc * 129 + 128].rearrange("p (o k) -> p o k", o=2)), 16)
            ythrottle(sy); sDX.inc(sy.dma_start(cc_in.ap()[1024 + dc * 128:1024 + (dc + 1) * 128, :],
                                 z_sb[fpar][:, dc * 64:(dc + 1) * 64]), 16)
    SY(f_ccd)
    ccd_t = sDX.v

    def f_cc(g, ccd_t=ccd_t):
        W.wge(g, "g", sDX, ccd_t)
        g.collective_compute("AllGather", ALU.bypass,
                             replica_groups=[list(range(NCORE))],
                             ins=[cc_in.ap().opt()], outs=[cc_out.ap().opt()]).then_inc(sCC.h)
        sCC.v += 1
        g.wait_ge(sCC.h, sCC.v)
        sG.inc(g.nop())
    GE(f_cc)
    barrier()

    for cm in actx[::-1]:
        cm.__exit__(None, None, None)

    bctx = []
    bsb = mksb(bctx)
    xffn = bsb("xffn", [128, DCN * TC], BF16)
    dwT = bsb("dwT", [8, 2048], BF16)
    agq = bsb("agq", [128, DCN * 64], F32)
    agz = bsb("agz", [128, DCN * 64], F32)
    gcur = bsb("gcur", [128, DCN * 64], F32)
    gnext = bsb("gnext", [128, DCN * 64], F32)
    gnw_bc = bsb("gnw_bc", [128, 1024], F32)
    gnb_bc = bsb("gnb_bc", [128, 1024], F32)
    gtmp = bsb("gtmpB", [1, 2048], BF16)

    def b_gld(g):
        W.wge(g, "g", sG, sG.v)
        gthrottle(g); sDW.inc(g.dma_start(gtmp[0:1, 0:1024], vecs_d[:, 1:2].rearrange("d o -> o d"), allow_slow_non_contiguous=True), 16)
        gthrottle(g); sDW.inc(g.dma_start(gtmp[0:1, 1024:2048], vecs_d[:, 2:3].rearrange("d o -> o d"), allow_slow_non_contiguous=True), 16)
    GE(b_gld)
    gld_t = sDW.v
    for half, dst in ((0, gnw_bc), (1, gnb_bc)):
        for piece in range(2):
            def bg_mm(t, half=half, piece=piece, gld_t=gld_t):
                W.wge(t, "t", sDW, gld_t)
                W.wge(t, "t", sV, sV.v)
                R("pbC").need_write(W, t, "t")
                t.matmul(pbC[:, 0:512], conesr[0:1, :],
                         gtmp[0:1, half * 1024 + piece * 512:half * 1024 + (piece + 1) * 512],
                         start=True, stop=True)
                sPE.inc(t.nop())
                R("pbC").wrote(sPE)
            TE(bg_mm)
            bg1 = sPE.v
            def bg_ev(v, dst=dst, piece=piece, bg1=bg1):
                W.wge(v, "v", sPE, bg1)
                sV.inc(v.tensor_copy(dst[:, piece * 512:(piece + 1) * 512], pbC[:, 0:512]))
                R("pbC").read(sV)
            VE(bg_ev)

    def b_g0(v):
        W.wge(v, "v", sG, sG.v)
        sV.inc(v.memset(gcur[:], 0.0))
    VE(b_g0)

    for j in range(NCORE):
        def b_ld(g, j=j):
            W.wge(g, "g", sV, sV.v)
            W.wge(g, "g", sDW, sDW.v)
            R("agq").need_write(W, g, "g")
            gthrottle(g); sDW.inc(g.dma_start(agq[:].rearrange("p (c k) -> p c k", c=DCN),
                                cc_out.ap()[j * 2048:j * 2048 + 1024, :]
                                .rearrange("(c p) k -> p c k", p=128)), 16)
            gthrottle(g); sDW.inc(g.dma_start(agz[:].rearrange("p (c k) -> p c k", c=DCN),
                                cc_out.ap()[j * 2048 + 1024:j * 2048 + 2048, :]
                                .rearrange("(c p) k -> p c k", p=128)), 16)
            R("agq").wrote(sDW)
        GE(b_ld)
        ld_t = sDW.v
        for dc in range(DCN):
            for oh in (0, 64):
                def b_cm(t, dc=dc, oh=oh, ld_t=ld_t):
                    W.wge(t, "t", sDW, ld_t)
                    W.wge(t, "t", sV, sV.v)
                    R("pbY").need_write(W, t, "t")
                    t.matmul(pbY[oh:oh + 64, 0:64],
                             agq[oh:oh + 64, dc * 64:(dc + 1) * 64],
                             gcur[oh:oh + 64, dc * 64:(dc + 1) * 64],
                             start=True, stop=True, tile_position=(oh, oh), skip_group_check=True)
                    sPE.inc(t.nop())
                    R("pbY").wrote(sPE)
                TE(b_cm)
                cmv = sPE.v
                def b_ce(v, j=j, dc=dc, oh=oh, cmv=cmv):
                    W.wge(v, "v", sPE, cmv)
                    v.tensor_tensor(gnext[oh:oh + 64, dc * 64:(dc + 1) * 64],
                                    pbY[oh:oh + 64, 0:64],
                                    agz[oh:oh + 64, dc * 64:(dc + 1) * 64], ALU.add)
                    sV.inc(v.scalar_tensor_tensor(
                        SinT[oh:oh + 64, dc * 64:(dc + 1) * 64],
                        gnext[oh:oh + 64, dc * 64:(dc + 1) * 64],
                        selrep[oh:oh + 64, j:j + 1],
                        SinT[oh:oh + 64, dc * 64:(dc + 1) * 64], ALU.mult, ALU.add))
                    R("pbY").read(sV)
                VE(b_ce)
        def b_swap(v, j=j):
            W.wge(v, "v", sV, sV.v)
            sV.inc(v.tensor_copy(gcur[:], gnext[:]))
        VE(b_swap)

    def b_sc(v):
        W.wge(v, "v", sV, sV.v)
        sV.inc(v.tensor_copy(sinb[:], SinT[:]))
    VE(b_sc)
    sin_t = sV.v

    if "attn" in taps:
        def tapsin(sy, sin_t=sin_t):
            W.wge(sy, "y", sV, sin_t)
            ythrottle(sy); sDX.inc(sy.dma_start(tap_d["tp_sin"][:], SinT[:]), 16)
        SY(tapsin)

    wo_sb = bsb("wo_sb", [128, DCN * 1024], BF16)
    def b_wol(g):
        gthrottle(g); sDW.inc(g.dma_start(wo_sb[:].rearrange("p (c e) -> p c e", c=DCN), attw_d[7].rearrange("(c p) e -> p c e", p=128)), 16)
    GE(b_wol)
    wol_t = sDW.v

    o_f = bsb("o_f", [128, 1024], F32)
    osq = bsb("osq", [128, 1024], F32)
    gn16 = bsb("gn16", [128, 64], F32)
    comb = bsb("comb", [128, 1024], F32)
    nrm = bsb("nrm", [128, 1024], BF16)
    gch = bsb("gch", [128, DCN * 128], BF16)
    ngT = bsb("ngT", [128, DCN * 128], BF16)
    bpsi = bsb("bpsi", [128, DCN * 128], BF16)
    bolo = bsb("bolo", [128, H * 64], BF16)

    for c in range(NCHK):
        def b_cl(g, c=c):
            W.wge(g, "g", sDW, sDW.v)
            R("bpsi").need_write(W, g, "g")
            R("bolo").need_write(W, g, "g")
            gthrottle(g); sDW.inc(g.dma_start(bpsi[:], psi_d[c * 128:(c + 1) * 128, :]), 16)
            gthrottle(g); sDW.inc(g.dma_start(bolo[:], olo_d[c * 128:(c + 1) * 128, :]), 16)
            gthrottle(g); sDW.inc(g.dma_start(gch[:].rearrange("p (c2 t) -> p c2 t", c2=DCN),
                                g_d.rearrange("(c2 p) t -> p c2 t", p=128)[:, :, c * 128:(c + 1) * 128]), 16)
            R("bpsi").wrote(sDW)
        GE(b_cl)
        bcl_t = sDW.v

        def b_sig1(s, bcl_t=bcl_t):
            W.wge(s, "s", sDW, bcl_t)
            R("comb").need_write(W, s, "s")
            for dc in range(DCN):
                s.activation(comb[:, dc * 128:(dc + 1) * 128],
                             gch[:, dc * 128:(dc + 1) * 128], AF.Exp, scale=-1.0)
            sS.inc(s.nop())
            R("comb").wrote(sS)
        SE(b_sig1)
        sg_t = sS.v
        def b_sig2(v, sg_t=sg_t):
            W.wge(v, "v", sS, sg_t)
            v.tensor_scalar_add(comb[:], comb[:], 1.0)
            v.reciprocal(comb[:], comb[:])
            sV.inc(v.tensor_copy(gch[:], comb[:]))
            R("comb").read(sV)
        VE(b_sig2)
        gsig_t = sV.v

        for half in range(2):
            bi = half
            def b_corr(t, half=half, bi=bi, sin_t=sin_t, bcl_t=bcl_t):
                W.wge(t, "t", sV, sin_t)
                W.wge(t, "t", sDW, bcl_t)
                R("pbP%d" % bi).need_write(W, t, "t")
                for ohh_pass in (0, 64):
                    for hx in range(8):
                        hh = half * 8 + hx
                        dch = hh // 2
                        ohh = 64 * (hh % 2)
                        if ohh != ohh_pass:
                            continue
                        t.matmul(pbP[bi][:, hx * 64:(hx + 1) * 64],
                                 bpsi[ohh:ohh + 64, dch * 128:(dch + 1) * 128],
                                 sinb[ohh:ohh + 64, dch * 64:(dch + 1) * 64],
                                 start=True, stop=True, tile_position=(ohh, 0), skip_group_check=True)
                    if ohh_pass == 0:
                        W.wge(t, "t", sPE, sPE.v)
                sPE.inc(t.nop())
                R("pbP%d" % bi).wrote(sPE)
                R("bpsi").read(sPE)
            TE(b_corr)
            cor_t = sPE.v
            def b_core(v, half=half, bi=bi, cor_t=cor_t):
                W.wge(v, "v", sPE, cor_t)
                R("o_f").need_write(W, v, "v")
                sV.inc(v.scalar_tensor_tensor(o_f[:, half * 512:(half + 1) * 512],
                                              pbP[bi][:, 0:512], 1.0,
                                              bolo[:, half * 512:(half + 1) * 512],
                                              ALU.mult, ALU.add))
                R("pbP%d" % bi).read(sV)
                R("bolo").read(sV)
            VE(b_core)
        oc_t = sV.v

        if "rec" in taps:
            def tapo(sy, c=c, oc_t=oc_t):
                W.wge(sy, "y", sV, oc_t)
                ythrottle(sy); sDX.inc(sy.dma_start(tap_d["tp_o"][c * 128:(c + 1) * 128, :], o_f[:]), 16)
                R("o_f").read(sDX)
            SY(tapo)

        def b_gn1(s, oc_t=oc_t):
            W.wge(s, "s", sV, oc_t)
            R("osq").need_write(W, s, "s")
            sS.inc(s.activation(osq[:], o_f[:], AF.Square))
            R("osq").wrote(sS)
        SE(b_gn1)
        gn1 = sS.v
        def b_gn2(v, gn1=gn1):
            W.wge(v, "v", sS, gn1)
            o3 = o_f[:].rearrange("p (h k) -> p h k", h=H)
            q3 = osq[:].rearrange("p (h k) -> p h k", h=H)
            v.tensor_reduce(gn16[:, 0:16].rearrange("p (h o) -> p h o", o=1), o3, AX.X, ALU.add)
            v.tensor_reduce(gn16[:, 16:32].rearrange("p (h o) -> p h o", o=1), q3, AX.X, ALU.add)
            v.tensor_scalar(gn16[:, 0:16], gn16[:, 0:16], 1.0 / KD, None, ALU.mult)
            v.tensor_scalar(gn16[:, 16:32], gn16[:, 16:32], 1.0 / KD, None, ALU.mult)
            v.tensor_tensor(gn16[:, 32:48], gn16[:, 0:16], gn16[:, 0:16], ALU.mult)
            sV.inc(v.tensor_tensor(gn16[:, 16:32], gn16[:, 16:32], gn16[:, 32:48], ALU.subtract))
            R("osq").read(sV)
        VE(b_gn2)
        gn2 = sV.v
        def b_gn3(s, gn2=gn2):
            W.wge(s, "s", sV, gn2)
            s.activation(gn16[:, 32:48], gn16[:, 16:32], AF.Ln, bias=kb[:, 2:3])
            sS.inc(s.activation(gn16[:, 32:48], gn16[:, 32:48], AF.Exp, scale=-0.5))
        SE(b_gn3)
        gn3 = sS.v
        def b_gn4(v, gn3=gn3):
            W.wge(v, "v", sS, gn3)
            rb = gn16[:, 32:48].rearrange("p (h o) -> p h o", o=1).broadcast_to([128, H, KD])
            mb = gn16[:, 0:16].rearrange("p (h o) -> p h o", o=1).broadcast_to([128, H, KD])
            v.tensor_tensor(comb[:].rearrange("p (h k) -> p h k", h=H), rb,
                            gnw_bc[:].rearrange("p (h k) -> p h k", h=H), ALU.mult)
            v.tensor_tensor(osq[:].rearrange("p (h k) -> p h k", h=H),
                            o_f[:].rearrange("p (h k) -> p h k", h=H), mb, ALU.subtract)
            v.tensor_tensor(osq[:], osq[:], comb[:], ALU.mult)
            sV.inc(v.tensor_tensor(nrm[:], osq[:], gnb_bc[:], ALU.add))
        VE(b_gn4)
        gn4 = sV.v

        for dc in range(DCN):
            def b_tp(t, dc=dc, gn4=gn4):
                W.wge(t, "t", sV, gn4)
                R("pbT").need_write(W, t, "t")
                t.transpose(pbT[:, 0:128], nrm[:, dc * 128:(dc + 1) * 128], cid_b[:])
                sPE.inc(t.nop())
                R("pbT").wrote(sPE)
            TE(b_tp)
            btp = sPE.v
            def b_tpe(v, dc=dc, btp=btp, gsig_t=gsig_t):
                W.wge(v, "v", sPE, btp)
                W.wge(v, "v", sV, gsig_t)
                sV.inc(v.tensor_tensor(ngT[:, dc * 128:(dc + 1) * 128], pbT[:, 0:128],
                                       gch[:, dc * 128:(dc + 1) * 128], ALU.mult))
                R("pbT").read(sV)
                if dc == DCN - 1:
                    R("bolo").read(sV)
            VE(b_tpe)
        ng_t = sV.v

        for ec in range(DCN):
            bi = ec % 2
            def b_wo(t, ec=ec, bi=bi, ng_t=ng_t, wol_t=wol_t):
                W.wge(t, "t", sV, ng_t)
                W.wge(t, "t", sDW, wol_t)
                R("pbP%d" % bi).need_write(W, t, "t")
                for dc in range(DCN):
                    t.matmul(pbP[bi][:, 0:128],
                             wo_sb[:, dc * 1024 + ec * 128:dc * 1024 + (ec + 1) * 128],
                             ngT[:, dc * 128:(dc + 1) * 128],
                             start=(dc == 0), stop=(dc == DCN - 1))
                sPE.inc(t.nop())
                R("pbP%d" % bi).wrote(sPE)
            TE(b_wo)
            wo_t = sPE.v
            def b_woe(s, ec=ec, bi=bi, wo_t=wo_t):
                W.wge(s, "s", sPE, wo_t)
                R("osq").need_write(W, s, "s")
                sS.inc(s.activation(osq[:, 0:128], pbP[bi][:, 0:128], AF.Identity,
                                    bias=vecs_sb[:, ec * 4:ec * 4 + 1]))
                R("osq").wrote(sS)
            SE(b_woe)
            woe_t = sS.v
            def b_res(v, c=c, ec=ec, bi=bi, woe_t=woe_t):
                W.wge(v, "v", sS, woe_t)
                xs = xT[:, ec * XS + 1 + c * 128: ec * XS + 1 + (c + 1) * 128]
                sV.inc(v.tensor_tensor(xs, xs, osq[:, 0:128], ALU.add))
                R("pbP%d" % bi).read(sV)
                R("osq").read(sV)
            VE(b_res)

    barrier()
    if "attn" in taps:
        def tapx1(sy):
            W.wge(sy, "y", sV, sV.v)
            for dc in range(DCN):
                ythrottle(sy); sDX.inc(sy.dma_start(tap_d["tp_x1"][dc * 128:(dc + 1) * 128, :],
                                     xT[:, dc * XS + 1:(dc + 1) * XS]), 16)
            R("xTr").read(sDX)
        SY(tapx1)

    bmsc = bsb("bmsc", [128, 512], BF16)
    bstatb = bsb("bstatb", [1, 1024], BF16)
    TB2 = 512
    for it in range(TC // TB2):
        t0 = it * TB2
        def l2_a(v, t0=t0):
            W.wge(v, "v", sV, sV.v)
            W.wge(v, "v", sDX, sDX.v)
            for dc in range(DCN):
                v.tensor_copy(xffn[:, dc * TC + t0:dc * TC + t0 + TB2],
                              xT[:, dc * XS + 1 + t0: dc * XS + 1 + t0 + TB2])
            sV.inc(v.engine_nop())
        VE(l2_a)
        l2a = sV.v
        def l2_mm1(t, t0=t0, l2a=l2a):
            W.wge(t, "t", sV, l2a)
            R("pbC").need_write(W, t, "t")
            for dc in range(DCN):
                t.matmul(pbC[0:1, 0:TB2], conesr[:, 0:1],
                         xffn[:, dc * TC + t0:dc * TC + t0 + TB2],
                         start=(dc == 0), stop=(dc == DCN - 1))
            sPE.inc(t.nop())
            R("pbC").wrote(sPE)
        TE(l2_mm1)
        l2m1 = sPE.v
        def l2_mv1(v, l2m1=l2m1):
            W.wge(v, "v", sPE, l2m1)
            sV.inc(v.tensor_scalar(wt2[0:1, 0:TB2], pbC[0:1, 0:TB2], 1.0 / D, None, ALU.mult))
            R("pbC").read(sV)
        VE(l2_mv1)
        for dc in range(DCN):
            def l2_sq2(v, dc=dc, t0=t0):
                R("bmsc").need_write(W, v, "v")
                sV.inc(v.tensor_tensor(bmsc[:, 0:TB2],
                                       xffn[:, dc * TC + t0:dc * TC + t0 + TB2],
                                       xffn[:, dc * TC + t0:dc * TC + t0 + TB2], ALU.mult))
                R("bmsc").wrote(sV)
            VE(l2_sq2)
            sq2 = sV.v
            def l2_mm2(t, dc=dc, sq2=sq2):
                W.wge(t, "t", sV, sq2)
                if dc == 0:
                    R("pbP0").need_write(W, t, "t")
                t.matmul(pbP[0][0:1, 0:TB2], conesr[:, 0:1], bmsc[:, 0:TB2],
                         start=(dc == 0), stop=(dc == DCN - 1))
                sPE.inc(t.nop())
                if dc == DCN - 1:
                    R("pbP0").wrote(sPE)
                R("bmsc").read(sPE)
            TE(l2_mm2)
        l2m2 = sPE.v
        def l2_mv2(v, l2m2=l2m2):
            W.wge(v, "v", sPE, l2m2)
            v.tensor_scalar(wt1[0:1, 0:TB2], pbP[0][0:1, 0:TB2], 1.0 / D, None, ALU.mult)
            R("pbP0").read(sV)
            v.tensor_tensor(osq[0:1, 0:TB2], wt2[0:1, 0:TB2], wt2[0:1, 0:TB2], ALU.mult)
            sV.inc(v.tensor_tensor(wt1[0:1, 0:TB2], wt1[0:1, 0:TB2],
                                   osq[0:1, 0:TB2], ALU.subtract))
        VE(l2_mv2)
        l2v = sV.v
        def l2_rs(s, l2v=l2v):
            W.wge(s, "s", sV, l2v)
            s.activation(wt1[0:1, 0:TB2], wt1[0:1, 0:TB2], AF.Ln, bias=kb[0:1, 0:1])
            sS.inc(s.activation(wt1[0:1, 0:TB2], wt1[0:1, 0:TB2], AF.Exp, scale=-0.5))
        SE(l2_rs)
        l2r = sS.v
        def l2_c2(v, l2r=l2r):
            W.wge(v, "v", sS, l2r)
            v.tensor_copy(bstatb[0:1, 0:TB2], wt2[0:1, 0:TB2])
            sV.inc(v.tensor_copy(bstatb[0:1, TB2:2 * TB2], wt1[0:1, 0:TB2]))
        VE(l2_c2)
        l2c2 = sV.v
        def l2_bc(t, l2c2=l2c2):
            W.wge(t, "t", sV, l2c2)
            R("pbC").need_write(W, t, "t")
            R("pbP0").need_write(W, t, "t")
            t.matmul(pbC[:, 0:TB2], conesr[0:1, :], bstatb[0:1, 0:TB2], start=True, stop=True)
            sPE.inc(t.matmul(pbP[0][:, 0:TB2], conesr[0:1, :], bstatb[0:1, TB2:2 * TB2],
                             start=True, stop=True))
            R("pbC").wrote(sPE)
            R("pbP0").wrote(sPE)
        TE(l2_bc)
        l2b = sPE.v
        def l2_f(v, t0=t0, l2b=l2b):
            W.wge(v, "v", sPE, l2b)
            R("wt1").need_write(W, v, "v")
            for dc in range(DCN):
                v.tensor_tensor(wt1[:, 0:TB2], xT[:, dc * XS + 1 + t0: dc * XS + 1 + t0 + TB2],
                                pbC[:, 0:TB2], ALU.subtract)
                v.tensor_tensor(xffn[:, dc * TC + t0:dc * TC + t0 + TB2],
                                wt1[:, 0:TB2], pbP[0][:, 0:TB2], ALU.mult)
            sV.inc(v.engine_nop())
            R("pbC").read(sV)
            R("pbP0").read(sV)
            R("wt1").read(sV)
            R("wt1").wrote(sV)
        VE(l2_f)

    if "xffn" in taps:
        for dc in range(DCN):
            for hl in range(2):
                def txf(v, dc=dc, hl=hl):
                    W.wge(v, "v", sV, sV.v)
                    W.wge(v, "v", sDX, sDX.v)
                    R("o_f").need_write(W, v, "v")
                    sV.inc(v.tensor_copy(o_f[:, 0:1024],
                                         xffn[:, dc * TC + hl * 1024:dc * TC + (hl + 1) * 1024]))
                VE(txf)
                t_ = sV.v
                def txfd(sy, dc=dc, hl=hl, t_=t_):
                    W.wge(sy, "y", sV, t_)
                    ythrottle(sy); sDX.inc(sy.dma_start(
                        tap_d["tp_xffn"][dc * 128:(dc + 1) * 128, hl * 1024:(hl + 1) * 1024],
                        o_f[:, 0:1024]), 16)
                    R("o_f").read(sDX)
                SY(txfd)

    logit = bsb("logit", [128, 16 * 8], F32)
    d1_sb = bsb("d1_sb", [128, 16 * 8], F32)
    d2_sb = bsb("d2_sb", [128, 16 * 8], F32)
    se1_sb = bsb("se1_sb", [128, 16 * 8], F32)
    se2_sb = bsb("se2_sb", [128, 16 * 8], F32)
    oh2_sb = bsb("oh2_sb", [128, 16 * 8], F32)
    oh1_sb = bsb("oh1_sb", [128, 16 * 8], F32)
    smx = bsb("smx", [128, 16], F32)
    dwtm = bsb("dwtm", [128, 8], BF16)

    for tt in range(16):
        def r_mm(t, tt=tt):
            W.wge(t, "t", sV, sV.v)
            R("pbC").need_write(W, t, "t")
            for dc in range(DCN):
                t.matmul(pbC[0:8, 0:128], routw_sb[:, dc * 8:(dc + 1) * 8],
                         xffn[:, dc * TC + tt * 128:dc * TC + (tt + 1) * 128],
                         start=(dc == 0), stop=(dc == DCN - 1))
            sPE.inc(t.nop())
            R("pbC").wrote(sPE)
        TE(r_mm)
        rmt = sPE.v
        def r_ev(v, tt=tt, rmt=rmt):
            W.wge(v, "v", sPE, rmt)
            R("wt1").need_write(W, v, "v")
            sV.inc(v.tensor_copy(wt1[0:8, 0:128], pbC[0:8, 0:128]))
            R("pbC").read(sV)
            R("wt1").wrote(sV)
        VE(r_ev)
        ret = sV.v
        def r_tp(t, tt=tt, ret=ret):
            W.wge(t, "t", sV, ret)
            R("pbX").need_write(W, t, "t")
            t.transpose(pbX[:, 0:8], wt1[0:8, 0:128], cid_f[0:8, 0:8])
            sPE.inc(t.nop())
            R("pbX").wrote(sPE)
            R("wt1").read(sPE)
        TE(r_tp)
        rtt = sPE.v
        def r_te(v, tt=tt, rtt=rtt):
            W.wge(v, "v", sPE, rtt)
            sV.inc(v.tensor_copy(logit[:, tt * 8:(tt + 1) * 8], pbX[:, 0:8]))
            R("pbX").read(sV)
        VE(r_te)

    def r_s1(v):
        W.wge(v, "v", sV, sV.v)
        for tt in range(16):
            lg = logit[:, tt * 8:(tt + 1) * 8]
            m1 = smx[:, 0:1]
            m2 = smx[:, 1:2]
            f_ = smx[:, 2:10]
            v.tensor_reduce(m1, lg, AX.X, ALU.max)
            m1b = m1.broadcast_to([128, 8])
            zb = kb[:, 3:4].broadcast_to([128, 8])
            v.tensor_tensor(oh1_sb[:, tt * 8:(tt + 1) * 8], lg, m1b, ALU.is_ge)
            v.tensor_scalar(d1_sb[:, tt * 8:(tt + 1) * 8], lg, m1, None, ALU.subtract)
            v.tensor_scalar(f_, lg, -1.0, None, ALU.mult)
            v.tensor_tensor(f_, f_, lg, ALU.max)
            v.tensor_scalar(f_, f_, m1, None, ALU.max)
            v.scalar_tensor_tensor(se1_sb[:, tt * 8:(tt + 1) * 8], f_, 0.02,
                                   d1_sb[:, tt * 8:(tt + 1) * 8], ALU.mult, ALU.add)
            v.tensor_tensor(se1_sb[:, tt * 8:(tt + 1) * 8],
                            se1_sb[:, tt * 8:(tt + 1) * 8], zb, ALU.is_ge)
            mk = smx[:, 2:10]
            v.scalar_tensor_tensor(mk, oh1_sb[:, tt * 8:(tt + 1) * 8], -1e30, lg,
                                   ALU.mult, ALU.add)
            v.tensor_reduce(m2, mk, AX.X, ALU.max)
            m2b = m2.broadcast_to([128, 8])
            v.tensor_tensor(oh2_sb[:, tt * 8:(tt + 1) * 8], mk, m2b, ALU.is_ge)
            v.tensor_scalar(d2_sb[:, tt * 8:(tt + 1) * 8], mk, m2, None, ALU.subtract)
            f2 = smx[:, 2:10]
            v.tensor_scalar(f2, lg, -1.0, None, ALU.mult)
            v.tensor_tensor(f2, f2, lg, ALU.max)
            v.tensor_scalar(f2, f2, m2, None, ALU.max)
            v.tensor_scalar(se2_sb[:, tt * 8:(tt + 1) * 8], lg, m2, None, ALU.subtract)
            v.scalar_tensor_tensor(se2_sb[:, tt * 8:(tt + 1) * 8], f2, 0.02,
                                   se2_sb[:, tt * 8:(tt + 1) * 8], ALU.mult, ALU.add)
            v.tensor_tensor(se2_sb[:, tt * 8:(tt + 1) * 8],
                            se2_sb[:, tt * 8:(tt + 1) * 8], zb, ALU.is_ge)
        sV.inc(v.engine_nop())
    VE(r_s1)
    s1_t = sV.v
    def r_exp(s, s1_t=s1_t):
        W.wge(s, "s", sV, s1_t)
        s.activation(d1_sb[:], d1_sb[:], AF.Exp)
        sS.inc(s.activation(d2_sb[:], d2_sb[:], AF.Exp))
    SE(r_exp)
    ex_t = sS.v
    def r_s2(v, ex_t=ex_t):
        W.wge(v, "v", sS, ex_t)
        for tt in range(16):
            sl_ = slice(tt * 8, (tt + 1) * 8)
            den1 = smx[:, 0:1]
            den2 = smx[:, 1:2]
            v.tensor_tensor(d1_sb[:, sl_], d1_sb[:, sl_], se1_sb[:, sl_], ALU.mult)
            v.tensor_tensor(d2_sb[:, sl_], d2_sb[:, sl_], se2_sb[:, sl_], ALU.mult)
            v.tensor_reduce(den1, d1_sb[:, sl_], AX.X, ALU.add)
            v.tensor_reduce(den2, d2_sb[:, sl_], AX.X, ALU.add)
            v.reciprocal(den1, den1)
            v.reciprocal(den2, den2)
            v.tensor_scalar(oh2_sb[:, sl_], oh2_sb[:, sl_], den2, None, ALU.mult)
            v.scalar_tensor_tensor(oh1_sb[:, sl_], oh1_sb[:, sl_], den1,
                                   oh2_sb[:, sl_], ALU.mult, ALU.add)
        sV.inc(v.engine_nop())
    VE(r_s2)
    dw_t = sV.v

    if "dw" in taps:
        def tapdwd(sy, dw_t=dw_t):
            W.wge(sy, "y", sV, dw_t)
            ythrottle(sy); sDX.inc(sy.dma_start(tap_d["tp_dw"].rearrange("(t p) e -> p t e", p=128),
                                 oh1_sb[:].rearrange("p (t e) -> p t e", e=8)), 16)
        SY(tapdwd)

    for tt in range(16):
        def dw_c(v, tt=tt, dw_t=dw_t):
            W.wge(v, "v", sV, dw_t)
            R("dwtm").need_write(W, v, "v")
            sV.inc(v.tensor_copy(dwtm[:], oh1_sb[:, tt * 8:(tt + 1) * 8]))
            R("dwtm").wrote(sV)
        VE(dw_c)
        dwc = sV.v
        def dw_tp(t, tt=tt, dwc=dwc):
            W.wge(t, "t", sV, dwc)
            R("pbT").need_write(W, t, "t")
            t.transpose(pbT[0:8, 0:128], dwtm[:], cid_b[:])
            sPE.inc(t.nop())
            R("pbT").wrote(sPE)
            R("dwtm").read(sPE)
        TE(dw_tp)
        dtt = sPE.v
        def dw_te(v, tt=tt, dtt=dtt):
            W.wge(v, "v", sPE, dtt)
            sV.inc(v.tensor_copy(dwT[0:8, tt * 128:(tt + 1) * 128], pbT[0:8, 0:128]))
            R("pbT").read(sV)
        VE(dw_te)

    bsp_t = sV.v
    def b_spill(sy, bsp_t=bsp_t):
        W.wge(sy, "y", sV, bsp_t)
        ythrottle(sy); sDX.inc(sy.dma_start(
            xffn_d.rearrange("(c p) t -> p c t", p=128),
            xffn[:].rearrange("p (c t) -> p c t", c=DCN)), 16)
        ythrottle(sy); sDX.inc(sy.dma_start(dwT_d, dwT[0:8, :]), 16)
        R("xffnd").wrote(sDX)
    SY(b_spill)

    barrier()
    if level <= 2:
        fin2 = sV.v
        def out_dma2(sy, fin2=fin2):
            W.wge(sy, "y", sV, fin2)
            for dc in range(DCN):
                ythrottle(sy); sDX.inc(sy.dma_start(out_d[dc * 128:(dc + 1) * 128, :],
                                     xT[:, dc * XS + 1:(dc + 1) * XS]), 16)
            sy.wait_ge(sDX.h, sDX.v)
        SY(out_dma2)
        blk_cm.__exit__(None, None, None)
        for cm in pctx[::-1]:
            cm.__exit__(None, None, None)
        for cm in bctx[::-1]:
            cm.__exit__(None, None, None)
        for cm in abctx[::-1]:
            cm.__exit__(None, None, None)
        for cm in bcctx[::-1]:
            cm.__exit__(None, None, None)
        for cm in ctx[::-1]:
            cm.__exit__(None, None, None)
        return nc
    for cm in bctx[::-1]:
        cm.__exit__(None, None, None)
    for cm in abctx[::-1]:
        cm.__exit__(None, None, None)
    abctx = []

    cctx = []
    csb = mksb(cctx)
    wgat = [csb("wgat%d" % p, [128, DCN * 1024], BF16) for p in range(2)]
    wup = csb("wup", [128, DCN * 1024], BF16)
    wdn = csb("wdn", [128, DCN * 1024], BF16)
    h_sb = csb("h_sb", [128, DCN * 1024], BF16)
    dwb = csb("dwb", [128, 1024], BF16)
    gs_sb = csb("gs_sb", [128, 512], F32)
    xffn = csb("xffn_c", [128, DCN * TC], BF16)
    dwT = csb("dwT_c", [8, 2048], BF16)

    def c_reload(g):
        R("xffnd").need_read(W, g, "g")
        gthrottle(g); sDW.inc(g.dma_start(
            xffn[:].rearrange("p (c t) -> p c t", c=DCN),
            xffn_d.rearrange("(c p) t -> p c t", p=128)), 16)
        gthrottle(g); sDW.inc(g.dma_start(dwT[0:8, :], dwT_d), 16)
        R("xffnc").wrote(sDW)
    GE(c_reload)

    def c_reload_w(v):
        R("xffnc").need_read(W, v, "v")
        sV.inc(v.engine_nop())
    VE(c_reload_w)
    xfr_t = sV.v

    pbG = [pbP[0], pbD]
    pbU = [pbP[1], pbY]
    pbDN = [pbC, pbX]

    TH = 1024
    units = [(e, fh) for e in range(E) for fh in range(2)]

    def moe_load_g(u):
        e, fh = units[u]
        p = u % 2
        def m_wl(g, e=e, fh=fh, p=p):
            R("wmoe%d" % p).need_write(W, g, "g")
            gthrottle(g); sDW.inc(g.dma_start(
                wgat[p][:].rearrange("p (c f) -> p c f", c=DCN),
                wg_d[e].rearrange("(c p) f -> p c f", p=128)[:, :, fh * 1024:(fh + 1) * 1024]), 16)
            R("wmoe%d" % p).wrote(sDW)
        GE(m_wl)
        return sDW.v

    def moe_load_u(u):
        e, fh = units[u]
        def m_wlu(g, e=e, fh=fh):
            R("wupr").need_write(W, g, "g")
            gthrottle(g); sDW.inc(g.dma_start(
                wup[:].rearrange("p (c f) -> p c f", c=DCN),
                wu_d[e].rearrange("(c p) f -> p c f", p=128)[:, :, fh * 1024:(fh + 1) * 1024]), 16)
            R("wupr").wrote(sDW)
        GE(m_wlu)
        return sDW.v

    def moe_load_dn(u):
        e, fh = units[u]
        def m_wl2(g, e=e, fh=fh):
            R("wmoed").need_write(W, g, "g")
            gthrottle(g); sDW.inc(g.dma_start(
                wdn[:].rearrange("p (c h2) -> p c h2", c=DCN),
                wd_d[e, fh * 1024:(fh + 1) * 1024, :]
                .rearrange("(c p) h2 -> p c h2", p=128)), 16)
            R("wmoed").wrote(sDW)
        GE(m_wl2)
        return sDW.v

    wl_g = {0: moe_load_g(0)}
    wl_u = {0: moe_load_u(0)}
    wl_dn = {0: moe_load_dn(0)}
    for u in range(len(units)):
        e, fh = units[u]
        p = u % 2
        if u + 1 < len(units):
            wl_g[u + 1] = moe_load_g(u + 1)
        mwl, mwlu, mwl2 = wl_g[u], wl_u[u], wl_dn[u]
        for th in range(2):
            for t2 in range(2):
                def m_dwb(t, th=th, e=e, t2=t2):
                    W.wge(t, "t", sV, sV.v)
                    R("pbM").need_write(W, t, "t")
                    t.matmul(pbM[:, 0:512], csel8[:, e * 128:(e + 1) * 128],
                             dwT[0:8, th * TH + t2 * 512:th * TH + (t2 + 1) * 512],
                             start=True, stop=True)
                    sPE.inc(t.nop())
                    R("pbM").wrote(sPE)
                TE(m_dwb)
                db = sPE.v
                def m_dwbe(v, t2=t2, db=db):
                    W.wge(v, "v", sPE, db)
                    R("dwb").need_write(W, v, "v")
                    sV.inc(v.tensor_copy(dwb[:, t2 * 512:(t2 + 1) * 512], pbM[:, 0:512]))
                    R("pbM").read(sV)
                    R("dwb").wrote(sV)
                VE(m_dwbe)
            dwb_t = sV.v

            for fc in range(DCN):
                for t2 in range(2):
                    tcol = th * TH + t2 * 512
                    bp = (fc * 2 + t2) % 2
                    def m_gu(t, fc=fc, t2=t2, tcol=tcol, bp=bp, p=p,
                             mwl=mwl, mwlu=mwlu, dwb_t=dwb_t):
                        W.wge(t, "t", sDW, max(mwl, mwlu))
                        W.wge(t, "t", sV, dwb_t)
                        if fc == 0 and t2 == 0:
                            R("hsb").need_write(W, t, "t")
                        R("pbG%d" % bp).need_write(W, t, "t")
                        for dc in range(DCN):
                            t.matmul(pbG[bp][:, 0:512],
                                     wgat[p][:, dc * 1024 + fc * 128:dc * 1024 + (fc + 1) * 128],
                                     xffn[:, dc * TC + tcol:dc * TC + tcol + 512],
                                     start=(dc == 0), stop=(dc == DCN - 1))
                        sPE.inc(t.nop())
                        R("pbG%d" % bp).wrote(sPE)
                        R("pbU%d" % bp).need_write(W, t, "t")
                        for dc in range(DCN):
                            t.matmul(pbU[bp][:, 0:512],
                                     wup[:, dc * 1024 + fc * 128:dc * 1024 + (fc + 1) * 128],
                                     xffn[:, dc * TC + tcol:dc * TC + tcol + 512],
                                     start=(dc == 0), stop=(dc == DCN - 1))
                        sPE.inc(t.nop())
                        R("pbU%d" % bp).wrote(sPE)
                        R("wmoe%d" % p).read(sPE)
                        R("wupr").read(sPE)
                    TE(m_gu)
                    gu_t = sPE.v
                    def m_si(s, bp=bp, gu_t=gu_t):
                        W.wge(s, "s", sPE, gu_t - 8)
                        R("gs").need_write(W, s, "s")
                        sS.inc(s.activation(gs_sb[:, 0:512], pbG[bp][:, 0:512],
                                            AF.Silu))
                        R("pbG%d" % bp).read(sS)
                        R("gs").wrote(sS)
                    SE(m_si)
                    si_t = sS.v
                    def m_h(v, fc=fc, t2=t2, bp=bp, si_t=si_t, gu_t=gu_t):
                        W.wge(v, "v", sS, si_t)
                        W.wge(v, "v", sPE, gu_t)
                        R("wt1").need_write(W, v, "v")
                        v.scalar_tensor_tensor(wt1[:, 0:512], pbU[bp][:, 0:512], 1.0,
                                               gs_sb[:, 0:512], ALU.mult, ALU.mult)
                        sV.inc(v.tensor_tensor(
                            h_sb[:, fc * 1024 + t2 * 512:fc * 1024 + t2 * 512 + 512],
                            wt1[:, 0:512], dwb[:, t2 * 512:(t2 + 1) * 512], ALU.mult))
                        R("pbU%d" % bp).read(sV)
                        R("gs").read(sV)
                        R("wt1").read(sV)
                        R("wt1").wrote(sV)
                        R("dwb").read(sV)
                    VE(m_h)
            h_t = sV.v
            if th == 1 and u + 1 < len(units):
                wl_u[u + 1] = moe_load_u(u + 1)

            for hc in range(DCN):
                for t2 in range(2):
                    bp = (hc * 2 + t2) % 2
                    def m_dn(t, hc=hc, t2=t2, bp=bp, h_t=h_t, mwl2=mwl2):
                        W.wge(t, "t", sV, h_t)
                        W.wge(t, "t", sDW, mwl2)
                        R("pbDN%d" % bp).need_write(W, t, "t")
                        for fc in range(DCN):
                            t.matmul(pbDN[bp][:, 0:512],
                                     wdn[:, fc * 1024 + hc * 128:fc * 1024 + (hc + 1) * 128],
                                     h_sb[:, fc * 1024 + t2 * 512:fc * 1024 + t2 * 512 + 512],
                                     start=(fc == 0), stop=(fc == DCN - 1))
                        sPE.inc(t.nop())
                        R("pbDN%d" % bp).wrote(sPE)
                        R("wmoed").read(sPE)
                        R("hsb").read(sPE)
                    TE(m_dn)
                    dn_t = sPE.v
                    def m_acc(v, hc=hc, t2=t2, th=th, bp=bp, dn_t=dn_t):
                        W.wge(v, "v", sPE, dn_t)
                        R("xTr").need_write(W, v, "v")
                        xs = xT[:, hc * XS + 1 + th * TH + t2 * 512:
                                hc * XS + 1 + th * TH + t2 * 512 + 512]
                        sV.inc(v.scalar_tensor_tensor(xs, pbDN[bp][:, 0:512], 1.0, xs,
                                                      ALU.mult, ALU.add))
                        R("pbDN%d" % bp).read(sV)
                    VE(m_acc)
        if u + 1 < len(units):
            wl_dn[u + 1] = moe_load_dn(u + 1)

    fin_t = sV.v
    def out_dma(sy, fin_t=fin_t):
        W.wge(sy, "y", sV, fin_t)
        for dc in range(DCN):
            ythrottle(sy); sDX.inc(sy.dma_start(out_d[dc * 128:(dc + 1) * 128, :],
                                 xT[:, dc * XS + 1:(dc + 1) * XS]), 16)
        sy.wait_ge(sDX.h, sDX.v)
    SY(out_dma)

    blk_cm.__exit__(None, None, None)
    for cm in pctx[::-1]:
        cm.__exit__(None, None, None)
    for cm in cctx[::-1]:
        cm.__exit__(None, None, None)
    for cm in bcctx[::-1]:
        cm.__exit__(None, None, None)
    for cm in abctx[::-1]:
        cm.__exit__(None, None, None)
    for cm in ctx[::-1]:
        cm.__exit__(None, None, None)
    return nc


ATTW_KEYS = ("L0_attn_W_r", "L0_attn_W_w", "L0_attn_W_k", "L0_attn_W_v",
             "L0_attn_W_a", "L0_attn_W_b", "L0_attn_W_g", "L0_attn_W_o")
MIX_KEYS = ("L0_attn_x_r", "L0_attn_x_w", "L0_attn_x_k", "L0_attn_x_v",
            "L0_attn_x_a", "L0_attn_x_b", "L0_attn_x_g")


def host_prep(inputs):
    x = np.ascontiguousarray(np.asarray(inputs["L0_attn_x"])[0].T)
    attw = np.stack([np.ascontiguousarray(np.asarray(inputs[k]).T)
                     for k in ATTW_KEYS])
    mixw = np.stack([np.asarray(inputs[k]) for k in MIX_KEYS] +
                    [np.zeros(D, np.float32)], axis=1).astype(np.float32)
    vecs = np.stack([np.asarray(inputs["L0_attn_b_o"]),
                     np.asarray(inputs["L0_attn_gn_w"]),
                     np.asarray(inputs["L0_attn_gn_b"]),
                     np.zeros(D, np.float32)], axis=1).astype(np.float32)
    routw = np.ascontiguousarray(np.asarray(inputs["L0_ffn_router_w"]).T).astype(np.float32)
    wgT = np.ascontiguousarray(np.asarray(inputs["L0_ffn_w_gate"]).transpose(0, 2, 1))
    wuT = np.ascontiguousarray(np.asarray(inputs["L0_ffn_w_up"]).transpose(0, 2, 1))
    wdT = np.ascontiguousarray(np.asarray(inputs["L0_ffn_w_down"]).transpose(0, 2, 1))

    cident = np.eye(128, dtype=np.float32)
    s_ = np.triu(np.ones((128, 128), np.float32), 1)
    i_ = np.triu(np.ones((128, 128), np.float32), 0)
    cmask4 = np.concatenate([s_, s_, i_, i_], axis=1)
    blocks = np.zeros((128, 32), np.float32)
    blocksp = np.zeros((128, 32), np.float32)
    eye32 = np.zeros((128, 32), np.float32)
    for i in range(4):
        blocks[32 * i:32 * i + 32] = np.triu(np.ones((32, 32), np.float32), 1)
        blocksp[32 * i:32 * i + 32] = np.tril(np.ones((32, 32), np.float32), -1)
        eye32[32 * i:32 * i + 32] = np.eye(32, dtype=np.float32)
    cm32 = np.concatenate([blocks, blocksp], axis=1)
    cones2 = np.zeros((128, 2), np.float32)
    cones2[0:64, 0] = 1.0
    cones2[64:128, 1] = 1.0
    csel2 = np.zeros((2, 128), np.float32)
    csel2[0, 0:64] = 1.0
    csel2[1, 64:128] = 1.0
    conesr = np.ones((128, 128), np.float32)
    csel8 = np.zeros((8, 1024), np.float32)
    for e in range(8):
        csel8[e, e * 128:(e + 1) * 128] = 1.0

    shared = dict(attw=attw, mixw=mixw, vecs=vecs, routw=routw,
                  wgT=wgT, wuT=wuT, wdT=wdT, cident=cident, cmask4=cmask4,
                  cm32=cm32, cI32=eye32, cones2=cones2, csel2=csel2,
                  conesr=conesr, csel8=csel8)
    in_maps = []
    for j in range(NCORE):
        xcj = np.zeros((D, TC + 1), np.float32)
        if j > 0:
            xcj[:, 0] = x[:, j * TC - 1]
        xcj[:, 1:] = x[:, j * TC:(j + 1) * TC]
        selrep = np.zeros((128, 8), np.float32)
        if j > 0:
            selrep[:, j - 1] = 1.0
        m = dict(shared)
        m["xT"] = np.ascontiguousarray(xcj)
        m["selrep"] = selrep
        in_maps.append(m)
    return in_maps


def kernel(**inputs):
    nc = bass.Bass("TRN2", target_bir_lowering=False)
    build(nc)
    in_maps = host_prep(inputs)
    res = run_bass_kernel_spmd(nc, in_maps, core_ids=list(range(NCORE)))
    outs = [np.asarray(r["out_ft"], np.float32) for r in res.results]
    full = np.concatenate([o.T for o in outs], axis=0)
    return full.reshape(1, T, D)


# version 12 (final)
# speedup vs baseline: 1.0037x; 1.0037x over previous
import numpy as np
import concourse.bass as bass
import concourse.mybir as mybir
from concourse.bass_utils import run_bass_kernel_spmd

F32 = mybir.dt.float32
BF16 = mybir.dt.bfloat16
AF = mybir.ActivationFunctionType
ALU = mybir.AluOpType
AX = mybir.AxisListType

D = 1024
DCN = 8
H = 16
KD = 64
NCORE = 8
T = 16384
TC = T // NCORE
CCH = 128
NCHK = TC // CCH
TBS = 256
NTB = TC // TBS
CPB = TBS // CCH
TBX = TBS + 1
E = 8
FF = 2048
LN_EPS = 1e-5
GN_EPS = 0.00064


class Sem:
    def __init__(self, nc, name, auto=False):
        self._cm = nc.semaphore(name)
        self.h = self._cm.__enter__()
        self.v = 0
        self.auto = auto

    def inc(self, inst, n=1):
        if self.auto:
            return inst
        inst.then_inc(self.h, n)
        self.v += n
        return inst

    def forced_inc(self, inst, n=1):
        inst.then_inc(self.h, n)
        self.v += n
        return inst


VOPS = {"memset", "memzero", "tensor_copy", "tensor_tensor", "tensor_add",
        "tensor_sub", "tensor_mul", "tensor_max", "tensor_scalar",
        "tensor_single_scalar", "tensor_scalar_add", "tensor_scalar_sub",
        "tensor_scalar_mul", "tensor_scalar_max", "tensor_scalar_min",
        "tensor_relu", "tensor_reduce", "reduce_max", "reduce_sum",
        "scalar_tensor_tensor", "tensor_tensor_scan", "reciprocal", "select",
        "copy_predicated", "transpose", "bn_stats", "bn_aggr", "engine_nop"}
SOPS = {"activation", "sign", "square", "sqrt", "mul", "copy", "add", "memzero"}
TOPS = {"matmul", "transpose"}


class EW:
    def __init__(self, eng, en, sem, W, ops, selfwait):
        self._e = eng
        self._en = en
        self._s = sem
        self._W = W
        self._ops = ops
        self._sw = selfwait

    def __getattr__(self, name):
        f = getattr(self._e, name)
        if name not in self._ops:
            return f

        def g(*a, **k):
            if self._sw:
                key = (self._en, id(self._s))
                if self._s.v > self._W.seen.get(key, -1):
                    self._e.wait_ge(self._s.h, self._s.v)
                    self._W.seen[key] = self._s.v
            inst = f(*a, **k)
            inst.then_inc(self._s.h, 1)
            self._s.v += 1
            return inst
        return g


class Waits:
    def __init__(self):
        self.seen = {}

    def wge(self, eng, ename, sem, val=None):
        val = sem.v if val is None else val
        if val <= 0:
            return
        key = (ename, id(sem))
        if self.seen.get(key, -1) >= val:
            return
        eng.wait_ge(sem.h, val)
        self.seen[key] = val


class Res:
    def __init__(self):
        self.w = None
        self.r = None

    @staticmethod
    def _tgt(rec):
        sem, val = rec
        if getattr(sem, "dma", False):
            return sem, sem.v
        return sem, val

    def need_write(self, W, eng, en):
        if self.r is not None:
            W.wge(eng, en, *self._tgt(self.r))
        if self.w is not None:
            W.wge(eng, en, *self._tgt(self.w))

    def need_read(self, W, eng, en):
        if self.w is not None:
            W.wge(eng, en, *self._tgt(self.w))

    def need_write_after_readers(self, W, eng, en):
        if self.r is not None:
            W.wge(eng, en, *self._tgt(self.r))

    def wrote(self, sem):
        self.w = (sem, sem.v)

    def read(self, sem):
        self.r = (sem, sem.v)


def build(nc, taps=(), level=3, prjsel=None, ntb=None, hs=9):
    taps = set(taps)
    xT_d = nc.dram_tensor("xT", [D, TC + 1], F32, kind="ExternalInput").ap()
    attw_d = nc.dram_tensor("attw", [8, D, D], F32, kind="ExternalInput").ap()
    mixw_d = nc.dram_tensor("mixw", [D, 8], F32, kind="ExternalInput").ap()
    vecs_d = nc.dram_tensor("vecs", [D, 4], F32, kind="ExternalInput").ap()
    routw_d = nc.dram_tensor("routw", [D, 8], F32, kind="ExternalInput").ap()
    wg_d = nc.dram_tensor("wgT", [E, D, FF], F32, kind="ExternalInput").ap()
    wu_d = nc.dram_tensor("wuT", [E, D, FF], F32, kind="ExternalInput").ap()
    wd_d = nc.dram_tensor("wdT", [E, FF, D], F32, kind="ExternalInput").ap()
    cident_d = nc.dram_tensor("cident", [128, 128], F32, kind="ExternalInput").ap()
    cmask4_d = nc.dram_tensor("cmask4", [128, 512], F32, kind="ExternalInput").ap()
    cm32_d = nc.dram_tensor("cm32", [128, 64], F32, kind="ExternalInput").ap()
    cI32_d = nc.dram_tensor("cI32", [128, 32], F32, kind="ExternalInput").ap()
    cones2_d = nc.dram_tensor("cones2", [128, 2], F32, kind="ExternalInput").ap()
    csel2_d = nc.dram_tensor("csel2", [2, 128], F32, kind="ExternalInput").ap()
    cones_d = nc.dram_tensor("conesr", [128, 128], F32, kind="ExternalInput").ap()
    csel8_d = nc.dram_tensor("csel8", [8, 1024], F32, kind="ExternalInput").ap()
    selrep_d = nc.dram_tensor("selrep", [128, 8], F32, kind="ExternalInput").ap()
    out_d = nc.dram_tensor("out_ft", [D, TC], F32, kind="ExternalOutput").ap()

    psi_d = nc.dram_tensor("psi_s", [NCHK * 128, 1024], BF16).ap()
    olo_d = nc.dram_tensor("olo_s", [NCHK * 128, 1024], BF16).ap()
    g_d = nc.dram_tensor("g_s", [D, TC], BF16).ap()
    xffn_d = nc.dram_tensor("xffn_s", [D, TC], BF16).ap()
    dwT_d = nc.dram_tensor("dwT_s", [8, 2048], BF16).ap()
    cc_in = nc.dram_tensor("cc_in", [2048, 64], F32)
    cc_out = nc.dram_tensor("cc_out", [8 * 2048, 64], F32)

    tap_d = {}
    if "proj" in taps:
        for n in ("tp_r", "tp_w", "tp_k", "tp_a", "tp_b"):
            tap_d[n] = nc.dram_tensor(n, [D, TBS], F32, kind="ExternalOutput").ap()
        tap_d["tp_v"] = nc.dram_tensor("tp_v", [CPB * 128, D], F32, kind="ExternalOutput").ap()
    if "rec" in taps:
        tap_d["tp_o"] = nc.dram_tensor("tp_o", [NCHK * 128, 1024], F32, kind="ExternalOutput").ap()
        tap_d["tp_z"] = nc.dram_tensor("tp_z", [128, 512], F32, kind="ExternalOutput").ap()
    if "attn" in taps:
        tap_d["tp_sin"] = nc.dram_tensor("tp_sin", [128, 512], F32, kind="ExternalOutput").ap()
        tap_d["tp_x1"] = nc.dram_tensor("tp_x1", [D, TC], F32, kind="ExternalOutput").ap()
    if "xffn" in taps:
        tap_d["tp_xffn"] = nc.dram_tensor("tp_xffn", [D, TC], F32, kind="ExternalOutput").ap()
    if "dw" in taps:
        tap_d["tp_dw"] = nc.dram_tensor("tp_dw", [16 * 128, 8], F32, kind="ExternalOutput").ap()

    W = Waits()
    sPE = Sem(nc, "sPE", auto=True)
    sV = Sem(nc, "sV", auto=True)
    sS = Sem(nc, "sS", auto=True)
    sG = Sem(nc, "sG")
    sDW = Sem(nc, "sDW")
    sDW.dma = True
    sDX = Sem(nc, "sDX")
    sDX.dma = True
    sCC = Sem(nc, "sCC")

    rs = {}
    def R(name):
        if name not in rs:
            rs[name] = Res()
        return rs[name]

    def gthrottle(g):
        if sDW.v - getattr(sDW, "lastw", 0) >= 16:
            tgt = sDW.v
            g.wait_ge(sDW.h, tgt)
            sDW.lastw = tgt
            W.seen[("g", id(sDW))] = tgt

    def ythrottle(sy):
        if sDX.v - getattr(sDX, "lastw", 0) >= 16:
            tgt = sDX.v
            sy.wait_ge(sDX.h, tgt)
            sDX.lastw = tgt
            W.seen[("y", id(sDX))] = tgt

    blk_cm = nc.Block()
    block = blk_cm.__enter__()
    def TE(f):
        block.tensor(lambda t: f(EW(t, "t", sPE, W, TOPS, False)))
    def VE(f):
        block.vector(lambda v: f(EW(v, "v", sV, W, VOPS, True)))
    def SE(f):
        block.scalar(lambda s2: f(EW(s2, "s", sS, W, SOPS, True)))
    GE, SY = block.gpsimd, block.sync

    def barrier():
        def bt(t):
            W.wge(t, "t", sV, sV.v)
            W.wge(t, "t", sS, sS.v)
            W.wge(t, "t", sG, sG.v)
            W.wge(t, "t", sDX, sDX.v)
            W.wge(t, "t", sDW, sDW.v)
            sPE.forced_inc(t.nop())
        TE(bt)
        pe_t = sPE.v
        def bo(e, en):
            W.wge(e, en, sPE, pe_t)
        VE(lambda v: (bo(v, "v"), sV.inc(v.engine_nop())))
        SE(lambda s: (bo(s, "s"), sS.forced_inc(s.nop())))
        GE(lambda g: (bo(g, "g"), sG.inc(g.nop())))
        SY(lambda y: bo(y, "y"))

    ctx = []
    abctx = []
    actx = []
    def mksb(lst):
        def sbf(name, shape, dt):
            cm = nc.sbuf_tensor(name, shape, dt)
            t = cm.__enter__()
            lst.append(cm)
            return t
        return sbf
    sb = mksb(ctx)
    absb = mksb(abctx)
    asb = mksb(actx)

    xT = sb("xT_sb", [128, DCN * (TC + 1)], F32)
    XS = TC + 1
    cid_b = sb("cid_b", [128, 128], BF16)
    cid_f = sb("cid_f", [128, 128], F32)
    cmask4 = sb("cmask4_s", [128, 512], BF16)
    cm32 = sb("cm32_s", [128, 64], BF16)
    cI32 = sb("cI32_s", [128, 32], BF16)
    cones2 = sb("cones2_s", [128, 2], BF16)
    csel2 = sb("csel2_s", [2, 128], BF16)
    conesr = sb("conesr_s", [128, 128], BF16)
    csel8 = sb("csel8_s", [8, 1024], BF16)
    mixw = sb("mixw_s", [128, DCN * 8], F32)
    omix = sb("omix_s", [128, DCN * 8], F32)
    vecs_sb = sb("vecs_s", [128, DCN * 4], F32)
    routw_sb = sb("routw_s", [128, DCN * 8], BF16)
    selrep = sb("selrep_s", [128, 8], F32)
    wt1 = sb("wt1", [128, 512], F32)
    wt2 = sb("wt2", [128, 512], F32)
    kb = sb("kb", [128, 4], F32)

    bcctx = []

    z_sb = [absb("zP%d" % p, [128, DCN * 64], F32) for p in range(2)]
    QT_sb = [absb("qtP%d" % p, [128, DCN * 64], F32) for p in range(2)]
    SinT = absb("sinT", [128, DCN * 64], F32)
    sinb = absb("sinb", [128, DCN * 64], BF16)
    psi_sb = absb("psi_sb", [128, DCN * 128], BF16)
    olo_sb = absb("olo_sb", [128, H * 64], BF16)
    stat_sb = absb("stat_sb", [1, 2 * TBX], F32)
    statb = absb("statb", [1, 2 * TBX], BF16)

    def init_loads(g):
        gthrottle(g); sDW.inc(g.dma_start(xT[:].rearrange("p (c t) -> p c t", c=DCN), xT_d.rearrange("(c p) t -> p c t", p=128)), 16)
        for dst, src in ((cid_b, cident_d), (cmask4, cmask4_d), (cm32, cm32_d),
                         (cI32, cI32_d), (cones2, cones2_d), (csel2, csel2_d),
                         (conesr, cones_d), (csel8, csel8_d),
                         (routw_sb[:].rearrange("p (c e) -> p c e", c=DCN), routw_d.rearrange("(c p) e -> p c e", p=128))):
            gthrottle(g); sDW.inc(g.dma_start(dst[:], src), 16)
        for dst, src in ((cid_f, cident_d),
                         (mixw[:].rearrange("p (c m) -> p c m", c=DCN), mixw_d.rearrange("(c p) m -> p c m", p=128)),
                         (vecs_sb[:].rearrange("p (c m) -> p c m", c=DCN), vecs_d.rearrange("(c p) m -> p c m", p=128)),
                         (selrep, selrep_d)):
            gthrottle(g); sDW.inc(g.dma_start(dst[:], src), 16)
    GE(init_loads)
    ld0w, ld0x = sDW.v, sDX.v

    def init_v(v):
        W.wge(v, "v", sDW, ld0w)
        W.wge(v, "v", sDX, ld0x)
        v.tensor_scalar(omix[:], mixw[:], -1.0, 1.0, ALU.mult, ALU.add)
        v.memset(kb[:, 0:1], LN_EPS)
        v.memset(kb[:, 1:2], 1.0)
        v.memset(kb[:, 2:3], GN_EPS)
        v.memset(kb[:, 3:4], 0.0)
        for p in range(2):
            v.memset(z_sb[p][:], 0.0)
            v.memset(QT_sb[p][:], 0.0)
        v.memset(SinT[:], 0.0)
        for dc in range(DCN):
            for oh in (0, 64):
                v.tensor_copy(QT_sb[0][oh:oh + 64, dc * 64:(dc + 1) * 64],
                              cid_f[oh:oh + 64, oh:oh + 64])
        sV.inc(v.engine_nop())
    VE(init_v)

    ln_sb = asb("ln_sb", [128, DCN * TBX], BF16)
    mixed = asb("mixed", [128, DCN * TBS], BF16)
    msc = asb("msc", [128, DCN * TBS], BF16)
    watt = [asb("watt%d" % p, [128, DCN * 1024], BF16) for p in range(2)]
    r_sb = asb("r_sb", [128, DCN * TBS], BF16)
    k_sb = asb("k_sb", [128, DCN * TBS], BF16)
    a_sb = asb("a_sb", [128, DCN * TBS], BF16)
    b_sb = asb("b_sb", [128, DCN * TBS], BF16)
    w_sb = asb("w_sb", [128, DCN * TBS], F32)
    v_sb = asb("v_sb", [128, CPB * 1024], BF16)
    gtmp2 = asb("gtmp2", [128, TBS], BF16)
    rnf = asb("rnf", [2, TBS], F32)
    rnb = asb("rnb", [2, TBS], BF16)
    cw_sb = asb("cw_sb", [128, DCN * 129], F32)
    lam_sb = asb("lam_sb", [128, DCN * 129], F32)
    lami_sb = asb("lami_sb", [128, DCN * 128], F32)
    sml = asb("sml", [128, DCN * 4], F32)
    ones128 = asb("ones128", [128, 128], F32)
    ah_sb = asb("ah_sb", [128, DCN * 128], BF16)
    bt_sb = asb("bt_sb", [128, DCN * 128], BF16)
    kt_sb = asb("kt_sb", [128, DCN * 128], BF16)
    rh_sb = asb("rh_sb", [128, DCN * 128], BF16)
    Bh_sb = asb("Bh_sb", [128, DCN * 128], BF16)
    Kh_sb = asb("Kh_sb", [128, DCN * 128], BF16)
    Btok = asb("Btok", [128, DCN * 128], BF16)
    Ktok = asb("Ktok", [128, DCN * 128], BF16)
    prod = [asb("prodP%d" % p, [128, 512], BF16) for p in range(2)]
    qpow = [asb("qpowP%d" % p, [128, 64], BF16) for p in range(2)]
    accp = [asb("accpP%d" % p, [128, 32], BF16) for p in range(2)]
    y_sb = [asb("ysbP%d" % p, [128, 128], BF16) for p in range(2)]
    X_sb = [asb("XsbP%d" % p, [128, 128], BF16) for p in range(2)]
    phiT = asb("phiT", [128, DCN * 128], BF16)
    zs_sb = asb("zs_sb", [128, DCN * 64], BF16)
    qts_sb = asb("qts_sb", [128, DCN * 64], BF16)
    pr_sb = [asb("prP%d" % p, [128, 64], BF16) for p in range(2)]
    xb_of = lambda dc: watt[0][:, dc * TBX:(dc + 1) * TBX]
    sq_of = lambda dc: ln_sb[:, dc * TBX:(dc + 1) * TBX]
    qfin = cw_sb

    pctx = []
    def mkps(name, shape, dt=F32):
        cm = nc.psum_tensor(name, shape, dt)
        t = cm.__enter__()
        pctx.append(cm)
        return t
    pbP = [mkps("pbP0", [128, 512]), mkps("pbP1", [128, 512])]
    pbC = mkps("pbC", [128, 512])
    pbD = mkps("pbD", [128, 512])
    pbY = mkps("pbY", [128, 512])
    pbX = mkps("pbX", [128, 512])
    pbM = mkps("pbM", [128, 512])
    pbT = mkps("pbT", [128, 1024], BF16)

    def vones(v):
        sV.inc(v.memset(ones128[:], 1.0))
    VE(vones)

    PRJ = ["r", "w", "k", "v", "a", "b", "g"] if prjsel is None else list(prjsel)
    prj_tiles = {"r": r_sb, "k": k_sb, "a": a_sb, "b": b_sb}

    if level <= -1:
        finm = sV.v
        def out_dmam(sy, finm=finm):
            W.wge(sy, "y", sV, finm)
            for dc in range(DCN):
                ythrottle(sy); sDX.inc(sy.dma_start(out_d[dc * 128:(dc + 1) * 128, :],
                                     xT[:, dc * XS + 1:(dc + 1) * XS]), 16)
            sy.wait_ge(sDX.h, sDX.v)
        SY(out_dmam)
        blk_cm.__exit__(None, None, None)
        for cm in pctx[::-1]:
            cm.__exit__(None, None, None)
        for cm in actx[::-1]:
            cm.__exit__(None, None, None)
        for cm in abctx[::-1]:
            cm.__exit__(None, None, None)
        for cm in ctx[::-1]:
            cm.__exit__(None, None, None)
        return nc

    for it in range(NTB if ntb is None else ntb):
        t0 = it * TBS

        def a_cast(v, t0=t0):
            W.wge(v, "v", sS, sS.v)
            R("watt0").need_write(W, v, "v")
            R("lnt").need_write(W, v, "v")
            for dc in range(DCN):
                v.tensor_copy(xb_of(dc), xT[:, dc * XS + t0: dc * XS + t0 + TBX])
            for dc in range(DCN):
                v.tensor_tensor(sq_of(dc), xb_of(dc), xb_of(dc), ALU.mult)
            sV.inc(v.engine_nop())
            R("watt0").wrote(sV)
            R("lnt").wrote(sV)
        VE(a_cast)
        xb_t = sV.v

        def a_stats(t, xb_t=xb_t):
            W.wge(t, "t", sV, xb_t)
            R("pbC").need_write(W, t, "t")
            R("pbP0").need_write(W, t, "t")
            for dc in range(DCN):
                t.matmul(pbC[0:1, 0:TBX], conesr[:, 0:1], xb_of(dc),
                         start=(dc == 0), stop=(dc == DCN - 1))
            for dc in range(DCN):
                t.matmul(pbP[0][0:1, 0:TBX], conesr[:, 0:1], sq_of(dc),
                         start=(dc == 0), stop=(dc == DCN - 1))
            sPE.inc(t.nop())
            R("pbC").wrote(sPE)
            R("pbP0").wrote(sPE)
            R("watt0").read(sPE)
            R("lnt").read(sPE)
        TE(a_stats)
        st_t = sPE.v

        def a_mv(v, st_t=st_t):
            W.wge(v, "v", sPE, st_t)
            v.tensor_scalar(stat_sb[0:1, 0:TBX], pbC[0:1, 0:TBX], 1.0 / D, None, ALU.mult)
            v.tensor_scalar(stat_sb[0:1, TBX:2 * TBX], pbP[0][0:1, 0:TBX],
                            1.0 / D, None, ALU.mult)
            R("pbC").read(sV)
            R("pbP0").read(sV)
            v.tensor_tensor(wt1[0:1, 0:TBX], stat_sb[0:1, 0:TBX],
                            stat_sb[0:1, 0:TBX], ALU.mult)
            sV.inc(v.tensor_tensor(stat_sb[0:1, TBX:2 * TBX],
                                   stat_sb[0:1, TBX:2 * TBX], wt1[0:1, 0:TBX],
                                   ALU.subtract))
        VE(a_mv)
        mv_t = sV.v

        def a_rstd(s, mv_t=mv_t):
            W.wge(s, "s", sV, mv_t)
            s.activation(stat_sb[0:1, TBX:2 * TBX], stat_sb[0:1, TBX:2 * TBX],
                         AF.Ln, bias=kb[0:1, 0:1])
            sS.inc(s.activation(stat_sb[0:1, TBX:2 * TBX],
                                stat_sb[0:1, TBX:2 * TBX], AF.Exp, scale=-0.5))
        SE(a_rstd)
        rs_t = sS.v

        def a_stc(v, rs_t=rs_t):
            W.wge(v, "v", sS, rs_t)
            sV.inc(v.tensor_copy(statb[0:1, 0:2 * TBX], stat_sb[0:1, 0:2 * TBX]))
        VE(a_stc)
        stc_t = sV.v

        def a_bc(t, stc_t=stc_t):
            W.wge(t, "t", sV, stc_t)
            R("pbC").need_write(W, t, "t")
            R("pbP0").need_write(W, t, "t")
            t.matmul(pbC[:, 0:TBX], conesr[0:1, :], statb[0:1, 0:TBX], start=True, stop=True)
            sPE.inc(t.matmul(pbP[0][:, 0:TBX], conesr[0:1, :], statb[0:1, TBX:2 * TBX],
                             start=True, stop=True))
            R("pbC").wrote(sPE)
            R("pbP0").wrote(sPE)
        TE(a_bc)
        bc_t = sPE.v

        def a_ln(v, t0=t0, bc_t=bc_t):
            W.wge(v, "v", sPE, bc_t)
            R("lnt").need_write(W, v, "v")
            for dc in range(DCN):
                v.tensor_tensor(wt1[:, 0:TBX], xT[:, dc * XS + t0: dc * XS + t0 + TBX],
                                pbC[:, 0:TBX], ALU.subtract)
                v.tensor_tensor(ln_sb[:, dc * TBX:(dc + 1) * TBX], wt1[:, 0:TBX],
                                pbP[0][:, 0:TBX], ALU.mult)
            sV.inc(v.engine_nop())
            R("pbC").read(sV)
            R("pbP0").read(sV)
            R("lnt").wrote(sV)
        VE(a_ln)
        ln_t = sV.v

        def a_wl_emit(pi):
            pw = (pi + 1) % 2
            def a_wl(g, pi=pi, pw=pw):
                R("watt%d" % pw).need_write(W, g, "g")
                gthrottle(g); sDW.inc(g.dma_start(watt[pw][:].rearrange("p (c e) -> p c e", c=DCN), attw_d[pi].rearrange("(c p) e -> p c e", p=128)), 16)
                R("watt%d" % pw).wrote(sDW)
            GE(a_wl)
            return sDW.v
        wl_ts = {0: a_wl_emit(0)}
        for pi, pn in enumerate(PRJ):
            pw = (pi + 1) % 2
            if pi + 1 < len(PRJ):
                wl_ts[pi + 1] = a_wl_emit(pi + 1)
            wl_t = wl_ts[pi]

            def a_mix(v, pi=pi, ln_t=ln_t):
                W.wge(v, "v", sV, ln_t)
                R("mixed").need_write(W, v, "v")
                R("msc").need_write(W, v, "v")
                for dc in range(DCN):
                    mu = mixw[:, dc * 8 + pi:dc * 8 + pi + 1]
                    om = omix[:, dc * 8 + pi:dc * 8 + pi + 1]
                    v.tensor_scalar(msc[:, dc * TBS:(dc + 1) * TBS],
                                    ln_sb[:, dc * TBX + 1:dc * TBX + TBX], om, None, ALU.mult)
                    v.scalar_tensor_tensor(mixed[:, dc * TBS:(dc + 1) * TBS],
                                           ln_sb[:, dc * TBX:dc * TBX + TBS], mu,
                                           msc[:, dc * TBS:(dc + 1) * TBS],
                                           ALU.mult, ALU.add)
                sV.inc(v.engine_nop())
                R("mixed").wrote(sV)
                R("msc").wrote(sV)
            VE(a_mix)
            mx_t = sV.v

            if pn == "v":
                for tt in range(CPB):
                    for eh in range(2):
                        bi = (tt * 2 + eh) % 2
                        def a_vmm(t, tt=tt, eh=eh, bi=bi, pw=pw, wl_t=wl_t, mx_t=mx_t):
                            W.wge(t, "t", sDW, wl_t)
                            W.wge(t, "t", sV, mx_t)
                            R("pbP%d" % bi).need_write(W, t, "t")
                            for dc in range(DCN):
                                t.matmul(pbP[bi][:, 0:512],
                                         mixed[:, dc * TBS + tt * 128: dc * TBS + (tt + 1) * 128],
                                         watt[pw][:, dc * 1024 + eh * 512: dc * 1024 + (eh + 1) * 512],
                                         start=(dc == 0), stop=(dc == DCN - 1))
                            sPE.inc(t.nop())
                            R("pbP%d" % bi).wrote(sPE)
                            R("watt%d" % pw).read(sPE)
                            R("mixed").read(sPE)
                        TE(a_vmm)
                        def a_vev(v, tt=tt, eh=eh, bi=bi):
                            R("pbP%d" % bi).need_read(W, v, "v")
                            sV.inc(v.tensor_copy(
                                v_sb[:, tt * 1024 + eh * 512: tt * 1024 + (eh + 1) * 512],
                                pbP[bi][:, 0:512]))
                            R("pbP%d" % bi).read(sV)
                        VE(a_vev)
                continue

            for ec in range(DCN):
                bi = ec % 2
                def a_pmm(t, ec=ec, bi=bi, pw=pw, wl_t=wl_t, mx_t=mx_t):
                    W.wge(t, "t", sDW, wl_t)
                    W.wge(t, "t", sV, mx_t)
                    R("pbP%d" % bi).need_write(W, t, "t")
                    for dc in range(DCN):
                        t.matmul(pbP[bi][:, 0:TBS],
                                 watt[pw][:, dc * 1024 + ec * 128: dc * 1024 + (ec + 1) * 128],
                                 mixed[:, dc * TBS:(dc + 1) * TBS],
                                 start=(dc == 0), stop=(dc == DCN - 1))
                    sPE.inc(t.nop())
                    R("pbP%d" % bi).wrote(sPE)
                    R("watt%d" % pw).read(sPE)
                    R("mixed").read(sPE)
                TE(a_pmm)

                if pn == "r":
                    def a_ev(v, ec=ec, bi=bi):
                        R("pbP%d" % bi).need_read(W, v, "v")
                        sV.inc(v.tensor_copy(r_sb[:, ec * TBS:(ec + 1) * TBS], pbP[bi][:, 0:TBS]))
                        R("pbP%d" % bi).read(sV)
                    VE(a_ev)
                elif pn == "g":
                    def a_ev(v, ec=ec, bi=bi):
                        R("pbP%d" % bi).need_read(W, v, "v")
                        R("gtmp2").need_write(W, v, "v")
                        sV.inc(v.tensor_copy(gtmp2[:, 0:TBS], pbP[bi][:, 0:TBS]))
                        R("pbP%d" % bi).read(sV)
                        R("gtmp2").wrote(sV)
                    VE(a_ev)
                    gt = sV.v
                    def a_gd(sy, ec=ec, t0=t0, gt=gt):
                        W.wge(sy, "y", sV, gt)
                        ythrottle(sy); sDX.inc(sy.dma_start(g_d[ec * 128:(ec + 1) * 128, t0:t0 + TBS],
                                             gtmp2[:, 0:TBS]), 16)
                        R("gtmp2").read(sDX)
                    SY(a_gd)
                elif pn == "w":
                    def a_ev(s, ec=ec, bi=bi):
                        R("pbP%d" % bi).need_read(W, s, "s")
                        R("wt1").need_write(W, s, "s")
                        R("wt2").need_write(W, s, "s")
                        s.activation(wt1[:, 0:TBS], pbP[bi][:, 0:TBS], AF.Abs)
                        s.activation(wt1[:, 0:TBS], wt1[:, 0:TBS], AF.Exp, scale=-1.0)
                        sS.inc(s.activation(wt2[:, 0:TBS], wt1[:, 0:TBS], AF.Ln, bias=kb[:, 1:2]))
                        R("wt1").wrote(sS)
                        R("wt2").wrote(sS)
                    SE(a_ev)
                    wet = sS.v
                    def a_ev2(v, ec=ec, bi=bi, wet=wet):
                        W.wge(v, "v", sS, wet)
                        v.tensor_scalar_max(wt1[:, 0:TBS], pbP[bi][:, 0:TBS], 0.0)
                        sV.inc(v.scalar_tensor_tensor(w_sb[:, ec * TBS:(ec + 1) * TBS],
                                                      wt1[:, 0:TBS], -1.0, wt2[:, 0:TBS],
                                                      ALU.mult, ALU.subtract))
                        R("pbP%d" % bi).read(sV)
                        R("wt1").read(sV)
                        R("wt2").read(sV)
                    VE(a_ev2)
                elif pn == "k":
                    def a_ev(s, ec=ec, bi=bi):
                        R("pbP%d" % bi).need_read(W, s, "s")
                        R("wt1").need_write(W, s, "s")
                        sS.inc(s.activation(wt1[:, 0:TBS], pbP[bi][:, 0:TBS], AF.Exp, scale=2.0))
                        R("wt1").wrote(sS)
                        R("pbP%d" % bi).read(sS)
                    SE(a_ev)
                    ket = sS.v
                    def a_ev2(v, ec=ec, ket=ket):
                        W.wge(v, "v", sS, ket)
                        R("wt2").need_write(W, v, "v")
                        v.tensor_scalar_add(wt2[:, 0:TBS], wt1[:, 0:TBS], 1.0)
                        v.reciprocal(wt2[:, 0:TBS], wt2[:, 0:TBS])
                        sV.inc(v.tensor_scalar(k_sb[:, ec * TBS:(ec + 1) * TBS],
                                               wt2[:, 0:TBS], -2.0, 1.0, ALU.mult, ALU.add))
                        R("wt1").read(sV)
                        R("wt2").read(sV)
                    VE(a_ev2)
                else:
                    dst = prj_tiles[pn]
                    def a_ev(v, ec=ec, bi=bi, dst=dst):
                        R("pbP%d" % bi).need_read(W, v, "v")
                        R("msq").need_write(W, v, "v")
                        v.tensor_copy(dst[:, ec * TBS:(ec + 1) * TBS], pbP[bi][:, 0:TBS])
                        sV.inc(v.tensor_tensor(msc[:, 0:TBS],
                                               dst[:, ec * TBS:(ec + 1) * TBS],
                                               dst[:, ec * TBS:(ec + 1) * TBS], ALU.mult))
                        R("pbP%d" % bi).read(sV)
                        R("msq").wrote(sV)
                    VE(a_ev)
                    sqt = sV.v
                    def a_nmm(t, sqt=sqt):
                        W.wge(t, "t", sV, sqt)
                        R("pbC").need_write(W, t, "t")
                        sPE.inc(t.matmul(pbC[0:2, 0:TBS], cones2[:], msc[:, 0:TBS],
                                         start=True, stop=True))
                        R("pbC").wrote(sPE)
                        R("msq").read(sPE)
                    TE(a_nmm)
                    nmt = sPE.v
                    def a_rn1(s, nmt=nmt):
                        W.wge(s, "s", sPE, nmt)
                        R("rnf").need_write(W, s, "s")
                        s.activation(rnf[0:2, 0:TBS], pbC[0:2, 0:TBS], AF.Ln)
                        sS.inc(s.activation(rnf[0:2, 0:TBS], rnf[0:2, 0:TBS],
                                            AF.Exp, scale=0.5))
                        R("pbC").read(sS)
                        R("rnf").wrote(sS)
                    SE(a_rn1)
                    rn1 = sS.v
                    def a_rn2(v, rn1=rn1):
                        W.wge(v, "v", sS, rn1)
                        v.tensor_scalar_max(rnf[0:2, 0:TBS], rnf[0:2, 0:TBS], 1e-6)
                        v.reciprocal(rnf[0:2, 0:TBS], rnf[0:2, 0:TBS])
                        sV.inc(v.tensor_copy(rnb[0:2, 0:TBS], rnf[0:2, 0:TBS]))
                        R("rnf").read(sV)
                    VE(a_rn2)
                    rn2 = sV.v
                    def a_rbc(t, rn2=rn2):
                        W.wge(t, "t", sV, rn2)
                        R("pbC").need_write(W, t, "t")
                        sPE.inc(t.matmul(pbC[:, 0:TBS], csel2[:], rnb[0:2, 0:TBS],
                                         start=True, stop=True))
                        R("pbC").wrote(sPE)
                    TE(a_rbc)
                    rbt = sPE.v
                    def a_rap(v, ec=ec, dst=dst, rbt=rbt):
                        W.wge(v, "v", sPE, rbt)
                        R("wt1").need_write(W, v, "v")
                        v.tensor_tensor(wt1[:, 0:TBS], dst[:, ec * TBS:(ec + 1) * TBS],
                                        pbC[:, 0:TBS], ALU.mult)
                        sV.inc(v.tensor_copy(dst[:, ec * TBS:(ec + 1) * TBS], wt1[:, 0:TBS]))
                        R("pbC").read(sV)
                        R("wt1").read(sV)
                    VE(a_rap)

        if "proj" in taps and it == 0:
            for name, tile in (("tp_r", r_sb), ("tp_k", k_sb), ("tp_a", a_sb),
                               ("tp_b", b_sb), ("tp_w", w_sb)):
                for dc in range(DCN):
                    def tpc(v, tile=tile, dc=dc):
                        W.wge(v, "v", sV, sV.v)
                        R("wt1").need_write(W, v, "v")
                        sV.inc(v.tensor_copy(wt1[:, 0:TBS], tile[:, dc * TBS:(dc + 1) * TBS]))
                        R("wt1").wrote(sV)
                    VE(tpc)
                    tt_ = sV.v
                    def tpd(sy, name=name, dc=dc, tt_=tt_):
                        W.wge(sy, "y", sV, tt_)
                        ythrottle(sy); sDX.inc(sy.dma_start(tap_d[name][dc * 128:(dc + 1) * 128, :],
                                             wt1[:, 0:TBS]), 16)
                        R("wt1").read(sDX)
                    SY(tpd)
            for tt in range(CPB):
                for eh in range(2):
                    def tpv(v, tt=tt, eh=eh):
                        W.wge(v, "v", sV, sV.v)
                        W.wge(v, "v", sDX, sDX.v)
                        R("wt1").need_write(W, v, "v")
                        sV.inc(v.tensor_copy(wt1[:, 0:512],
                                             v_sb[:, tt * 1024 + eh * 512:tt * 1024 + (eh + 1) * 512]))
                        R("wt1").wrote(sV)
                    VE(tpv)
                    tt_ = sV.v
                    def tpvd(sy, tt=tt, eh=eh, tt_=tt_):
                        W.wge(sy, "y", sV, tt_)
                        ythrottle(sy); sDX.inc(sy.dma_start(
                            tap_d["tp_v"][tt * 128:(tt + 1) * 128, eh * 512:(eh + 1) * 512],
                            wt1[:, 0:512]), 16)
                        R("wt1").read(sDX)
                    SY(tpvd)

        for cc in range(CPB):
            if level < 1:
                barrier()
                continue
            c = it * CPB + cc
            par = c % 2
            ct0 = cc * 128

            def c_ew_v(v, ct0=ct0):
                W.wge(v, "v", sV, sV.v)
                W.wge(v, "v", sS, sS.v)
                R("cw").need_write(W, v, "v")
                for dc in range(DCN):
                    v.memset(cw_sb[:, dc * 129:dc * 129 + 1], 0.0)
                    v.tensor_tensor_scan(cw_sb[:, dc * 129 + 1:(dc + 1) * 129],
                                         ones128[:, 0:128],
                                         w_sb[:, dc * TBS + ct0:dc * TBS + ct0 + 128],
                                         0.0, ALU.mult, ALU.add)
                    v.tensor_scalar(sml[:, dc * 4:dc * 4 + 1],
                                    cw_sb[:, (dc + 1) * 129 - 1:(dc + 1) * 129],
                                    -0.5, None, ALU.mult)
                sV.inc(v.engine_nop())
                R("cw").wrote(sV)
            VE(c_ew_v)
            cwt = sV.v

            def c_ew_s(s, cwt=cwt):
                W.wge(s, "s", sV, cwt)
                R("lam").need_write(W, s, "s")
                for dc in range(DCN):
                    s.activation(lam_sb[:, dc * 129:(dc + 1) * 129],
                                 cw_sb[:, dc * 129:(dc + 1) * 129],
                                 AF.Exp, bias=sml[:, dc * 4:dc * 4 + 1])
                    s.activation(sml[:, dc * 4 + 1:dc * 4 + 2],
                                 sml[:, dc * 4:dc * 4 + 1], AF.Exp, scale=-1.0)
                    s.activation(sml[:, dc * 4 + 2:dc * 4 + 3],
                                 cw_sb[:, (dc + 1) * 129 - 1:(dc + 1) * 129], AF.Exp)
                sS.inc(s.nop())
                R("lam").wrote(sS)
            SE(c_ew_s)
            lam_t = sS.v

            def c_ops(v, ct0=ct0, lam_t=lam_t):
                W.wge(v, "v", sS, lam_t)
                R("ops").need_write(W, v, "v")
                for dc in range(DCN):
                    sl_l = lam_sb[:, dc * 129 + 1:(dc + 1) * 129]
                    sl_p = lam_sb[:, dc * 129:(dc + 1) * 129 - 1]
                    v.reciprocal(lami_sb[:, dc * 128:(dc + 1) * 128], sl_l)
                    v.tensor_tensor(ah_sb[:, dc * 128:(dc + 1) * 128],
                                    a_sb[:, dc * TBS + ct0:dc * TBS + ct0 + 128], sl_p, ALU.mult)
                    v.tensor_tensor(bt_sb[:, dc * 128:(dc + 1) * 128],
                                    b_sb[:, dc * TBS + ct0:dc * TBS + ct0 + 128],
                                    lami_sb[:, dc * 128:(dc + 1) * 128], ALU.mult)
                    v.tensor_tensor(kt_sb[:, dc * 128:(dc + 1) * 128],
                                    k_sb[:, dc * TBS + ct0:dc * TBS + ct0 + 128],
                                    lami_sb[:, dc * 128:(dc + 1) * 128], ALU.mult)
                    v.tensor_tensor(rh_sb[:, dc * 128:(dc + 1) * 128],
                                    r_sb[:, dc * TBS + ct0:dc * TBS + ct0 + 128], sl_l, ALU.mult)
                    lc = lam_sb[:, (dc + 1) * 129 - 1:(dc + 1) * 129]
                    v.tensor_scalar(Bh_sb[:, dc * 128:(dc + 1) * 128],
                                    bt_sb[:, dc * 128:(dc + 1) * 128], lc, None, ALU.mult)
                    v.tensor_scalar(Kh_sb[:, dc * 128:(dc + 1) * 128],
                                    kt_sb[:, dc * 128:(dc + 1) * 128], lc, None, ALU.mult)
                sV.inc(v.engine_nop())
                R("ops").wrote(sV)
            VE(c_ops)
            ops_t = sV.v

            for dc in range(DCN):
                for src, dsttile in ((Bh_sb, Btok), (Kh_sb, Ktok), (ah_sb, None)):
                    def c_tp(t, dc=dc, src=src, ops_t=ops_t):
                        W.wge(t, "t", sV, ops_t)
                        R("pbT").need_write(W, t, "t")
                        t.transpose(pbT[:, 0:128], src[:, dc * 128:(dc + 1) * 128], cid_b[:])
                        sPE.inc(t.nop())
                        R("pbT").wrote(sPE)
                    TE(c_tp)
                    tpt = sPE.v
                    def c_tpe(v, dc=dc, dsttile=dsttile, tpt=tpt):
                        W.wge(v, "v", sPE, tpt)
                        if dsttile is None:
                            R("watt0").need_write_after_readers(W, v, "v")
                            sV.inc(v.tensor_copy(watt[0][:, dc * 128:(dc + 1) * 128], pbT[:, 0:128]))
                            R("watt0").wrote(sV)
                        else:
                            sV.inc(v.tensor_copy(dsttile[:, dc * 128:(dc + 1) * 128], pbT[:, 0:128]))
                        R("pbT").read(sV)
                    VE(c_tpe)
            tok_t = sV.v

            for hh in range(H):
                if hs < 1:
                    break
                dc = hh // 2
                oh = 64 * (hh % 2)
                hp = hh % 2
                vtk = v_sb[:, cc * 1024 + hh * 64: cc * 1024 + (hh + 1) * 64]

                def h_prod(t, dc=dc, oh=oh, hp=hp, tok_t=tok_t):
                    W.wge(t, "t", sV, tok_t)
                    R("pbP%d" % hp).need_write(W, t, "t")
                    R("pbD").need_write(W, t, "t")
                    bankp = pbP[hp]
                    bt_ = bt_sb[oh:oh + 64, dc * 128:(dc + 1) * 128]
                    kt_ = kt_sb[oh:oh + 64, dc * 128:(dc + 1) * 128]
                    ah_ = ah_sb[oh:oh + 64, dc * 128:(dc + 1) * 128]
                    rh_ = rh_sb[oh:oh + 64, dc * 128:(dc + 1) * 128]
                    t.matmul(bankp[:, 0:128], bt_, ah_, start=True, stop=False)
                    t.matmul(bankp[:, 256:384], bt_, rh_, start=False, stop=False)
                    t.matmul(bankp[:, 128:256], kt_, ah_, start=False, stop=False)
                    t.matmul(bankp[:, 384:512], kt_, rh_, start=False, stop=True)
                    for i in range(4):
                        t.matmul(pbD[32 * i:32 * i + 32, 0:32],
                                 bt_sb[oh:oh + 64, dc * 128 + 32 * i:dc * 128 + 32 * i + 32],
                                 ah_sb[oh:oh + 64, dc * 128 + 32 * i:dc * 128 + 32 * i + 32],
                                 start=True, stop=False, tile_position=(oh, 32 * i), skip_group_check=True)
                    for i in range(4):
                        t.matmul(pbD[32 * i:32 * i + 32, 32:64],
                                 ah_sb[oh:oh + 64, dc * 128 + 32 * i:dc * 128 + 32 * i + 32],
                                 bt_sb[oh:oh + 64, dc * 128 + 32 * i:dc * 128 + 32 * i + 32],
                                 start=False, stop=True, tile_position=(oh, 32 * i), skip_group_check=True)
                    sPE.inc(t.nop())
                    R("pbP%d" % hp).wrote(sPE)
                    R("pbD").wrote(sPE)
                TE(h_prod)
                prod_t = sPE.v

                def h_mask(v, hp=hp, prod_t=prod_t):
                    W.wge(v, "v", sPE, prod_t)
                    R("prod%d" % hp).need_write(W, v, "v")
                    R("qpow%d" % hp).need_write(W, v, "v")
                    R("accp%d" % hp).need_write(W, v, "v")
                    v.scalar_tensor_tensor(prod[hp][:, 0:512], pbP[hp][:, 0:512], 1.0,
                                           cmask4[:], ALU.mult, ALU.mult)
                    R("pbP%d" % hp).read(sV)
                    v.tensor_tensor(qpow[hp][:, 0:64], pbD[:, 0:64], cm32[:], ALU.mult)
                    sV.inc(v.tensor_tensor(accp[hp][:, 0:32], qpow[hp][:, 32:64],
                                           cI32[:], ALU.add))
                    R("pbD").read(sV)
                    R("prod%d" % hp).wrote(sV)
                    R("qpow%d" % hp).wrote(sV)
                    R("accp%d" % hp).wrote(sV)
                VE(h_mask)
                if hs < 2:
                    continue

                for lvl in range(4):
                    def h_nm(t, hp=hp):
                        W.wge(t, "t", sV, sV.v)
                        R("pbD").need_write(W, t, "t")
                        for i in range(4):
                            t.matmul(pbD[32 * i:32 * i + 32, 64:96],
                                     qpow[hp][32 * i:32 * i + 32, 32:64],
                                     qpow[hp][32 * i:32 * i + 32, 0:32],
                                     start=True, stop=False, tile_position=(32 * i, 32 * i), skip_group_check=True)
                        for i in range(4):
                            t.matmul(pbD[32 * i:32 * i + 32, 96:128],
                                     qpow[hp][32 * i:32 * i + 32, 0:32],
                                     qpow[hp][32 * i:32 * i + 32, 32:64],
                                     start=False, stop=True, tile_position=(32 * i, 32 * i), skip_group_check=True)
                        sPE.inc(t.nop())
                        R("pbD").wrote(sPE)
                    TE(h_nm)
                    nm1 = sPE.v
                    def h_nme(v, hp=hp, nm1=nm1):
                        W.wge(v, "v", sPE, nm1)
                        R("qpow%d" % hp).need_write(W, v, "v")
                        sV.inc(v.tensor_copy(qpow[hp][:, 0:64], pbD[:, 64:128]))
                        R("pbD").read(sV)
                        R("qpow%d" % hp).wrote(sV)
                    VE(h_nme)
                    nme = sV.v
                    def h_na(t, hp=hp, nme=nme):
                        W.wge(t, "t", sV, nme)
                        R("pbD").need_write(W, t, "t")
                        for i in range(4):
                            t.matmul(pbD[32 * i:32 * i + 32, 128:160],
                                     qpow[hp][32 * i:32 * i + 32, 0:32],
                                     accp[hp][32 * i:32 * i + 32, 0:32],
                                     start=True, stop=True, tile_position=(32 * i, 32 * i), skip_group_check=True)
                        sPE.inc(t.nop())
                        R("pbD").wrote(sPE)
                    TE(h_na)
                    na1 = sPE.v
                    def h_nae(v, hp=hp, na1=na1):
                        W.wge(v, "v", sPE, na1)
                        sV.inc(v.scalar_tensor_tensor(accp[hp][:, 0:32], pbD[:, 128:160], 1.0,
                                                      accp[hp][:, 0:32], ALU.mult, ALU.add))
                        R("pbD").read(sV)
                        R("accp%d" % hp).wrote(sV)
                    VE(h_nae)

                if hs < 3:
                    continue
                NA = prod[hp][:, 0:128]
                NK = prod[hp][:, 128:256]
                MBt = prod[hp][:, 256:384]
                MKt = prod[hp][:, 384:512]
                for i in range(4):
                    def h_y(t, i=i, hp=hp, dc=dc, oh=oh, vtk=vtk, NA=NA, NK=NK):
                        W.wge(t, "t", sV, sV.v)
                        R("pbY").need_write(W, t, "t")
                        if i > 0:
                            t.matmul(pbY[32 * i:32 * i + 32, 0:64],
                                     NA[0:32 * i, 32 * i:32 * i + 32],
                                     X_sb[hp][0:32 * i, 0:64],
                                     start=True, stop=True, tile_position=(0, 32 * i), skip_group_check=True)
                        t.matmul(pbY[32 * i:32 * i + 32, 64:128],
                                 NK[0:32 * i + 32, 32 * i:32 * i + 32],
                                 vtk[0:32 * i + 32, :],
                                 start=True, stop=(i == 0), tile_position=(0, 32 * i), skip_group_check=True)
                        if i > 0:
                            t.matmul(pbY[32 * i:32 * i + 32, 64:128],
                                     NA[0:32 * i, 32 * i:32 * i + 32],
                                     X_sb[hp][0:32 * i, 64:128],
                                     start=False, stop=True, tile_position=(0, 32 * i), skip_group_check=True)
                        sPE.inc(t.nop())
                        R("pbY").wrote(sPE)
                    TE(h_y)
                    yt = sPE.v
                    def h_ye(v, i=i, hp=hp, dc=dc, oh=oh, yt=yt):
                        W.wge(v, "v", sPE, yt)
                        atk = watt[0][32 * i:32 * i + 32, dc * 128 + oh:dc * 128 + oh + 64]
                        if i > 0:
                            v.tensor_tensor(y_sb[hp][32 * i:32 * i + 32, 0:64],
                                            pbY[32 * i:32 * i + 32, 0:64], atk, ALU.add)
                        else:
                            v.tensor_copy(y_sb[hp][0:32, 0:64], atk)
                        sV.inc(v.tensor_copy(y_sb[hp][32 * i:32 * i + 32, 64:128],
                                             pbY[32 * i:32 * i + 32, 64:128]))
                        R("pbY").read(sV)
                        R("watt0").read(sV)
                    VE(h_ye)
                    yet = sV.v
                    def h_x(t, i=i, hp=hp, yet=yet):
                        W.wge(t, "t", sV, yet)
                        R("pbX").need_write(W, t, "t")
                        t.matmul(pbX[32 * i:32 * i + 32, 0:128],
                                 accp[hp][32 * i:32 * i + 32, 0:32],
                                 y_sb[hp][32 * i:32 * i + 32, :],
                                 start=True, stop=True, tile_position=(32 * i, 32 * i), skip_group_check=True)
                        sPE.inc(t.nop())
                        R("pbX").wrote(sPE)
                    TE(h_x)
                    xt = sPE.v
                    def h_xe(v, i=i, hp=hp, xt=xt):
                        W.wge(v, "v", sPE, xt)
                        sV.inc(v.tensor_copy(X_sb[hp][32 * i:32 * i + 32, :],
                                             pbX[32 * i:32 * i + 32, 0:128]))
                        R("pbX").read(sV)
                    VE(h_xe)
                solve_t = sV.v
                if hs < 4:
                    continue

                def h_zs(v, dc=dc, oh=oh, par=par, lam_t=lam_t):
                    W.wge(v, "v", sS, lam_t)
                    zo = z_sb[par][oh:oh + 64, dc * 64:(dc + 1) * 64]
                    qo = QT_sb[par][oh:oh + 64, dc * 64:(dc + 1) * 64]
                    v.tensor_scalar(zs_sb[oh:oh + 64, dc * 64:(dc + 1) * 64], zo,
                                    sml[oh:oh + 64, dc * 4 + 1:dc * 4 + 2], None, ALU.mult)
                    sV.inc(v.tensor_scalar(qts_sb[oh:oh + 64, dc * 64:(dc + 1) * 64], qo,
                                           sml[oh:oh + 64, dc * 4 + 1:dc * 4 + 2],
                                           None, ALU.mult))
                VE(h_zs)
                zs_t = sV.v

                def h_phi(t, hp=hp, oh=oh, solve_t=solve_t, MBt=MBt):
                    W.wge(t, "t", sV, solve_t)
                    R("pbM").need_write(W, t, "t")
                    for i in range(4):
                        t.matmul(pbM[oh:oh + 64, 64 + 32 * i:64 + 32 * (i + 1)],
                                 X_sb[hp][0:32 * (i + 1), 0:64],
                                 MBt[0:32 * (i + 1), 32 * i:32 * (i + 1)],
                                 start=(i == 0), stop=(i == 3), tile_position=(0, oh),
                                 skip_group_check=True)
                    sPE.inc(t.nop())
                    R("pbM").wrote(sPE)
                TE(h_phi)
                phi_t = sPE.v
                def h_phie(v, dc=dc, oh=oh, phi_t=phi_t):
                    W.wge(v, "v", sPE, phi_t)
                    sV.inc(v.scalar_tensor_tensor(phiT[oh:oh + 64, dc * 128:(dc + 1) * 128],
                                                  pbM[oh:oh + 64, 64:192], 1.0,
                                                  rh_sb[oh:oh + 64, dc * 128:(dc + 1) * 128],
                                                  ALU.mult, ALU.add))
                    R("pbM").read(sV)
                VE(h_phie)
                phie_t = sV.v
                if hs < 5:
                    continue

                def h_o(t, hp=hp, dc=dc, oh=oh, vtk=vtk, phie_t=phie_t, zs_t=zs_t,
                        MBt=MBt, MKt=MKt):
                    W.wge(t, "t", sV, phie_t)
                    W.wge(t, "t", sV, zs_t)
                    R("pbM").need_write(W, t, "t")
                    for i in range(4):
                        t.matmul(pbM[32 * i:32 * i + 32, 0:64],
                                 MBt[0:32 * (i + 1), 32 * i:32 * i + 32],
                                 X_sb[hp][0:32 * (i + 1), 64:128],
                                 start=True, stop=False, tile_position=(0, 32 * i), skip_group_check=True)
                        t.matmul(pbM[32 * i:32 * i + 32, 0:64],
                                 MKt[0:32 * (i + 1), 32 * i:32 * i + 32],
                                 vtk[0:32 * (i + 1), :],
                                 start=False, stop=True, tile_position=(0, 32 * i), skip_group_check=True)
                    W.wge(t, "t", sPE, sPE.v)
                    t.matmul(pbM[:, 64:128], phiT[oh:oh + 64, dc * 128:(dc + 1) * 128],
                             zs_sb[oh:oh + 64, dc * 64:(dc + 1) * 64],
                             start=True, stop=True, tile_position=(oh, 0), skip_group_check=True)
                    sPE.inc(t.nop())
                    R("pbM").wrote(sPE)
                TE(h_o)
                o_t = sPE.v
                def h_oe(v, hh=hh, o_t=o_t):
                    W.wge(v, "v", sPE, o_t)
                    v.tensor_copy(olo_sb[:, hh * 64:(hh + 1) * 64], pbM[:, 0:64])
                    sV.inc(v.tensor_tensor(olo_sb[:, hh * 64:(hh + 1) * 64],
                                           olo_sb[:, hh * 64:(hh + 1) * 64],
                                           pbM[:, 64:128], ALU.add))
                    R("pbM").read(sV)
                VE(h_oe)
                if hs < 6:
                    continue

                def h_psi(t, dc=dc, oh=oh):
                    W.wge(t, "t", sV, sV.v)
                    R("pbM").need_write(W, t, "t")
                    t.matmul(pbM[oh:oh + 64, 192:320],
                             qts_sb[oh:oh + 64, dc * 64:(dc + 1) * 64],
                             phiT[oh:oh + 64, dc * 128:(dc + 1) * 128],
                             start=True, stop=True, tile_position=(oh, oh), skip_group_check=True)
                    sPE.inc(t.nop())
                    R("pbM").wrote(sPE)
                TE(h_psi)
                psi_t = sPE.v
                def h_psie(v, dc=dc, oh=oh, psi_t=psi_t):
                    W.wge(v, "v", sPE, psi_t)
                    sV.inc(v.tensor_copy(psi_sb[oh:oh + 64, dc * 128:(dc + 1) * 128],
                                         pbM[oh:oh + 64, 192:320]))
                    R("pbM").read(sV)
                VE(h_psie)

                def h_p(t, hp=hp, dc=dc, oh=oh):
                    W.wge(t, "t", sV, sV.v)
                    R("pbM").need_write(W, t, "t")
                    t.matmul(pbM[oh:oh + 64, 320:384], X_sb[hp][:, 0:64],
                             Btok[:, dc * 128 + oh:dc * 128 + oh + 64],
                             start=True, stop=True, tile_position=(0, oh), skip_group_check=True)
                    sPE.inc(t.nop())
                    R("pbM").wrote(sPE)
                TE(h_p)
                p_t = sPE.v
                def h_pe(v, hp=hp, oh=oh, p_t=p_t):
                    W.wge(v, "v", sPE, p_t)
                    R("pr%d" % hp).need_write(W, v, "v")
                    sV.inc(v.tensor_copy(pr_sb[hp][oh:oh + 64, :], pbM[oh:oh + 64, 320:384]))
                    R("pbM").read(sV)
                    R("pr%d" % hp).wrote(sV)
                VE(h_pe)
                pe_t = sV.v
                if hs < 7:
                    continue

                def h_sloc(t, hp=hp, dc=dc, oh=oh, vtk=vtk, pe_t=pe_t):
                    W.wge(t, "t", sV, pe_t)
                    R("pbM").need_write(W, t, "t")
                    t.matmul(pbM[oh:oh + 64, 384:448],
                             Btok[:, dc * 128 + oh:dc * 128 + oh + 64],
                             X_sb[hp][:, 64:128], start=True, stop=False,
                             tile_position=(0, oh), skip_group_check=True)
                    t.matmul(pbM[oh:oh + 64, 384:448],
                             Ktok[:, dc * 128 + oh:dc * 128 + oh + 64],
                             vtk[:], start=False, stop=True, tile_position=(0, oh), skip_group_check=True)
                    W.wge(t, "t", sPE, sPE.v)
                    t.matmul(pbM[oh:oh + 64, 64:128], pr_sb[hp][oh:oh + 64, :],
                             zs_sb[oh:oh + 64, dc * 64:(dc + 1) * 64],
                             start=True, stop=True, tile_position=(oh, oh), skip_group_check=True)
                    t.matmul(pbM[oh:oh + 64, 448:512], pr_sb[hp][oh:oh + 64, :],
                             qts_sb[oh:oh + 64, dc * 64:(dc + 1) * 64],
                             start=True, stop=True, tile_position=(oh, oh), skip_group_check=True)
                    sPE.inc(t.nop())
                    R("pbM").wrote(sPE)
                TE(h_sloc)
                sl_t = sPE.v
                def h_chain(v, par=par, dc=dc, oh=oh, sl_t=sl_t):
                    W.wge(v, "v", sPE, sl_t)
                    zo = z_sb[par][oh:oh + 64, dc * 64:(dc + 1) * 64]
                    zn = z_sb[1 - par][oh:oh + 64, dc * 64:(dc + 1) * 64]
                    qo = QT_sb[par][oh:oh + 64, dc * 64:(dc + 1) * 64]
                    qn = QT_sb[1 - par][oh:oh + 64, dc * 64:(dc + 1) * 64]
                    lca = sml[oh:oh + 64, dc * 4 + 2:dc * 4 + 3]
                    v.scalar_tensor_tensor(zn, zo, lca, pbM[oh:oh + 64, 384:448],
                                           ALU.mult, ALU.add)
                    v.tensor_tensor(zn, zn, pbM[oh:oh + 64, 64:128], ALU.add)
                    sV.inc(v.scalar_tensor_tensor(qn, qo, lca, pbM[oh:oh + 64, 448:512],
                                                  ALU.mult, ALU.add))
                    R("pbM").read(sV)
                VE(h_chain)

            sp_t = sV.v
            def c_spill(sy, c=c, sp_t=sp_t):
                W.wge(sy, "y", sV, sp_t)
                ythrottle(sy); sDX.inc(sy.dma_start(psi_d[c * 128:(c + 1) * 128, :], psi_sb[:]), 16)
                ythrottle(sy); sDX.inc(sy.dma_start(olo_d[c * 128:(c + 1) * 128, :], olo_sb[:]), 16)
            SY(c_spill)
            barrier()

    fpar = NCHK % 2

    if "rec" in taps:
        def tapzd(sy, fpar=fpar):
            W.wge(sy, "y", sV, sV.v)
            ythrottle(sy); sDX.inc(sy.dma_start(tap_d["tp_z"][:], z_sb[fpar][:]), 16)
        SY(tapzd)

    if level <= 1:
        fin0 = sV.v
        def out_dma0(sy, fin0=fin0):
            W.wge(sy, "y", sV, fin0)
            W.wge(sy, "y", sS, sS.v)
            for dc in range(DCN):
                ythrottle(sy); sDX.inc(sy.dma_start(out_d[dc * 128:(dc + 1) * 128, :],
                                     xT[:, dc * XS + 1:(dc + 1) * XS]), 16)
            sy.wait_ge(sDX.h, sDX.v)
        SY(out_dma0)
        blk_cm.__exit__(None, None, None)
        for cm in pctx[::-1]:
            cm.__exit__(None, None, None)
        for cm in actx[::-1]:
            cm.__exit__(None, None, None)
        for cm in abctx[::-1]:
            cm.__exit__(None, None, None)
        for cm in ctx[::-1]:
            cm.__exit__(None, None, None)
        return nc

    for dc in range(DCN):
        def f_q(t, dc=dc, fpar=fpar):
            W.wge(t, "t", sV, sV.v)
            R("pbY").need_write(W, t, "t")
            t.transpose(pbY[0:64, 0:128], QT_sb[fpar][:, dc * 64:(dc + 1) * 64], cid_f[:])
            sPE.inc(t.nop())
            R("pbY").wrote(sPE)
        TE(f_q)
        fq1 = sPE.v
        def f_qe(v, dc=dc, fq1=fq1):
            W.wge(v, "v", sPE, fq1)
            R("cw").need_write(W, v, "v")
            sV.inc(v.tensor_copy(qfin[0:64, dc * 129:dc * 129 + 128], pbY[0:64, 0:128]))
            R("pbY").read(sV)
            R("cw").wrote(sV)
        VE(f_qe)
    fq_t = sV.v

    def f_ccd(sy, fpar=fpar, fq_t=fq_t):
        W.wge(sy, "y", sV, fq_t)
        for dc in range(DCN):
            ythrottle(sy); sDX.inc(sy.dma_start(
                cc_in.ap()[dc * 128:(dc + 1) * 128, :].rearrange("(o p) k -> p o k", o=2),
                qfin[0:64, dc * 129:dc * 129 + 128].rearrange("p (o k) -> p o k", o=2)), 16)
            ythrottle(sy); sDX.inc(sy.dma_start(cc_in.ap()[1024 + dc * 128:1024 + (dc + 1) * 128, :],
                                 z_sb[fpar][:, dc * 64:(dc + 1) * 64]), 16)
    SY(f_ccd)
    ccd_t = sDX.v

    def f_cc(g, ccd_t=ccd_t):
        W.wge(g, "g", sDX, ccd_t)
        g.collective_compute("AllGather", ALU.bypass,
                             replica_groups=[list(range(NCORE))],
                             ins=[cc_in.ap().opt()], outs=[cc_out.ap().opt()]).then_inc(sCC.h)
        sCC.v += 1
        g.wait_ge(sCC.h, sCC.v)
        sG.inc(g.nop())
    GE(f_cc)
    barrier()

    for cm in actx[::-1]:
        cm.__exit__(None, None, None)

    bctx = []
    bsb = mksb(bctx)
    xffn = bsb("xffn", [128, DCN * TC], BF16)
    dwT = bsb("dwT", [8, 2048], BF16)
    agq = bsb("agq", [128, DCN * 64], F32)
    agz = bsb("agz", [128, DCN * 64], F32)
    gcur = bsb("gcur", [128, DCN * 64], F32)
    gnext = bsb("gnext", [128, DCN * 64], F32)
    gnw_bc = bsb("gnw_bc", [128, 1024], F32)
    gnb_bc = bsb("gnb_bc", [128, 1024], F32)
    gtmp = bsb("gtmpB", [1, 2048], BF16)

    def b_gld(g):
        W.wge(g, "g", sG, sG.v)
        gthrottle(g); sDW.inc(g.dma_start(gtmp[0:1, 0:1024], vecs_d[:, 1:2].rearrange("d o -> o d"), allow_slow_non_contiguous=True), 16)
        gthrottle(g); sDW.inc(g.dma_start(gtmp[0:1, 1024:2048], vecs_d[:, 2:3].rearrange("d o -> o d"), allow_slow_non_contiguous=True), 16)
    GE(b_gld)
    gld_t = sDW.v
    for half, dst in ((0, gnw_bc), (1, gnb_bc)):
        for piece in range(2):
            def bg_mm(t, half=half, piece=piece, gld_t=gld_t):
                W.wge(t, "t", sDW, gld_t)
                W.wge(t, "t", sV, sV.v)
                R("pbC").need_write(W, t, "t")
                t.matmul(pbC[:, 0:512], conesr[0:1, :],
                         gtmp[0:1, half * 1024 + piece * 512:half * 1024 + (piece + 1) * 512],
                         start=True, stop=True)
                sPE.inc(t.nop())
                R("pbC").wrote(sPE)
            TE(bg_mm)
            bg1 = sPE.v
            def bg_ev(v, dst=dst, piece=piece, bg1=bg1):
                W.wge(v, "v", sPE, bg1)
                sV.inc(v.tensor_copy(dst[:, piece * 512:(piece + 1) * 512], pbC[:, 0:512]))
                R("pbC").read(sV)
            VE(bg_ev)

    def b_g0(v):
        W.wge(v, "v", sG, sG.v)
        sV.inc(v.memset(gcur[:], 0.0))
    VE(b_g0)

    for j in range(NCORE):
        def b_ld(g, j=j):
            W.wge(g, "g", sV, sV.v)
            W.wge(g, "g", sDW, sDW.v)
            R("agq").need_write(W, g, "g")
            gthrottle(g); sDW.inc(g.dma_start(agq[:].rearrange("p (c k) -> p c k", c=DCN),
                                cc_out.ap()[j * 2048:j * 2048 + 1024, :]
                                .rearrange("(c p) k -> p c k", p=128)), 16)
            gthrottle(g); sDW.inc(g.dma_start(agz[:].rearrange("p (c k) -> p c k", c=DCN),
                                cc_out.ap()[j * 2048 + 1024:j * 2048 + 2048, :]
                                .rearrange("(c p) k -> p c k", p=128)), 16)
            R("agq").wrote(sDW)
        GE(b_ld)
        ld_t = sDW.v
        def b_cm(t, ld_t=ld_t):
            W.wge(t, "t", sDW, ld_t)
            W.wge(t, "t", sV, sV.v)
            R("pbY").need_write(W, t, "t")
            for oh in (0, 64):
                for dc in range(DCN):
                    t.matmul(pbY[oh:oh + 64, dc * 64:(dc + 1) * 64],
                             agq[oh:oh + 64, dc * 64:(dc + 1) * 64],
                             gcur[oh:oh + 64, dc * 64:(dc + 1) * 64],
                             start=True, stop=True, tile_position=(oh, oh), skip_group_check=True)
            R("pbY").wrote(sPE)
        TE(b_cm)
        cmv = sPE.v
        def b_ce(v, j=j, cmv=cmv):
            W.wge(v, "v", sPE, cmv)
            for oh in (0, 64):
                for dc in range(DCN):
                    v.tensor_tensor(gnext[oh:oh + 64, dc * 64:(dc + 1) * 64],
                                    pbY[oh:oh + 64, dc * 64:(dc + 1) * 64],
                                    agz[oh:oh + 64, dc * 64:(dc + 1) * 64], ALU.add)
                    v.scalar_tensor_tensor(
                        SinT[oh:oh + 64, dc * 64:(dc + 1) * 64],
                        gnext[oh:oh + 64, dc * 64:(dc + 1) * 64],
                        selrep[oh:oh + 64, j:j + 1],
                        SinT[oh:oh + 64, dc * 64:(dc + 1) * 64], ALU.mult, ALU.add)
            sV.inc(v.engine_nop())
            R("pbY").read(sV)
        VE(b_ce)
        def b_swap(v, j=j):
            W.wge(v, "v", sV, sV.v)
            sV.inc(v.tensor_copy(gcur[:], gnext[:]))
        VE(b_swap)

    def b_sc(v):
        W.wge(v, "v", sV, sV.v)
        sV.inc(v.tensor_copy(sinb[:], SinT[:]))
    VE(b_sc)
    sin_t = sV.v

    if "attn" in taps:
        def tapsin(sy, sin_t=sin_t):
            W.wge(sy, "y", sV, sin_t)
            ythrottle(sy); sDX.inc(sy.dma_start(tap_d["tp_sin"][:], SinT[:]), 16)
        SY(tapsin)

    wo_sb = bsb("wo_sb", [128, DCN * 1024], BF16)
    def b_wol(g):
        gthrottle(g); sDW.inc(g.dma_start(wo_sb[:].rearrange("p (c e) -> p c e", c=DCN), attw_d[7].rearrange("(c p) e -> p c e", p=128)), 16)
    GE(b_wol)
    wol_t = sDW.v

    o_f = bsb("o_f", [128, 1024], F32)
    osq = bsb("osq", [128, 1024], F32)
    gn16 = bsb("gn16", [128, 64], F32)
    comb = bsb("comb", [128, 1024], F32)
    nrm = bsb("nrm", [128, 1024], BF16)
    gch = bsb("gch", [128, DCN * 128], BF16)
    ngT = bsb("ngT", [128, DCN * 128], BF16)
    bpsi = bsb("bpsi", [128, DCN * 128], BF16)
    bolo = bsb("bolo", [128, H * 64], BF16)

    for c in range(NCHK):
        def b_cl(g, c=c):
            W.wge(g, "g", sDW, sDW.v)
            R("bpsi").need_write(W, g, "g")
            R("bolo").need_write(W, g, "g")
            gthrottle(g); sDW.inc(g.dma_start(bpsi[:], psi_d[c * 128:(c + 1) * 128, :]), 16)
            gthrottle(g); sDW.inc(g.dma_start(bolo[:], olo_d[c * 128:(c + 1) * 128, :]), 16)
            gthrottle(g); sDW.inc(g.dma_start(gch[:].rearrange("p (c2 t) -> p c2 t", c2=DCN),
                                g_d.rearrange("(c2 p) t -> p c2 t", p=128)[:, :, c * 128:(c + 1) * 128]), 16)
            R("bpsi").wrote(sDW)
        GE(b_cl)
        bcl_t = sDW.v

        def b_sig1(s, bcl_t=bcl_t):
            W.wge(s, "s", sDW, bcl_t)
            R("comb").need_write(W, s, "s")
            for dc in range(DCN):
                s.activation(comb[:, dc * 128:(dc + 1) * 128],
                             gch[:, dc * 128:(dc + 1) * 128], AF.Exp, scale=-1.0)
            sS.inc(s.nop())
            R("comb").wrote(sS)
        SE(b_sig1)
        sg_t = sS.v
        def b_sig2(v, sg_t=sg_t):
            W.wge(v, "v", sS, sg_t)
            v.tensor_scalar_add(comb[:], comb[:], 1.0)
            v.reciprocal(comb[:], comb[:])
            sV.inc(v.tensor_copy(gch[:], comb[:]))
            R("comb").read(sV)
        VE(b_sig2)
        gsig_t = sV.v

        for half in range(2):
            bi = half
            def b_corr(t, half=half, bi=bi, sin_t=sin_t, bcl_t=bcl_t):
                W.wge(t, "t", sV, sin_t)
                W.wge(t, "t", sDW, bcl_t)
                R("pbP%d" % bi).need_write(W, t, "t")
                for ohh_pass in (0, 64):
                    for hx in range(8):
                        hh = half * 8 + hx
                        dch = hh // 2
                        ohh = 64 * (hh % 2)
                        if ohh != ohh_pass:
                            continue
                        t.matmul(pbP[bi][:, hx * 64:(hx + 1) * 64],
                                 bpsi[ohh:ohh + 64, dch * 128:(dch + 1) * 128],
                                 sinb[ohh:ohh + 64, dch * 64:(dch + 1) * 64],
                                 start=True, stop=True, tile_position=(ohh, 0), skip_group_check=True)
                    if ohh_pass == 0:
                        W.wge(t, "t", sPE, sPE.v)
                sPE.inc(t.nop())
                R("pbP%d" % bi).wrote(sPE)
                R("bpsi").read(sPE)
            TE(b_corr)
            cor_t = sPE.v
            def b_core(v, half=half, bi=bi, cor_t=cor_t):
                W.wge(v, "v", sPE, cor_t)
                R("o_f").need_write(W, v, "v")
                sV.inc(v.scalar_tensor_tensor(o_f[:, half * 512:(half + 1) * 512],
                                              pbP[bi][:, 0:512], 1.0,
                                              bolo[:, half * 512:(half + 1) * 512],
                                              ALU.mult, ALU.add))
                R("pbP%d" % bi).read(sV)
                R("bolo").read(sV)
            VE(b_core)
        oc_t = sV.v

        if "rec" in taps:
            def tapo(sy, c=c, oc_t=oc_t):
                W.wge(sy, "y", sV, oc_t)
                ythrottle(sy); sDX.inc(sy.dma_start(tap_d["tp_o"][c * 128:(c + 1) * 128, :], o_f[:]), 16)
                R("o_f").read(sDX)
            SY(tapo)

        def b_gn1(s, oc_t=oc_t):
            W.wge(s, "s", sV, oc_t)
            R("osq").need_write(W, s, "s")
            sS.inc(s.activation(osq[:], o_f[:], AF.Square))
            R("osq").wrote(sS)
        SE(b_gn1)
        gn1 = sS.v
        def b_gn2(v, gn1=gn1):
            W.wge(v, "v", sS, gn1)
            o3 = o_f[:].rearrange("p (h k) -> p h k", h=H)
            q3 = osq[:].rearrange("p (h k) -> p h k", h=H)
            v.tensor_reduce(gn16[:, 0:16].rearrange("p (h o) -> p h o", o=1), o3, AX.X, ALU.add)
            v.tensor_reduce(gn16[:, 16:32].rearrange("p (h o) -> p h o", o=1), q3, AX.X, ALU.add)
            v.tensor_scalar(gn16[:, 0:16], gn16[:, 0:16], 1.0 / KD, None, ALU.mult)
            v.tensor_scalar(gn16[:, 16:32], gn16[:, 16:32], 1.0 / KD, None, ALU.mult)
            v.tensor_tensor(gn16[:, 32:48], gn16[:, 0:16], gn16[:, 0:16], ALU.mult)
            sV.inc(v.tensor_tensor(gn16[:, 16:32], gn16[:, 16:32], gn16[:, 32:48], ALU.subtract))
            R("osq").read(sV)
        VE(b_gn2)
        gn2 = sV.v
        def b_gn3(s, gn2=gn2):
            W.wge(s, "s", sV, gn2)
            s.activation(gn16[:, 32:48], gn16[:, 16:32], AF.Ln, bias=kb[:, 2:3])
            sS.inc(s.activation(gn16[:, 32:48], gn16[:, 32:48], AF.Exp, scale=-0.5))
        SE(b_gn3)
        gn3 = sS.v
        def b_gn4(v, gn3=gn3):
            W.wge(v, "v", sS, gn3)
            rb = gn16[:, 32:48].rearrange("p (h o) -> p h o", o=1).broadcast_to([128, H, KD])
            mb = gn16[:, 0:16].rearrange("p (h o) -> p h o", o=1).broadcast_to([128, H, KD])
            v.tensor_tensor(comb[:].rearrange("p (h k) -> p h k", h=H), rb,
                            gnw_bc[:].rearrange("p (h k) -> p h k", h=H), ALU.mult)
            v.tensor_tensor(osq[:].rearrange("p (h k) -> p h k", h=H),
                            o_f[:].rearrange("p (h k) -> p h k", h=H), mb, ALU.subtract)
            v.tensor_tensor(osq[:], osq[:], comb[:], ALU.mult)
            sV.inc(v.tensor_tensor(nrm[:], osq[:], gnb_bc[:], ALU.add))
        VE(b_gn4)
        gn4 = sV.v

        for dc in range(DCN):
            def b_tp(t, dc=dc, gn4=gn4):
                W.wge(t, "t", sV, gn4)
                R("pbT").need_write(W, t, "t")
                t.transpose(pbT[:, 0:128], nrm[:, dc * 128:(dc + 1) * 128], cid_b[:])
                sPE.inc(t.nop())
                R("pbT").wrote(sPE)
            TE(b_tp)
            btp = sPE.v
            def b_tpe(v, dc=dc, btp=btp, gsig_t=gsig_t):
                W.wge(v, "v", sPE, btp)
                W.wge(v, "v", sV, gsig_t)
                sV.inc(v.tensor_tensor(ngT[:, dc * 128:(dc + 1) * 128], pbT[:, 0:128],
                                       gch[:, dc * 128:(dc + 1) * 128], ALU.mult))
                R("pbT").read(sV)
                if dc == DCN - 1:
                    R("bolo").read(sV)
            VE(b_tpe)
        ng_t = sV.v

        for ec in range(DCN):
            bi = ec % 2
            def b_wo(t, ec=ec, bi=bi, ng_t=ng_t, wol_t=wol_t):
                W.wge(t, "t", sV, ng_t)
                W.wge(t, "t", sDW, wol_t)
                R("pbP%d" % bi).need_write(W, t, "t")
                for dc in range(DCN):
                    t.matmul(pbP[bi][:, 0:128],
                             wo_sb[:, dc * 1024 + ec * 128:dc * 1024 + (ec + 1) * 128],
                             ngT[:, dc * 128:(dc + 1) * 128],
                             start=(dc == 0), stop=(dc == DCN - 1))
                sPE.inc(t.nop())
                R("pbP%d" % bi).wrote(sPE)
            TE(b_wo)
            wo_t = sPE.v
            def b_woe(s, ec=ec, bi=bi, wo_t=wo_t):
                W.wge(s, "s", sPE, wo_t)
                R("osq").need_write(W, s, "s")
                sS.inc(s.activation(osq[:, 0:128], pbP[bi][:, 0:128], AF.Identity,
                                    bias=vecs_sb[:, ec * 4:ec * 4 + 1]))
                R("osq").wrote(sS)
            SE(b_woe)
            woe_t = sS.v
            def b_res(v, c=c, ec=ec, bi=bi, woe_t=woe_t):
                W.wge(v, "v", sS, woe_t)
                xs = xT[:, ec * XS + 1 + c * 128: ec * XS + 1 + (c + 1) * 128]
                sV.inc(v.tensor_tensor(xs, xs, osq[:, 0:128], ALU.add))
                R("pbP%d" % bi).read(sV)
                R("osq").read(sV)
            VE(b_res)

    barrier()
    if "attn" in taps:
        def tapx1(sy):
            W.wge(sy, "y", sV, sV.v)
            for dc in range(DCN):
                ythrottle(sy); sDX.inc(sy.dma_start(tap_d["tp_x1"][dc * 128:(dc + 1) * 128, :],
                                     xT[:, dc * XS + 1:(dc + 1) * XS]), 16)
            R("xTr").read(sDX)
        SY(tapx1)

    bmsc = bsb("bmsc", [128, 512], BF16)
    bstatb = bsb("bstatb", [1, 1024], BF16)
    TB2 = 512
    for it in range(TC // TB2):
        t0 = it * TB2
        def l2_a(v, t0=t0):
            W.wge(v, "v", sV, sV.v)
            W.wge(v, "v", sDX, sDX.v)
            for dc in range(DCN):
                v.tensor_copy(xffn[:, dc * TC + t0:dc * TC + t0 + TB2],
                              xT[:, dc * XS + 1 + t0: dc * XS + 1 + t0 + TB2])
            sV.inc(v.engine_nop())
        VE(l2_a)
        l2a = sV.v
        def l2_mm1(t, t0=t0, l2a=l2a):
            W.wge(t, "t", sV, l2a)
            R("pbC").need_write(W, t, "t")
            for dc in range(DCN):
                t.matmul(pbC[0:1, 0:TB2], conesr[:, 0:1],
                         xffn[:, dc * TC + t0:dc * TC + t0 + TB2],
                         start=(dc == 0), stop=(dc == DCN - 1))
            sPE.inc(t.nop())
            R("pbC").wrote(sPE)
        TE(l2_mm1)
        l2m1 = sPE.v
        def l2_mv1(v, l2m1=l2m1):
            W.wge(v, "v", sPE, l2m1)
            sV.inc(v.tensor_scalar(wt2[0:1, 0:TB2], pbC[0:1, 0:TB2], 1.0 / D, None, ALU.mult))
            R("pbC").read(sV)
        VE(l2_mv1)
        for dc in range(DCN):
            def l2_sq2(v, dc=dc, t0=t0):
                R("bmsc").need_write(W, v, "v")
                sV.inc(v.tensor_tensor(bmsc[:, 0:TB2],
                                       xffn[:, dc * TC + t0:dc * TC + t0 + TB2],
                                       xffn[:, dc * TC + t0:dc * TC + t0 + TB2], ALU.mult))
                R("bmsc").wrote(sV)
            VE(l2_sq2)
            sq2 = sV.v
            def l2_mm2(t, dc=dc, sq2=sq2):
                W.wge(t, "t", sV, sq2)
                if dc == 0:
                    R("pbP0").need_write(W, t, "t")
                t.matmul(pbP[0][0:1, 0:TB2], conesr[:, 0:1], bmsc[:, 0:TB2],
                         start=(dc == 0), stop=(dc == DCN - 1))
                sPE.inc(t.nop())
                if dc == DCN - 1:
                    R("pbP0").wrote(sPE)
                R("bmsc").read(sPE)
            TE(l2_mm2)
        l2m2 = sPE.v
        def l2_mv2(v, l2m2=l2m2):
            W.wge(v, "v", sPE, l2m2)
            v.tensor_scalar(wt1[0:1, 0:TB2], pbP[0][0:1, 0:TB2], 1.0 / D, None, ALU.mult)
            R("pbP0").read(sV)
            v.tensor_tensor(osq[0:1, 0:TB2], wt2[0:1, 0:TB2], wt2[0:1, 0:TB2], ALU.mult)
            sV.inc(v.tensor_tensor(wt1[0:1, 0:TB2], wt1[0:1, 0:TB2],
                                   osq[0:1, 0:TB2], ALU.subtract))
        VE(l2_mv2)
        l2v = sV.v
        def l2_rs(s, l2v=l2v):
            W.wge(s, "s", sV, l2v)
            s.activation(wt1[0:1, 0:TB2], wt1[0:1, 0:TB2], AF.Ln, bias=kb[0:1, 0:1])
            sS.inc(s.activation(wt1[0:1, 0:TB2], wt1[0:1, 0:TB2], AF.Exp, scale=-0.5))
        SE(l2_rs)
        l2r = sS.v
        def l2_c2(v, l2r=l2r):
            W.wge(v, "v", sS, l2r)
            v.tensor_copy(bstatb[0:1, 0:TB2], wt2[0:1, 0:TB2])
            sV.inc(v.tensor_copy(bstatb[0:1, TB2:2 * TB2], wt1[0:1, 0:TB2]))
        VE(l2_c2)
        l2c2 = sV.v
        def l2_bc(t, l2c2=l2c2):
            W.wge(t, "t", sV, l2c2)
            R("pbC").need_write(W, t, "t")
            R("pbP0").need_write(W, t, "t")
            t.matmul(pbC[:, 0:TB2], conesr[0:1, :], bstatb[0:1, 0:TB2], start=True, stop=True)
            sPE.inc(t.matmul(pbP[0][:, 0:TB2], conesr[0:1, :], bstatb[0:1, TB2:2 * TB2],
                             start=True, stop=True))
            R("pbC").wrote(sPE)
            R("pbP0").wrote(sPE)
        TE(l2_bc)
        l2b = sPE.v
        def l2_f(v, t0=t0, l2b=l2b):
            W.wge(v, "v", sPE, l2b)
            R("wt1").need_write(W, v, "v")
            for dc in range(DCN):
                v.tensor_tensor(wt1[:, 0:TB2], xT[:, dc * XS + 1 + t0: dc * XS + 1 + t0 + TB2],
                                pbC[:, 0:TB2], ALU.subtract)
                v.tensor_tensor(xffn[:, dc * TC + t0:dc * TC + t0 + TB2],
                                wt1[:, 0:TB2], pbP[0][:, 0:TB2], ALU.mult)
            sV.inc(v.engine_nop())
            R("pbC").read(sV)
            R("pbP0").read(sV)
            R("wt1").read(sV)
            R("wt1").wrote(sV)
        VE(l2_f)

    if "xffn" in taps:
        for dc in range(DCN):
            for hl in range(2):
                def txf(v, dc=dc, hl=hl):
                    W.wge(v, "v", sV, sV.v)
                    W.wge(v, "v", sDX, sDX.v)
                    R("o_f").need_write(W, v, "v")
                    sV.inc(v.tensor_copy(o_f[:, 0:1024],
                                         xffn[:, dc * TC + hl * 1024:dc * TC + (hl + 1) * 1024]))
                VE(txf)
                t_ = sV.v
                def txfd(sy, dc=dc, hl=hl, t_=t_):
                    W.wge(sy, "y", sV, t_)
                    ythrottle(sy); sDX.inc(sy.dma_start(
                        tap_d["tp_xffn"][dc * 128:(dc + 1) * 128, hl * 1024:(hl + 1) * 1024],
                        o_f[:, 0:1024]), 16)
                    R("o_f").read(sDX)
                SY(txfd)

    logit = bsb("logit", [128, 16 * 8], F32)
    d1_sb = bsb("d1_sb", [128, 16 * 8], F32)
    d2_sb = bsb("d2_sb", [128, 16 * 8], F32)
    se1_sb = bsb("se1_sb", [128, 16 * 8], F32)
    se2_sb = bsb("se2_sb", [128, 16 * 8], F32)
    oh2_sb = bsb("oh2_sb", [128, 16 * 8], F32)
    oh1_sb = bsb("oh1_sb", [128, 16 * 8], F32)
    smx = bsb("smx", [128, 16], F32)
    dwtm = bsb("dwtm", [128, 8], BF16)

    for tt in range(16):
        def r_mm(t, tt=tt):
            W.wge(t, "t", sV, sV.v)
            R("pbC").need_write(W, t, "t")
            for dc in range(DCN):
                t.matmul(pbC[0:8, 0:128], routw_sb[:, dc * 8:(dc + 1) * 8],
                         xffn[:, dc * TC + tt * 128:dc * TC + (tt + 1) * 128],
                         start=(dc == 0), stop=(dc == DCN - 1))
            sPE.inc(t.nop())
            R("pbC").wrote(sPE)
        TE(r_mm)
        rmt = sPE.v
        def r_ev(v, tt=tt, rmt=rmt):
            W.wge(v, "v", sPE, rmt)
            R("wt1").need_write(W, v, "v")
            sV.inc(v.tensor_copy(wt1[0:8, 0:128], pbC[0:8, 0:128]))
            R("pbC").read(sV)
            R("wt1").wrote(sV)
        VE(r_ev)
        ret = sV.v
        def r_tp(t, tt=tt, ret=ret):
            W.wge(t, "t", sV, ret)
            R("pbX").need_write(W, t, "t")
            t.transpose(pbX[:, 0:8], wt1[0:8, 0:128], cid_f[0:8, 0:8])
            sPE.inc(t.nop())
            R("pbX").wrote(sPE)
            R("wt1").read(sPE)
        TE(r_tp)
        rtt = sPE.v
        def r_te(v, tt=tt, rtt=rtt):
            W.wge(v, "v", sPE, rtt)
            sV.inc(v.tensor_copy(logit[:, tt * 8:(tt + 1) * 8], pbX[:, 0:8]))
            R("pbX").read(sV)
        VE(r_te)

    def r_s1(v):
        W.wge(v, "v", sV, sV.v)
        for tt in range(16):
            lg = logit[:, tt * 8:(tt + 1) * 8]
            m1 = smx[:, 0:1]
            m2 = smx[:, 1:2]
            f_ = smx[:, 2:10]
            v.tensor_reduce(m1, lg, AX.X, ALU.max)
            m1b = m1.broadcast_to([128, 8])
            zb = kb[:, 3:4].broadcast_to([128, 8])
            v.tensor_tensor(oh1_sb[:, tt * 8:(tt + 1) * 8], lg, m1b, ALU.is_ge)
            v.tensor_scalar(d1_sb[:, tt * 8:(tt + 1) * 8], lg, m1, None, ALU.subtract)
            v.tensor_scalar(f_, lg, -1.0, None, ALU.mult)
            v.tensor_tensor(f_, f_, lg, ALU.max)
            v.tensor_scalar(f_, f_, m1, None, ALU.max)
            v.scalar_tensor_tensor(se1_sb[:, tt * 8:(tt + 1) * 8], f_, 0.02,
                                   d1_sb[:, tt * 8:(tt + 1) * 8], ALU.mult, ALU.add)
            v.tensor_tensor(se1_sb[:, tt * 8:(tt + 1) * 8],
                            se1_sb[:, tt * 8:(tt + 1) * 8], zb, ALU.is_ge)
            mk = smx[:, 2:10]
            v.scalar_tensor_tensor(mk, oh1_sb[:, tt * 8:(tt + 1) * 8], -1e30, lg,
                                   ALU.mult, ALU.add)
            v.tensor_reduce(m2, mk, AX.X, ALU.max)
            m2b = m2.broadcast_to([128, 8])
            v.tensor_tensor(oh2_sb[:, tt * 8:(tt + 1) * 8], mk, m2b, ALU.is_ge)
            v.tensor_scalar(d2_sb[:, tt * 8:(tt + 1) * 8], mk, m2, None, ALU.subtract)
            f2 = smx[:, 2:10]
            v.tensor_scalar(f2, lg, -1.0, None, ALU.mult)
            v.tensor_tensor(f2, f2, lg, ALU.max)
            v.tensor_scalar(f2, f2, m2, None, ALU.max)
            v.tensor_scalar(se2_sb[:, tt * 8:(tt + 1) * 8], lg, m2, None, ALU.subtract)
            v.scalar_tensor_tensor(se2_sb[:, tt * 8:(tt + 1) * 8], f2, 0.02,
                                   se2_sb[:, tt * 8:(tt + 1) * 8], ALU.mult, ALU.add)
            v.tensor_tensor(se2_sb[:, tt * 8:(tt + 1) * 8],
                            se2_sb[:, tt * 8:(tt + 1) * 8], zb, ALU.is_ge)
        sV.inc(v.engine_nop())
    VE(r_s1)
    s1_t = sV.v
    def r_exp(s, s1_t=s1_t):
        W.wge(s, "s", sV, s1_t)
        s.activation(d1_sb[:], d1_sb[:], AF.Exp)
        sS.inc(s.activation(d2_sb[:], d2_sb[:], AF.Exp))
    SE(r_exp)
    ex_t = sS.v
    def r_s2(v, ex_t=ex_t):
        W.wge(v, "v", sS, ex_t)
        for tt in range(16):
            sl_ = slice(tt * 8, (tt + 1) * 8)
            den1 = smx[:, 0:1]
            den2 = smx[:, 1:2]
            v.tensor_tensor(d1_sb[:, sl_], d1_sb[:, sl_], se1_sb[:, sl_], ALU.mult)
            v.tensor_tensor(d2_sb[:, sl_], d2_sb[:, sl_], se2_sb[:, sl_], ALU.mult)
            v.tensor_reduce(den1, d1_sb[:, sl_], AX.X, ALU.add)
            v.tensor_reduce(den2, d2_sb[:, sl_], AX.X, ALU.add)
            v.reciprocal(den1, den1)
            v.reciprocal(den2, den2)
            v.tensor_scalar(oh2_sb[:, sl_], oh2_sb[:, sl_], den2, None, ALU.mult)
            v.scalar_tensor_tensor(oh1_sb[:, sl_], oh1_sb[:, sl_], den1,
                                   oh2_sb[:, sl_], ALU.mult, ALU.add)
        sV.inc(v.engine_nop())
    VE(r_s2)
    dw_t = sV.v

    if "dw" in taps:
        def tapdwd(sy, dw_t=dw_t):
            W.wge(sy, "y", sV, dw_t)
            ythrottle(sy); sDX.inc(sy.dma_start(tap_d["tp_dw"].rearrange("(t p) e -> p t e", p=128),
                                 oh1_sb[:].rearrange("p (t e) -> p t e", e=8)), 16)
        SY(tapdwd)

    for tt in range(16):
        def dw_c(v, tt=tt, dw_t=dw_t):
            W.wge(v, "v", sV, dw_t)
            R("dwtm").need_write(W, v, "v")
            sV.inc(v.tensor_copy(dwtm[:], oh1_sb[:, tt * 8:(tt + 1) * 8]))
            R("dwtm").wrote(sV)
        VE(dw_c)
        dwc = sV.v
        def dw_tp(t, tt=tt, dwc=dwc):
            W.wge(t, "t", sV, dwc)
            R("pbT").need_write(W, t, "t")
            t.transpose(pbT[0:8, 0:128], dwtm[:], cid_b[:])
            sPE.inc(t.nop())
            R("pbT").wrote(sPE)
            R("dwtm").read(sPE)
        TE(dw_tp)
        dtt = sPE.v
        def dw_te(v, tt=tt, dtt=dtt):
            W.wge(v, "v", sPE, dtt)
            sV.inc(v.tensor_copy(dwT[0:8, tt * 128:(tt + 1) * 128], pbT[0:8, 0:128]))
            R("pbT").read(sV)
        VE(dw_te)

    bsp_t = sV.v
    def b_spill(sy, bsp_t=bsp_t):
        W.wge(sy, "y", sV, bsp_t)
        ythrottle(sy); sDX.inc(sy.dma_start(
            xffn_d.rearrange("(c p) t -> p c t", p=128),
            xffn[:].rearrange("p (c t) -> p c t", c=DCN)), 16)
        ythrottle(sy); sDX.inc(sy.dma_start(dwT_d, dwT[0:8, :]), 16)
        R("xffnd").wrote(sDX)
    SY(b_spill)

    barrier()
    if level <= 2:
        fin2 = sV.v
        def out_dma2(sy, fin2=fin2):
            W.wge(sy, "y", sV, fin2)
            for dc in range(DCN):
                ythrottle(sy); sDX.inc(sy.dma_start(out_d[dc * 128:(dc + 1) * 128, :],
                                     xT[:, dc * XS + 1:(dc + 1) * XS]), 16)
            sy.wait_ge(sDX.h, sDX.v)
        SY(out_dma2)
        blk_cm.__exit__(None, None, None)
        for cm in pctx[::-1]:
            cm.__exit__(None, None, None)
        for cm in bctx[::-1]:
            cm.__exit__(None, None, None)
        for cm in abctx[::-1]:
            cm.__exit__(None, None, None)
        for cm in bcctx[::-1]:
            cm.__exit__(None, None, None)
        for cm in ctx[::-1]:
            cm.__exit__(None, None, None)
        return nc
    for cm in bctx[::-1]:
        cm.__exit__(None, None, None)
    for cm in abctx[::-1]:
        cm.__exit__(None, None, None)
    abctx = []

    cctx = []
    csb = mksb(cctx)
    wgat = [csb("wgat%d" % p, [128, DCN * 1024], BF16) for p in range(2)]
    wup = csb("wup", [128, DCN * 1024], BF16)
    wdn = csb("wdn", [128, DCN * 1024], BF16)
    h_sb = csb("h_sb", [128, DCN * 1024], BF16)
    dwb = csb("dwb", [128, 1024], BF16)
    gs_sb = csb("gs_sb", [128, 512], F32)
    xffn = csb("xffn_c", [128, DCN * TC], BF16)
    dwT = csb("dwT_c", [8, 2048], BF16)

    def c_reload(g):
        R("xffnd").need_read(W, g, "g")
        gthrottle(g); sDW.inc(g.dma_start(
            xffn[:].rearrange("p (c t) -> p c t", c=DCN),
            xffn_d.rearrange("(c p) t -> p c t", p=128)), 16)
        gthrottle(g); sDW.inc(g.dma_start(dwT[0:8, :], dwT_d), 16)
        R("xffnc").wrote(sDW)
    GE(c_reload)

    def c_reload_w(v):
        R("xffnc").need_read(W, v, "v")
        sV.inc(v.engine_nop())
    VE(c_reload_w)
    xfr_t = sV.v

    pbG = [pbP[0], pbD]
    pbU = [pbP[1], pbY]
    pbDN = [pbC, pbX]

    TH = 1024
    units = [(e, fh) for e in range(E) for fh in range(2)]

    def moe_load_g(u):
        e, fh = units[u]
        p = u % 2
        def m_wl(g, e=e, fh=fh, p=p):
            R("wmoe%d" % p).need_write(W, g, "g")
            gthrottle(g); sDW.inc(g.dma_start(
                wgat[p][:].rearrange("p (c f) -> p c f", c=DCN),
                wg_d[e].rearrange("(c p) f -> p c f", p=128)[:, :, fh * 1024:(fh + 1) * 1024]), 16)
            R("wmoe%d" % p).wrote(sDW)
        GE(m_wl)
        return sDW.v

    def moe_load_u(u):
        e, fh = units[u]
        def m_wlu(g, e=e, fh=fh):
            R("wupr").need_write(W, g, "g")
            gthrottle(g); sDW.inc(g.dma_start(
                wup[:].rearrange("p (c f) -> p c f", c=DCN),
                wu_d[e].rearrange("(c p) f -> p c f", p=128)[:, :, fh * 1024:(fh + 1) * 1024]), 16)
            R("wupr").wrote(sDW)
        GE(m_wlu)
        return sDW.v

    def moe_load_dn(u):
        e, fh = units[u]
        def m_wl2(g, e=e, fh=fh):
            R("wmoed").need_write(W, g, "g")
            gthrottle(g); sDW.inc(g.dma_start(
                wdn[:].rearrange("p (c h2) -> p c h2", c=DCN),
                wd_d[e, fh * 1024:(fh + 1) * 1024, :]
                .rearrange("(c p) h2 -> p c h2", p=128)), 16)
            R("wmoed").wrote(sDW)
        GE(m_wl2)
        return sDW.v

    wl_g = {0: moe_load_g(0)}
    wl_u = {0: moe_load_u(0)}
    wl_dn = {0: moe_load_dn(0)}
    for u in range(len(units)):
        e, fh = units[u]
        p = u % 2
        if u + 1 < len(units):
            wl_g[u + 1] = moe_load_g(u + 1)
        mwl, mwlu, mwl2 = wl_g[u], wl_u[u], wl_dn[u]
        for th in range(2):
            for t2 in range(2):
                def m_dwb(t, th=th, e=e, t2=t2):
                    W.wge(t, "t", sV, sV.v)
                    R("pbM").need_write(W, t, "t")
                    t.matmul(pbM[:, 0:512], csel8[:, e * 128:(e + 1) * 128],
                             dwT[0:8, th * TH + t2 * 512:th * TH + (t2 + 1) * 512],
                             start=True, stop=True)
                    sPE.inc(t.nop())
                    R("pbM").wrote(sPE)
                TE(m_dwb)
                db = sPE.v
                def m_dwbe(v, t2=t2, db=db):
                    W.wge(v, "v", sPE, db)
                    R("dwb").need_write(W, v, "v")
                    sV.inc(v.tensor_copy(dwb[:, t2 * 512:(t2 + 1) * 512], pbM[:, 0:512]))
                    R("pbM").read(sV)
                    R("dwb").wrote(sV)
                VE(m_dwbe)
            dwb_t = sV.v

            for fc in range(DCN):
                for t2 in range(2):
                    tcol = th * TH + t2 * 512
                    bp = (fc * 2 + t2) % 2
                    def m_gu(t, fc=fc, t2=t2, tcol=tcol, bp=bp, p=p,
                             mwl=mwl, mwlu=mwlu, dwb_t=dwb_t):
                        W.wge(t, "t", sDW, max(mwl, mwlu))
                        W.wge(t, "t", sV, dwb_t)
                        if fc == 0 and t2 == 0:
                            R("hsb").need_write(W, t, "t")
                        R("pbG%d" % bp).need_write(W, t, "t")
                        for dc in range(DCN):
                            t.matmul(pbG[bp][:, 0:512],
                                     wgat[p][:, dc * 1024 + fc * 128:dc * 1024 + (fc + 1) * 128],
                                     xffn[:, dc * TC + tcol:dc * TC + tcol + 512],
                                     start=(dc == 0), stop=(dc == DCN - 1))
                        sPE.inc(t.nop())
                        R("pbG%d" % bp).wrote(sPE)
                        R("pbU%d" % bp).need_write(W, t, "t")
                        for dc in range(DCN):
                            t.matmul(pbU[bp][:, 0:512],
                                     wup[:, dc * 1024 + fc * 128:dc * 1024 + (fc + 1) * 128],
                                     xffn[:, dc * TC + tcol:dc * TC + tcol + 512],
                                     start=(dc == 0), stop=(dc == DCN - 1))
                        sPE.inc(t.nop())
                        R("pbU%d" % bp).wrote(sPE)
                        R("wmoe%d" % p).read(sPE)
                        R("wupr").read(sPE)
                    TE(m_gu)
                    gu_t = sPE.v
                    def m_si(s, bp=bp, gu_t=gu_t):
                        W.wge(s, "s", sPE, gu_t - 8)
                        R("gs").need_write(W, s, "s")
                        sS.inc(s.activation(gs_sb[:, 0:512], pbG[bp][:, 0:512],
                                            AF.Silu))
                        R("pbG%d" % bp).read(sS)
                        R("gs").wrote(sS)
                    SE(m_si)
                    si_t = sS.v
                    def m_h(v, fc=fc, t2=t2, bp=bp, si_t=si_t, gu_t=gu_t):
                        W.wge(v, "v", sS, si_t)
                        W.wge(v, "v", sPE, gu_t)
                        R("wt1").need_write(W, v, "v")
                        v.scalar_tensor_tensor(wt1[:, 0:512], pbU[bp][:, 0:512], 1.0,
                                               gs_sb[:, 0:512], ALU.mult, ALU.mult)
                        sV.inc(v.tensor_tensor(
                            h_sb[:, fc * 1024 + t2 * 512:fc * 1024 + t2 * 512 + 512],
                            wt1[:, 0:512], dwb[:, t2 * 512:(t2 + 1) * 512], ALU.mult))
                        R("pbU%d" % bp).read(sV)
                        R("gs").read(sV)
                        R("wt1").read(sV)
                        R("wt1").wrote(sV)
                        R("dwb").read(sV)
                    VE(m_h)
            h_t = sV.v
            if th == 1 and u + 1 < len(units):
                wl_u[u + 1] = moe_load_u(u + 1)

            for hc in range(DCN):
                for t2 in range(2):
                    bp = (hc * 2 + t2) % 2
                    def m_dn(t, hc=hc, t2=t2, bp=bp, h_t=h_t, mwl2=mwl2):
                        W.wge(t, "t", sV, h_t)
                        W.wge(t, "t", sDW, mwl2)
                        R("pbDN%d" % bp).need_write(W, t, "t")
                        for fc in range(DCN):
                            t.matmul(pbDN[bp][:, 0:512],
                                     wdn[:, fc * 1024 + hc * 128:fc * 1024 + (hc + 1) * 128],
                                     h_sb[:, fc * 1024 + t2 * 512:fc * 1024 + t2 * 512 + 512],
                                     start=(fc == 0), stop=(fc == DCN - 1))
                        sPE.inc(t.nop())
                        R("pbDN%d" % bp).wrote(sPE)
                        R("wmoed").read(sPE)
                        R("hsb").read(sPE)
                    TE(m_dn)
                    dn_t = sPE.v
                    def m_acc(v, hc=hc, t2=t2, th=th, bp=bp, dn_t=dn_t):
                        W.wge(v, "v", sPE, dn_t)
                        R("xTr").need_write(W, v, "v")
                        xs = xT[:, hc * XS + 1 + th * TH + t2 * 512:
                                hc * XS + 1 + th * TH + t2 * 512 + 512]
                        sV.inc(v.scalar_tensor_tensor(xs, pbDN[bp][:, 0:512], 1.0, xs,
                                                      ALU.mult, ALU.add))
                        R("pbDN%d" % bp).read(sV)
                    VE(m_acc)
        if u + 1 < len(units):
            wl_dn[u + 1] = moe_load_dn(u + 1)

    fin_t = sV.v
    def out_dma(sy, fin_t=fin_t):
        W.wge(sy, "y", sV, fin_t)
        for dc in range(DCN):
            ythrottle(sy); sDX.inc(sy.dma_start(out_d[dc * 128:(dc + 1) * 128, :],
                                 xT[:, dc * XS + 1:(dc + 1) * XS]), 16)
        sy.wait_ge(sDX.h, sDX.v)
    SY(out_dma)

    blk_cm.__exit__(None, None, None)
    for cm in pctx[::-1]:
        cm.__exit__(None, None, None)
    for cm in cctx[::-1]:
        cm.__exit__(None, None, None)
    for cm in bcctx[::-1]:
        cm.__exit__(None, None, None)
    for cm in abctx[::-1]:
        cm.__exit__(None, None, None)
    for cm in ctx[::-1]:
        cm.__exit__(None, None, None)
    return nc


ATTW_KEYS = ("L0_attn_W_r", "L0_attn_W_w", "L0_attn_W_k", "L0_attn_W_v",
             "L0_attn_W_a", "L0_attn_W_b", "L0_attn_W_g", "L0_attn_W_o")
MIX_KEYS = ("L0_attn_x_r", "L0_attn_x_w", "L0_attn_x_k", "L0_attn_x_v",
            "L0_attn_x_a", "L0_attn_x_b", "L0_attn_x_g")


def host_prep(inputs):
    x = np.ascontiguousarray(np.asarray(inputs["L0_attn_x"])[0].T)
    attw = np.stack([np.ascontiguousarray(np.asarray(inputs[k]).T)
                     for k in ATTW_KEYS])
    mixw = np.stack([np.asarray(inputs[k]) for k in MIX_KEYS] +
                    [np.zeros(D, np.float32)], axis=1).astype(np.float32)
    vecs = np.stack([np.asarray(inputs["L0_attn_b_o"]),
                     np.asarray(inputs["L0_attn_gn_w"]),
                     np.asarray(inputs["L0_attn_gn_b"]),
                     np.zeros(D, np.float32)], axis=1).astype(np.float32)
    routw = np.ascontiguousarray(np.asarray(inputs["L0_ffn_router_w"]).T).astype(np.float32)
    wgT = np.ascontiguousarray(np.asarray(inputs["L0_ffn_w_gate"]).transpose(0, 2, 1))
    wuT = np.ascontiguousarray(np.asarray(inputs["L0_ffn_w_up"]).transpose(0, 2, 1))
    wdT = np.ascontiguousarray(np.asarray(inputs["L0_ffn_w_down"]).transpose(0, 2, 1))

    cident = np.eye(128, dtype=np.float32)
    s_ = np.triu(np.ones((128, 128), np.float32), 1)
    i_ = np.triu(np.ones((128, 128), np.float32), 0)
    cmask4 = np.concatenate([s_, s_, i_, i_], axis=1)
    blocks = np.zeros((128, 32), np.float32)
    blocksp = np.zeros((128, 32), np.float32)
    eye32 = np.zeros((128, 32), np.float32)
    for i in range(4):
        blocks[32 * i:32 * i + 32] = np.triu(np.ones((32, 32), np.float32), 1)
        blocksp[32 * i:32 * i + 32] = np.tril(np.ones((32, 32), np.float32), -1)
        eye32[32 * i:32 * i + 32] = np.eye(32, dtype=np.float32)
    cm32 = np.concatenate([blocks, blocksp], axis=1)
    cones2 = np.zeros((128, 2), np.float32)
    cones2[0:64, 0] = 1.0
    cones2[64:128, 1] = 1.0
    csel2 = np.zeros((2, 128), np.float32)
    csel2[0, 0:64] = 1.0
    csel2[1, 64:128] = 1.0
    conesr = np.ones((128, 128), np.float32)
    csel8 = np.zeros((8, 1024), np.float32)
    for e in range(8):
        csel8[e, e * 128:(e + 1) * 128] = 1.0

    shared = dict(attw=attw, mixw=mixw, vecs=vecs, routw=routw,
                  wgT=wgT, wuT=wuT, wdT=wdT, cident=cident, cmask4=cmask4,
                  cm32=cm32, cI32=eye32, cones2=cones2, csel2=csel2,
                  conesr=conesr, csel8=csel8)
    in_maps = []
    for j in range(NCORE):
        xcj = np.zeros((D, TC + 1), np.float32)
        if j > 0:
            xcj[:, 0] = x[:, j * TC - 1]
        xcj[:, 1:] = x[:, j * TC:(j + 1) * TC]
        selrep = np.zeros((128, 8), np.float32)
        if j > 0:
            selrep[:, j - 1] = 1.0
        m = dict(shared)
        m["xT"] = np.ascontiguousarray(xcj)
        m["selrep"] = selrep
        in_maps.append(m)
    return in_maps


def kernel(**inputs):
    nc = bass.Bass("TRN2", target_bir_lowering=False)
    build(nc)
    in_maps = host_prep(inputs)
    res = run_bass_kernel_spmd(nc, in_maps, core_ids=list(range(NCORE)))
    outs = [np.asarray(r["out_ft"], np.float32) for r in res.results]
    full = np.concatenate([o.T for o in outs], axis=0)
    return full.reshape(1, T, D)
